# version 06
# baseline (speedup 1.0000x reference)
import numpy as np
import ml_dtypes
import concourse.bass as bass
import concourse.mybir as mybir
from concourse.bass_utils import run_bass_kernel_spmd

F32 = mybir.dt.float32
BF16 = mybir.dt.bfloat16
F16 = mybir.dt.float16
AF = mybir.ActivationFunctionType
ALU = mybir.AluOpType
AX = mybir.AxisListType
ET = mybir.EngineType

NCORES = 8
B = 8
S = 2048
D = 1024
TLOC = S // NCORES
R = B * TLOC
NT = R // 128
I0 = 2816
I1 = 3072
IC0 = I0 // 128
IC1 = I1 // 128
MAGIC = float(1.5 * 2.0 ** 23)
RG = [list(range(NCORES))]
ENGS = {"sync": ET.SP, "scalar": ET.Activation, "vector": ET.DVE,
        "tensor": ET.PE, "gpsimd": ET.Pool}

CONFIG = {"groups": [[0, 1, 2, 3, 4, 5]]}
DEBUG = {"dumps": []}

import os as _os
ELIDE_SAME_ENGINE = bool(int(_os.environ.get('MKELIDE', '0')))
ARN = ("R0", "R1", "R1b", "R2", "R2b", "R2c")
ACK = ("AC",)


class Seq:

    def __init__(self, nc):
        self.nc = nc
        self.ops = []
        self.pid = {}

    def op(self, eng, fn, r=(), w=(), fuse=False):
        if fuse and self.ops and self.ops[-1][0] == eng and not self.ops[-1][2]:
            p = self.ops[-1]
            self.ops[-1] = (eng, p[1] + [fn], False,
                            tuple(set(p[3]) | set(r)), tuple(set(p[4]) | set(w)))
        else:
            self.ops.append((eng, [fn], False, tuple(r), tuple(w)))

    def dma(self, eng, fn, r=(), w=()):
        self.ops.append((eng, [fn], True, tuple(r), tuple(w)))

    def core_id(self, name):
        if name not in self.pid:
            self.pid[name] = self.nc.partition_id(engines=[ENGS[name]])
        return self.pid[name]

    DMAK = 8

    def emit(self, block, sems):
        skey = []
        pos = []
        counts = {}
        for eng, fn, is_dma, r, w in self.ops:
            k = (eng, is_dma)
            skey.append(k)
            pos.append(counts.get(k, 0))
            counts[k] = counts.get(k, 0) + 1
        writers = {}
        readers = {}
        deps = [set() for _ in self.ops]
        for i, (eng, fn, is_dma, r, w) in enumerate(self.ops):
            for k in r:
                for j in writers.get(k, ()):
                    deps[i].add(j)
                readers.setdefault(k, []).append(i)
            for k in w:
                for j in writers.get(k, ()):
                    deps[i].add(j)
                for j in readers.get(k, ()):
                    deps[i].add(j)
                writers.setdefault(k, []).append(i)
        import os
        serial = bool(int(os.environ.get("MKSERIAL", "0")))
        def wtarget(k, p):
            if not k[1]:
                return (k, p + 1)
            return ((k[0], True, p % self.DMAK), 16 * (p // self.DMAK + 1))

        redeps = []
        for i in range(len(self.ops)):
            m = {}
            if serial:
                deps[i] = set(range(i))
            for j in deps[i]:
                if j == i:
                    continue
                kj = skey[j]
                if ELIDE_SAME_ENGINE and kj == skey[i] and not kj[1]:
                    continue
                sk, v = wtarget(kj, pos[j])
                m[sk] = max(m.get(sk, -1), v)
            redeps.append(m)

        finals = {}
        for k, n in counts.items():
            if not k[1]:
                finals[k] = n
            else:
                for sl in range(min(n, self.DMAK)):
                    uses = (n - sl + self.DMAK - 1) // self.DMAK
                    finals[(k[0], True, sl)] = 16 * uses

        def section(engobj, name):
            waited = {}

            def do_wait(sk, v):
                if v > waited.get(sk, 0):
                    engobj.wait_ge(sems[sk], v)
                    waited[sk] = v

            for i, (eng, fn, is_dma, r, w) in enumerate(self.ops):
                if eng != name:
                    continue
                if is_dma:
                    p = pos[i]
                    if p >= self.DMAK:
                        sk = (eng, True, p % self.DMAK)
                        do_wait(sk, 16 * (p // self.DMAK))
                for sk, v in sorted(redeps[i].items(), key=lambda x: str(x[0])):
                    do_wait(sk, v)
                if is_dma:
                    sk = (eng, True, pos[i] % self.DMAK)
                    fn[0](engobj).then_inc(sems[sk], 16)
                else:
                    for f in fn[:-1]:
                        f(engobj)
                    fn[-1](engobj).then_inc(sems[skey[i]], 1)
            for k, v in sorted(finals.items(), key=lambda x: str(x[0])):
                do_wait(k, v)

        for name in ("sync", "scalar", "vector", "tensor", "gpsimd"):
            getattr(block, name)(lambda e, n=name: section(e, n))
        return counts


class MK:
    def __init__(self, nc, s):
        self.nc = nc
        self.s = s
        self.t = {}

    def sb(self, name, shape, dt):
        self.t[name] = self.nc.alloc_sbuf_tensor(name, list(shape), dt)
        return self.t[name]

    def ps(self, name, shape, dt=F32):
        self.t[name] = self.nc.alloc_psum_tensor(name, list(shape), dt)
        return self.t[name]

    def dram(self, name, shape, dt, kind="Internal", addr_space="Local"):
        self.t[name] = self.nc.dram_tensor(name, list(shape), dt, kind=kind,
                                           addr_space=addr_space).ap()
        return self.t[name]

    def view(self, name, ap):
        self.t[name] = ap
        return ap

    def __getitem__(self, name):
        v = self.t[name]
        return v if isinstance(v, bass.AP) else v


def ap_of(x):
    return x if isinstance(x, bass.AP) else x.ap()[:, :]



def _lpbq_dequant_ops(mk, wv, ifull, scr, scr2, sg, isg, out_bf16, rkeys):
    s = mk.s
    ng = ifull // 32
    s.op("scalar", lambda e: e.activation(scr[:, 0:ifull], wv, AF.Abs),
         r=rkeys, w=("R0", "R1"))
    cur, oth = scr, scr2
    curw = 32
    while curw > 1:
        half = curw // 2
        v = cur[:, 0:ng * curw].rearrange("p (g w) -> p g w", w=curw)
        dv = oth[:, 0:ng * half].rearrange("p (g w) -> p g w", w=half)
        s.op("vector", lambda e, v=v, dv=dv, half=half: e.tensor_tensor(
            out=dv, in0=v[:, :, 0:half], in1=v[:, :, half:2 * half], op=ALU.max),
            r=("R0", "R1"), w=("R0", "R1"))
        cur, oth = oth, cur
        curw = half
    s.op("vector", lambda e, cur=cur: e.tensor_scalar(
        sg[:, 0:ng], cur[:, 0:ng], 1.0 / 7.0, 1e-12, ALU.mult, ALU.max),
        r=("R0", "R1"), w=("PAMW",))
    s.op("vector", lambda e: e.reciprocal(isg[:, 0:ng], sg[:, 0:ng]),
         r=("PAMW",), w=("PAMW",))
    sgb = sg[:, 0:ng].rearrange("p (g o) -> p g o", o=1).broadcast_to([128, ng, 32])
    isgb = isg[:, 0:ng].rearrange("p (g o) -> p g o", o=1).broadcast_to([128, ng, 32])
    wg_ = wv.rearrange("p (g w) -> p g w", w=32)
    scrg = scr[:, 0:ifull].rearrange("p (g w) -> p g w", w=32)
    scr2g = scr2[:, 0:ifull].rearrange("p (g w) -> p g w", w=32)
    s.op("vector", lambda e: e.tensor_tensor(out=scrg, in0=wg_, in1=isgb, op=ALU.mult),
         r=rkeys + ("PAMW",), w=("R0", "R1"))
    s.op("vector", lambda e: e.tensor_scalar(scr[:, 0:ifull], scr[:, 0:ifull],
                                             MAGIC, MAGIC, ALU.add, ALU.subtract),
         r=("R0", "R1"), w=("R0", "R1"))
    s.op("vector", lambda e: e.tensor_tensor(out=scr2g, in0=scrg, in1=sgb, op=ALU.mult),
         r=("R0", "R1", "PAMW"), w=("R0", "R1"))
    if out_bf16 is not None:
        s.op("vector", lambda e: e.tensor_copy(out=out_bf16, in_=scr2[:, 0:ifull]),
             r=("R0", "R1"), w=("R2",))


def prep_ffn_gateup(mk, wname, blobname, sblobname, lpbq):
    s = mk.s
    w = mk[wname]
    shard = mk.dram(blobname + "_shard", [3, 8, 128, 128], BF16)
    if sblobname:
        mk.dram(sblobname + "_shard", [3, 128], F32)
    wn, scr, scr2 = mk["p_wnat"], mk["p_scr"], mk["p_scr2"]
    am = mk["p_am"]
    stg, stg2 = mk["p_stage"], mk["p_stage2"]
    idb = mk["identb"]
    shk = "sh_" + blobname

    for k in range(3):
        def ld(e, k=k):
            pid = s.core_id("sync")
            return e.dma_start(out=wn[:, 0:D], in_=w[bass.ds(pid * 384 + k * 128, 128), :])
        s.dma("sync", ld, r=(), w=("R0",))
        if not lpbq:
            s.op("vector", lambda e: e.reduce_max(am[:, 0:1], wn[:, 0:D], AX.X,
                                                  apply_absolute_value=True),
                 r=("R0",), w=("HM2",))
            s.op("vector", lambda e: e.tensor_scalar(am[:, 1:2], am[:, 0:1],
                                                     1.0 / 127.0, 1e-30, ALU.mult, ALU.max),
                 r=("HM2",), w=("HM2",))
            s.op("vector", lambda e: e.reciprocal(am[:, 2:3], am[:, 1:2]),
                 r=("HM2",), w=("HM2",))
            s.dma("sync", lambda e, k=k: e.dma_start(out=mk[sblobname + "_shard"][k, :],
                                                     in_=am[:, 1:2]),
                  r=("HM2",), w=("sh_" + sblobname,))
            s.op("vector", lambda e: e.tensor_scalar(scr[:, 0:D], wn[:, 0:D],
                                                     am[:, 2:3], MAGIC, ALU.mult, ALU.add),
                 r=("R0", "HM2"), w=("R0",))
            s.op("vector", lambda e: e.tensor_scalar(stg[:, 0:D], scr[:, 0:D],
                                                     MAGIC, None, ALU.subtract),
                 r=("R0",), w=("R2",))
        else:
            _lpbq_dequant_ops(mk, wn[:, 0:D], D, scr, scr2, mk["p_am"], mk["p_am2"],
                              stg[:, 0:D], ("R0",))
        pst = mk["ptb0"] if k % 2 == 0 else mk["ptb1"]
        pbk = "ptb0" if k % 2 == 0 else "ptb1"
        for c in range(8):
            s.op("tensor", lambda e, c=c, pst=pst: e.transpose(
                pst[:, c * 128:(c + 1) * 128], stg[:, c * 128:(c + 1) * 128], idb[:]),
                r=("R2", "IDB"), w=(pbk,), fuse=(c != 0))
        s.op("vector", lambda e, pst=pst: e.tensor_copy(out=stg2[:, 0:D], in_=pst[:, 0:D]),
             r=(pbk,), w=("R2",))
        s.dma("sync", lambda e, k=k: e.dma_start(
            out=shard[k].rearrange("c h i -> h c i"),
            in_=stg2[:, 0:D].rearrange("h (c i) -> h c i", i=128)),
            r=("R2",), w=(shk,))


def prep_ffn_down(mk, wname, blobname, lpbq, ifull):
    s = mk.s
    w = mk[wname]
    shard = mk.dram(blobname + "_shard", [1, 24, 128, 128], BF16)
    wn, scr, scr2 = mk["p_wnat3"], mk["p_scr3"], mk["p_scr3b"]
    am = mk["p_am"]
    stg, stg2 = mk["p_stage"], mk["p_stage2"]
    idb = mk["identb"]
    shk = "sh_" + blobname

    def ld(e):
        pid = s.core_id("sync")
        return e.dma_start(out=wn[:, 0:ifull], in_=w[bass.ds(pid * 128, 128), :])
    s.dma("sync", ld, r=(), w=("R0",))
    if not lpbq:
        s.op("vector", lambda e: e.reduce_max(am[:, 0:1], wn[:, 0:ifull], AX.X,
                                              apply_absolute_value=True),
             r=("R0",), w=("HM2",))
        s.op("vector", lambda e: e.tensor_scalar(am[:, 1:2], am[:, 0:1],
                                                 1.0 / 127.0, 1e-30, ALU.mult, ALU.max),
             r=("HM2",), w=("HM2",))
        s.op("vector", lambda e: e.reciprocal(am[:, 2:3], am[:, 1:2]),
             r=("HM2",), w=("HM2",))
        s.op("vector", lambda e: e.tensor_scalar(scr[:, 0:ifull], wn[:, 0:ifull],
                                                 am[:, 2:3], MAGIC, ALU.mult, ALU.add),
             r=("R0", "HM2"), w=("R0", "R1"))
        s.op("vector", lambda e: e.tensor_scalar(scr2[:, 0:ifull], scr[:, 0:ifull],
                                                 MAGIC, am[:, 1:2], ALU.subtract, ALU.mult),
             r=("R0", "R1", "HM2"), w=("R1",))
    else:
        _lpbq_dequant_ops(mk, wn[:, 0:ifull], ifull, scr, scr2, mk["p_amw"],
                          mk["p_amw2"], None, ("R0",))
    nic_real = ifull // 128
    for ic in range(24):
        ic_src = min(ic, nic_real - 1)
        cc = ic % 8
        pst = mk["ptb0"] if (ic // 8) % 2 == 0 else mk["ptb1"]
        pbk = "ptb0" if (ic // 8) % 2 == 0 else "ptb1"
        s.op("vector", lambda e, ics=ic_src, cc=cc: e.tensor_copy(
            out=stg[:, cc * 128:(cc + 1) * 128],
            in_=scr2[:, ics * 128:(ics + 1) * 128]),
            r=("R0", "R1"), w=("R2",))
        s.op("tensor", lambda e, cc=cc, pst=pst: e.transpose(
            pst[:, cc * 128:(cc + 1) * 128], stg[:, cc * 128:(cc + 1) * 128], idb[:]),
            r=("R2", "IDB"), w=(pbk,))
        if cc == 7 or ic == 24 - 1:
            lo, n = ic - cc, cc + 1
            s.op("vector", lambda e, n=n, pst=pst: e.tensor_copy(
                out=stg2[:, 0:n * 128], in_=pst[:, 0:n * 128]),
                r=(pbk,), w=("R2",))
            s.dma("sync", lambda e, lo=lo, n=n: e.dma_start(
                out=shard[0, lo:lo + n].rearrange("c i h -> i c h"),
                in_=stg2[:, 0:n * 128].rearrange("i (c h) -> i c h", h=128)),
                r=("R2",), w=(shk,))


def prep_all(mk, want_l0, want_l1):
    s = mk.s
    pairs = []
    if want_l0:
        prep_ffn_gateup(mk, "L0w_gate", "b_wg0", "b_swg0", False)
        prep_ffn_gateup(mk, "L0w_up", "b_wu0", "b_swu0", False)
        prep_ffn_down(mk, "L0w_down", "b_wd0", False, I0)
        pairs += [("b_wg0", "sh_b_wg0"), ("b_swg0", "sh_b_swg0"),
                  ("b_wu0", "sh_b_wu0"), ("b_swu0", "sh_b_swu0"),
                  ("b_wd0", "sh_b_wd0")]
    if want_l1:
        prep_ffn_gateup(mk, "L1w_gate", "b_wg1", None, True)
        prep_ffn_gateup(mk, "L1w_up", "b_wu1", None, True)
        prep_ffn_down(mk, "L1w_down", "b_wd1", True, I1)
        pairs += [("b_wg1", "sh_b_wg1"), ("b_wu1", "sh_b_wu1"),
                  ("b_wd1", "sh_b_wd1")]
    for blob, shk in pairs:
        i_ = mk[blob + "_shard"][:]
        o_ = mk[blob][:]
        s.op("gpsimd", lambda e, i_=i_, o_=o_: e.collective_compute(
            "AllGather", ALU.bypass, replica_groups=RG, ins=[i_], outs=[o_]),
            r=(shk,), w=("hb_" + blob,))



def xkeys():
    return tuple(f"x{i}" for i in range(NT))


def layernorm_stats(mk):
    s = mk.s
    XT, scr = mk["XT"], mk["scr1"]
    SUMS, SSQ = mk["st_sum"], mk["st_ssq"]
    MEAN, RSTD, NMR, VAR = mk["st_mean"], mk["st_rstd"], mk["st_nmr"], mk["st_var"]
    scrB = mk["scr2"]
    for i in range(NT):
        X = XT[:, i * D:(i + 1) * D]
        SS, SSk = (scr, "R1") if i % 2 == 0 else (scrB, "R1b")
        s.op("scalar", lambda e, X=X, SS=SS: e.activation(SS[:, 0:D], X, AF.Square),
             r=(f"x{i}",), w=(SSk,))
        s.op("vector", lambda e, i=i, SS=SS: e.reduce_sum(SSQ[:, i:i + 1], SS[:, 0:D], AX.X),
             r=(SSk,), w=("SSQ",))
        s.op("vector", lambda e, X=X, i=i: e.reduce_sum(SUMS[:, i:i + 1], X, AX.X),
             r=(f"x{i}",), w=("SUMS",))
    s.op("vector", lambda e: e.tensor_scalar(MEAN[:], SUMS[:], 1.0 / D, None, ALU.mult),
         r=("SUMS",), w=("MEAN",))
    s.op("vector", lambda e: e.tensor_tensor(out=VAR[:], in0=MEAN[:], in1=MEAN[:],
                                             op=ALU.mult), r=("MEAN",), w=("VAR",))
    s.op("vector", lambda e: e.scalar_tensor_tensor(VAR[:], SSQ[:], 1.0 / D, VAR[:],
                                                    ALU.mult, ALU.subtract),
         r=("SSQ", "VAR"), w=("VAR",))
    s.op("vector", lambda e: e.tensor_scalar(SUMS[:], VAR[:], 1e-5, None, ALU.add),
         r=("VAR",), w=("SUMS",))
    s.op("scalar", lambda e: e.activation(RSTD[:], SUMS[:], AF.Sqrt),
         r=("SUMS",), w=("RSTD",))
    s.op("vector", lambda e: e.reciprocal(RSTD[:], RSTD[:]), r=("RSTD",), w=("RSTD",))
    s.op("vector", lambda e: e.scalar_tensor_tensor(NMR[:], MEAN[:], -1.0, RSTD[:],
                                                    ALU.mult, ALU.mult),
         r=("MEAN", "RSTD"), w=("NMR",))


def cross_partition_max(mk, AM, amkey, ncols, dstkey):
    s = mk.s
    red1, idf = mk["st_red1"], mk["identf"]
    pt = mk["pt2"]
    s.op("vector", lambda e: e.reduce_max(red1[:, 0:1], AM[:, 0:ncols], AX.X),
         r=(amkey,), w=("RED1",))
    s.op("tensor", lambda e: e.transpose(pt[0:1, 0:128], red1[:, 0:1], idf[:]),
         r=("RED1", "IDF"), w=("pt2",))
    s.op("vector", lambda e: e.reduce_max(mk["st_g1"][0:1, 0:1], pt[0:1, 0:128], AX.X),
         r=("pt2",), w=(dstkey,))


def global_absmax_of_ln(mk):
    s = mk.s
    XT, scr = mk["XT"], mk["scr1"]
    RSTD, NMR, AM = mk["st_rstd"], mk["st_nmr"], mk["st_amax"]
    scrB = mk["scr2"]
    for i in range(NT):
        X = XT[:, i * D:(i + 1) * D]
        SS, SSk = (scr, "R1") if i % 2 == 0 else (scrB, "R1b")
        s.op("scalar", lambda e, X=X, i=i, SS=SS: e.activation(
            SS[:, 0:D], X, AF.Abs, bias=NMR[:, i:i + 1], scale=RSTD[:, i:i + 1]),
            r=(f"x{i}", "RSTD", "NMR"), w=(SSk,))
        s.op("vector", lambda e, i=i, SS=SS: e.reduce_max(AM[:, i:i + 1], SS[:, 0:D], AX.X),
             r=(SSk,), w=("AMAX",))
    cross_partition_max(mk, AM, "AMAX", NT, "G1")


def allreduce_max(mk, n):
    s = mk.s
    cin, cout = mk["cc_in"], mk["cc_out"]
    s.dma("sync", lambda e: e.dma_start(out=cin[0:1, 0:n], in_=mk["ccstage"][0:1, 0:n]),
          r=("CCS",), w=("hb_cc",))
    s.op("gpsimd", lambda e: e.collective_compute(
        "AllReduce", ALU.max, replica_groups=RG,
        ins=[cin[0:1, 0:n]], outs=[cout[0:1, 0:n]]), r=("hb_cc",), w=("hb_cc2",))
    s.dma("sync", lambda e: e.dma_start(out=mk["st_gmax"][0:1, 0:n], in_=cout[0:1, 0:n]),
          r=("hb_cc2",), w=("GMAX",))


def broadcast_scalars(mk, src_ap, srckeys, dst, dkey, ncols):
    s = mk.s
    pt = mk["pt2"]
    s.op("tensor", lambda e: e.matmul(pt[:, 0:ncols], mk["onesf"][0:1, :], src_ap,
                                      start=True, stop=True),
         r=srckeys + ("ONESF",), w=("pt2",))
    s.op("vector", lambda e: e.tensor_copy(out=mk[dst][:, 0:ncols], in_=pt[:, 0:ncols]),
         r=("pt2",), w=(dkey,))


def quantize_x_to_xqt(mk, mode="int8"):
    s = mk.s
    XT, scr, scrb = mk["XT"], mk["scr1"], mk["scr2"]
    RSTD, NMR, RS, NMRS = mk["st_rstd"], mk["st_nmr"], mk["st_rs"], mk["st_nmrs"]
    XQK, XQT = mk["xqk"], mk["XQT"]
    idb = mk["identb"]
    s.op("vector", lambda e: e.tensor_scalar(RS[:], RSTD[:], mk["sxb"][:, 1:2],
                                             None, ALU.mult),
         r=("RSTD", "SXB"), w=("RS",))
    s.op("vector", lambda e: e.tensor_scalar(NMRS[:], NMR[:], mk["sxb"][:, 1:2],
                                             None, ALU.mult),
         r=("NMR", "SXB"), w=("NMRS",))
    for i in range(NT):
        X = XT[:, i * D:(i + 1) * D]
        if i % 2 == 0:
            SA, SB, XO = scr, scrb, XQK
            SAk, SBk, XOk = "R1", "R1b", "R2"
        else:
            SA, SB, XO = mk["p_scr"], mk["p_scr2"], mk["attt"]
            SAk, SBk, XOk = "R0", "R0", "R2c"
        s.op("scalar", lambda e, X=X, i=i, SA=SA: e.activation(
            SA[:, 0:D], X, AF.Identity, bias=NMRS[:, i:i + 1], scale=RS[:, i:i + 1]),
            r=(f"x{i}", "RS", "NMRS"), w=(SAk,))
        if mode == "int8":
            s.op("vector", lambda e, SA=SA, SB=SB: e.tensor_scalar(
                SB[:, 0:D], SA[:, 0:D], MAGIC, MAGIC, ALU.add, ALU.subtract),
                r=(SAk,), w=(SBk,))
            s.op("vector", lambda e, SB=SB, XO=XO: e.tensor_scalar(
                XO[:, 0:D], SB[:, 0:D], 127.0, -128.0, ALU.min, ALU.max),
                r=(SBk,), w=(XOk,))
        else:
            s.op("scalar", lambda e, SA=SA, SB=SB: e.activation(
                SB[:, 0:D], SA[:, 0:D], AF.Identity, scale=mk["rcst"][:, 3:4],
                bias=mk["rcst"][:, 0:1]),
                r=(SAk, "RCST"), w=(SBk,))
            s.op("vector", lambda e, SB=SB, XO=XO: e.tensor_scalar(
                XO[:, 0:D], SB[:, 0:D], MAGIC, mk["sxb"][:, 0:1],
                ALU.subtract, ALU.mult),
                r=(SBk, "SXB"), w=(XOk,))
        pst = mk["ptb0"] if i % 2 == 0 else mk["ptb1"]
        pbk = "ptb0" if i % 2 == 0 else "ptb1"
        for c in range(8):
            s.op("tensor", lambda e, c=c, pst=pst, XO=XO: e.transpose(
                pst[:, c * 128:(c + 1) * 128], XO[:, c * 128:(c + 1) * 128], idb[:]),
                r=(XOk, "IDB"), w=(pbk,), fuse=(c != 0))
        dst = XQT.rearrange("p (c t) -> p c t", t=R)[:, :, i * 128:(i + 1) * 128]
        srcv = ap_of(pst)[:, 0:D].rearrange("p (c t) -> p c t", t=128)
        s.op("vector", lambda e, dst=dst, srcv=srcv: e.tensor_copy(out=dst, in_=srcv),
             r=(pbk,), w=("AC", "ACt", "ACt2"))


def load_scale_blob(mk, blobname, dstname, dkey, nic):
    s = mk.s
    tmp, idf = mk["sc_tmp"], mk["identf"]
    pt = mk["pt2"]
    s.dma("sync", lambda e: e.dma_start(out=tmp[0:24, :], in_=mk[blobname][:, :]),
          r=("hb_" + blobname,), w=("R2",))
    s.op("tensor", lambda e: e.transpose(pt[:, 0:24], tmp[0:24, :], idf[0:24, 0:24]),
         r=("R2", "IDF"), w=("pt2",))
    s.op("vector", lambda e: e.tensor_copy(out=mk[dstname][:, 0:nic], in_=pt[:, 0:nic]),
         r=("pt2",), w=(dkey,))



def sublayer_l0ffn(mk):
    s = mk.s

    def dump(tag, ap, shape, dt, rk):
        if tag in DEBUG["dumps"] and ("dbg_" + tag) not in mk.t:
            o = mk.dram("dbg_" + tag, list(shape), dt, kind="ExternalOutput")
            mk.s.dma("sync", lambda e: e.dma_start(out=o, in_=ap), r=rk, w=())
    layernorm_stats(mk)
    global_absmax_of_ln(mk)
    SC = mk["scal"]
    s.op("vector", lambda e: e.tensor_copy(out=mk["ccstage"][0:1, 0:1],
                                           in_=mk["st_g1"][0:1, 0:1]),
         r=("G1",), w=("CCS",))
    allreduce_max(mk, 1)
    s.op("scalar", lambda e: e.activation(SC[0:1, 0:1], mk["clipx_t"][0:1, 0:1],
                                          AF.Sigmoid), r=("CLX",), w=("SC",))
    s.op("vector", lambda e: e.scalar_tensor_tensor(
        SC[0:1, 1:2], SC[0:1, 0:1], 1.0 / 127.0, mk["st_gmax"][0:1, 0:1],
        ALU.mult, ALU.mult), r=("SC", "GMAX"), w=("SC",))
    s.op("vector", lambda e: e.reciprocal(SC[0:1, 2:3], SC[0:1, 1:2]),
         r=("SC",), w=("SC",))
    broadcast_scalars(mk, SC[0:1, 1:3], ("SC",), "sxb", "SXB", 2)
    dump("rstd", ap_of(mk.t["st_rstd"]), (128, 16), F32, ("RSTD",))
    dump("sxb", ap_of(mk.t["sxb"]), (128, 2), F32, ("SXB",))
    quantize_x_to_xqt(mk)
    dump("xqt", mk["XQT"], (128, 8 * R), BF16, ("AC", "ACt"))

    load_scale_blob(mk, "b_swg0", "swg", "SWG", IC0)
    load_scale_blob(mk, "b_swu0", "swu", "SWU", IC0)
    s.op("vector", lambda e: e.tensor_scalar(mk["swg"][:, 0:IC0], mk["swg"][:, 0:IC0],
                                             mk["sxb"][:, 0:1], None, ALU.mult),
         r=("SWG", "SXB"), w=("SWG",))
    s.op("vector", lambda e: e.tensor_scalar(mk["swu"][:, 0:IC0], mk["swu"][:, 0:IC0],
                                             mk["sxb"][:, 0:1], None, ALU.mult),
         r=("SWU", "SXB"), w=("SWU",))

    XQTv = mk["XQT"].rearrange("p (c t) -> p c t", t=R)
    WG, WU = mk["wg_t"], mk["wu_t"]
    silg, upr = mk["silg"], mk["upr"]
    Hf16 = mk["Hf16"]
    HM = mk["hmax"]
    for ic in range(IC0):
        sl = ic % 2
        wgk, wuk = f"WGT{sl}", "WUT0"
        s.dma("sync", lambda e, ic=ic, sl=sl: e.dma_start(
            out=ap_of(WG)[:, sl * D:(sl + 1) * D].rearrange("p (c i) -> p c i", i=128),
            in_=mk["b_wg0"][ic].rearrange("c p i -> p c i")),
            r=("hb_b_wg0", "hb_b_wd0"), w=(wgk,))
        s.dma("sync", lambda e, ic=ic: e.dma_start(
            out=ap_of(WU)[:, 0:D].rearrange("p (c i) -> p c i", i=128),
            in_=mk["b_wu0"][ic].rearrange("c p i -> p c i")),
            r=("hb_b_wu0", "hb_b_wd0"), w=(wuk,))
        for tt in range(4):
            pg, pgk = (mk["pt0"], "pt0") if tt % 2 == 0 else (mk["pt1"], "pt1")
            pu, puk = (mk["pt2"], "pt2") if tt % 2 == 0 else (mk["pt3"], "pt3")
            for hc in range(8):
                s.op("tensor", lambda e, hc=hc, tt=tt, sl=sl, pg=pg: e.matmul(
                    pg[:, 0:512], WG[:, sl * D + hc * 128:sl * D + (hc + 1) * 128],
                    XQTv[:, hc, tt * 512:(tt + 1) * 512],
                    start=(hc == 0), stop=(hc == 7)),
                    r=(wgk, "AC", "ACt", "ACt2"), w=(pgk,), fuse=(hc != 0))
            for hc in range(8):
                s.op("tensor", lambda e, hc=hc, tt=tt, pu=pu: e.matmul(
                    pu[:, 0:512], WU[:, hc * 128:(hc + 1) * 128],
                    XQTv[:, hc, tt * 512:(tt + 1) * 512],
                    start=(hc == 0), stop=(hc == 7)),
                    r=(wuk, "AC", "ACt", "ACt2"), w=(puk,), fuse=(hc != 0))
            s.op("scalar", lambda e, ic=ic, pg=pg: e.activation(
                silg[:], pg[:, 0:512], AF.Silu, scale=mk["swg"][:, ic:ic + 1]),
                r=(pgk, "SWG"), w=("SILG",))
            s.op("scalar", lambda e, ic=ic, pu=pu: e.activation(
                upr[:], pu[:, 0:512], AF.Identity, scale=mk["swu"][:, ic:ic + 1]),
                r=(puk, "SWU"), w=("UPR",))
            hv = Hf16[:, ic * R + tt * 512: ic * R + (tt + 1) * 512]
            s.op("vector", lambda e, hv=hv: e.tensor_tensor(out=hv, in0=silg[:],
                                                            in1=upr[:], op=ALU.mult),
                 r=("SILG", "UPR"), w=ARN)
        hic = Hf16[:, ic * R:(ic + 1) * R]
        s.op("vector", lambda e, hic=hic, ic=ic: e.reduce_max(
            HM[:, ic:ic + 1], hic, AX.X,
            apply_absolute_value=True), r=ARN, w=("HM",))

    dump("hf", mk["Hf16"], (128, IC0 * R), F16, ARN)
    dump("swg", ap_of(mk.t["swg"]), (128, 24), F32, ("SWG",))
    cross_partition_max(mk, HM, "HM", IC0, "G1")
    s.op("vector", lambda e: e.tensor_copy(out=mk["ccstage"][0:1, 0:1],
                                           in_=mk["st_g1"][0:1, 0:1]),
         r=("G1",), w=("CCS",))
    allreduce_max(mk, 1)
    s.op("scalar", lambda e: e.activation(SC[0:1, 0:1], mk["clipd_t"][0:1, 0:1],
                                          AF.Sigmoid), r=("CLD",), w=("SC",))
    s.op("vector", lambda e: e.scalar_tensor_tensor(
        SC[0:1, 1:2], SC[0:1, 0:1], 1.0 / 127.0, mk["st_gmax"][0:1, 0:1],
        ALU.mult, ALU.mult), r=("SC", "GMAX"), w=("SC",))
    s.op("vector", lambda e: e.reciprocal(SC[0:1, 2:3], SC[0:1, 1:2]),
         r=("SC",), w=("SC",))
    s.op("vector", lambda e: e.tensor_scalar(SC[0:1, 3:4], SC[0:1, 1:2], 0.5,
                                             None, ALU.mult), r=("SC",), w=("SC",))
    broadcast_scalars(mk, SC[0:1, 1:4], ("SC",), "shb", "SHB", 3)

    HQ = mk["H"]
    for ic in range(IC0):
        for tt in range(4):
            par = (ic * 4 + tt) % 2
            S, Sk = (silg, "SILG") if par == 0 else (mk["silgB"], "SILGB")
            U, Uk = (upr, "UPR") if par == 0 else (mk["uprB"], "UPRB")
            Sx = (Sk,) if par == 0 else (Sk, "ACt2")
            Ux = (Uk,) if par == 0 else (Uk, "ACt2")
            hv = Hf16[:, ic * R + tt * 512: ic * R + (tt + 1) * 512]
            hq = HQ[:, ic * R + tt * 512: ic * R + (tt + 1) * 512]
            s.op("scalar", lambda e, hv=hv, S=S: e.activation(
                S[:, 0:512], hv, AF.Identity, scale=mk["shb"][:, 1:2],
                bias=mk["rcst"][:, 0:1]),
                r=ARN + ("SHB", "RCST"), w=Sx)
            s.op("vector", lambda e, S=S, U=U: e.tensor_scalar(
                U[:, 0:512], S[:, 0:512], MAGIC, 127.0, ALU.subtract, ALU.min),
                r=Sx, w=Ux)
            s.op("vector", lambda e, hq=hq, U=U: e.tensor_scalar(
                hq, U[:, 0:512], -128.0, None, ALU.max), r=Ux, w=ARN)

    dump("hq", mk["H"], (128, IC0 * R), BF16, ARN)
    XT = mk["XT"]
    WD = mk["wd_t"]
    for half in range(2):
        for ic in range(IC0):
            src = mk["b_wd0"][half * 4:(half + 1) * 4, ic, :, :].rearrange("c i h -> i c h")
            s.dma("sync", lambda e, ic=ic, src=src: e.dma_start(
                out=WD[:, ic * 512:(ic + 1) * 512].rearrange("i (c h) -> i c h", h=128),
                in_=src), r=("hb_b_wd0",), w=("AC",))
        for i in range(NT):
            pd, pdk = (mk["pt0"], "pt0") if i % 2 == 0 else (mk["pt1"], "pt1")
            for ic in range(IC0):
                s.op("tensor", lambda e, i=i, ic=ic, pd=pd: e.matmul(
                    pd[:, 0:512], HQ[:, ic * R + i * 128: ic * R + (i + 1) * 128],
                    WD[:, ic * 512:(ic + 1) * 512],
                    start=(ic == 0), stop=(ic == IC0 - 1)),
                    r=ARN + ("AC",), w=(pdk,), fuse=(ic != 0))
            dstx = XT[:, i * D + half * 512: i * D + (half + 1) * 512]
            s.op("vector", lambda e, dstx=dstx, pd=pd: e.scalar_tensor_tensor(
                dstx, pd[:, 0:512], mk["shb"][:, 2:3], dstx, ALU.mult, ALU.add),
                r=(pdk, "SHB", f"x{i}"), w=(f"x{i}",))



ATTN_STD = [1.0, 2.0, 4.0, 8.0, 1.0, 2.0, 4.0, 8.0]


def ln_to_actt(mk, reorder=False):
    s = mk.s
    XT = mk["XT"]
    XQK, XQT = mk["xqk"], mk["XQT"]
    idb = mk["identb"]
    for i in range(NT):
        X = XT[:, i * D:(i + 1) * D]
        XO, XOk = (XQK, "R2") if i % 2 == 0 else (mk["attt"], "R2c")
        s.op("scalar", lambda e, X=X, i=i, XO=XO: e.activation(
            XO[:, 0:D], X, AF.Identity, bias=mk["st_nmr"][:, i:i + 1],
            scale=mk["st_rstd"][:, i:i + 1]),
            r=(f"x{i}", "RSTD", "NMR"), w=(XOk,))
        pst = mk["ptb0"] if i % 2 == 0 else mk["ptb1"]
        pbk = "ptb0" if i % 2 == 0 else "ptb1"
        for c in range(8):
            s.op("tensor", lambda e, c=c, pst=pst, XO=XO: e.transpose(
                pst[:, c * 128:(c + 1) * 128], XO[:, c * 128:(c + 1) * 128], idb[:]),
                r=(XOk, "IDB"), w=(pbk,), fuse=(c != 0))
        if not reorder:
            dst = XQT.rearrange("p (c t) -> p c t", t=R)[:, :, i * 128:(i + 1) * 128]
            srcv = ap_of(pst)[:, 0:D].rearrange("p (c t) -> p c t", t=128)
        else:
            b, tc = i // 2, i % 2
            dst = XQT.rearrange("p (c G j bb) -> p c G j bb", c=8, G=16, j=16, bb=8)[
                :, :, tc * 8:(tc + 1) * 8, :, b]
            srcv = ap_of(pst)[:, 0:D].rearrange(
                "p (c gl j) -> p c gl j", c=8, gl=8, j=16)
        s.op("vector", lambda e, dst=dst, srcv=srcv: e.tensor_copy(out=dst, in_=srcv),
             r=(pbk,), w=("AC", "ACt", "ACt2"))


def sublayer_l0attn(mk):
    import math
    s = mk.s
    layernorm_stats(mk)
    ln_to_actt(mk)

    s.dma("gpsimd", lambda e: e.dma_start(
        out=mk["WIT"].rearrange("p (c m) -> p c m", m=D),
        in_=mk["W_inT_d"].rearrange("(c p) m -> p c m", p=128)),
        r=(), w=("R2", "R2c"))
    LNXT = mk["XQT"].rearrange("p (c t) -> p c t", t=R)
    VTOK = mk["VTOK"]
    for i in range(NT):
        if i % 2 == 0:
            ph = ((mk["pt0"], "pt0"), (mk["pt1"], "pt1"))
        else:
            ph = ((mk["pt2"], "pt2"), (mk["pt3"], "pt3"))
        for half in range(2):
            pp, ppk = ph[half]
            for hc in range(8):
                s.op("tensor", lambda e, i=i, half=half, hc=hc, pp=pp: e.matmul(
                    pp[:, 0:512],
                    LNXT[:, hc, i * 128:(i + 1) * 128],
                    mk["WIT"][:, hc * D + half * 512: hc * D + (half + 1) * 512],
                    start=(hc == 0), stop=(hc == 7)),
                    r=("AC", "ACt", "ACt2", "R2"), w=(ppk,), fuse=(hc != 0))
            dst = VTOK[:, i * D + half * 512:i * D + (half + 1) * 512]
            s.op("scalar", lambda e, dst=dst, pp=pp: e.activation(
                dst, pp[:, 0:512], AF.Identity), r=(ppk,), w=("R0",))

    PBT = mk["pbt"]
    dm = mk["scr1"]
    s.dma("sync", lambda e: e.dma_start(out=dm[:, 0:1536], in_=mk["dm_d"]),
          r=(), w=("R1",))
    for h in range(8):
        std = ATTN_STD[h]
        sidx = {1.0: 0, 2.0: 1, 4.0: 2, 8.0: 3}[std]
        inv_norm = 1.0 / (std * math.sqrt(2.0 * math.pi))
        s.op("scalar", lambda e, sidx=sidx: e.activation(
            mk["scr2"][:, 0:384], dm[:, sidx * 384:(sidx + 1) * 384], AF.Square),
            r=("R1",), w=("R1b",))
        s.op("scalar", lambda e: e.activation(
            mk["scr2"][:, 384:768], mk["scr2"][:, 0:384], AF.Exp, scale=-0.5),
            r=("R1b",), w=("R1b",))
        s.op("vector", lambda e, h=h, inv_norm=inv_norm: e.tensor_scalar(
            PBT[:, h * 384:(h + 1) * 384], mk["scr2"][:, 384:768],
            float(inv_norm), None, ALU.mult), r=("R1b",), w=("R2",))

    vshard = mk["v_eg_shard"]
    for side in range(2):
        src = VTOK.rearrange("p (b tc m) -> p b tc m", tc=2, m=D)[:, :, side, :]
        s.dma("sync", lambda e, side=side, src=src: e.dma_start(
            out=vshard[side].rearrange("b p m -> p b m"), in_=src),
            r=("R0",), w=("hb_vsh",))
    HALO = mk["HALO"]
    s.op("vector", lambda e: e.memset(HALO, 0.0), r=("AC", "ACt"), w=("AC", "ACt", "ACt2"))
    zsrc = HALO.rearrange("p (s b m) -> p s b m", s=2, m=D)
    for g in (0, NCORES + 1):
        s.dma("sync", lambda e, g=g: e.dma_start(
            out=mk["v_eg_g"][g].rearrange("s b p m -> p s b m"), in_=zsrc),
            r=("AC", "ACt", "ACt2"), w=("hb_veg",))
    s.op("gpsimd", lambda e: e.collective_compute(
        "AllGather", ALU.bypass, replica_groups=RG,
        ins=[vshard[:]], outs=[mk["v_eg_g"][1:NCORES + 1]]),
        r=("hb_vsh", "hb_veg"), w=("hb_veg",))

    def ld_halo(e, hside):
        pid = s.core_id("sync")
        slot = pid if hside == 0 else pid + 2
        src = mk["v_eg_g"][bass.ds(slot, 1), 1 - hside].rearrange("o b p m -> p (o b) m")
        dst = HALO.rearrange("p (g m) -> p g m", m=D)[:, hside * 8:(hside + 1) * 8, :]
        return e.dma_start(out=dst, in_=src)
    s.dma("sync", lambda e: ld_halo(e, 0), r=("hb_veg",), w=("AC", "ACt", "ACt2"))
    s.dma("sync", lambda e: ld_halo(e, 1), r=("hb_veg",), w=("AC", "ACt", "ACt2"))

    ATTOK = mk["ATTOK"]
    psa = mk["pt_a"]
    for tc in range(2):
        for h in range(8):
            if h % 2 == 0:
                ph = ((psa, "pta", 0), (psa, "pta", 512))
            else:
                ph = ((mk["pt0"], "pt0", 0), (mk["pt1"], "pt1", 0))
            for half in range(2):
                pp, ppk, po = ph[half]
                order = [r for r in range(3) if 0 <= tc + r - 1 <= 1] + \
                        [r for r in range(3) if not (0 <= tc + r - 1 <= 1)]
                for j, r in enumerate(order):
                    kc = tc + r - 1
                    if kc == -1:
                        rhs = HALO.rearrange("p (g m) -> p g m", m=D)[
                            :, half * 4:half * 4 + 4, h * 128:(h + 1) * 128]
                        rk = "ACh"
                    elif kc == 2:
                        rhs = HALO.rearrange("p (g m) -> p g m", m=D)[
                            :, 8 + half * 4:8 + half * 4 + 4, h * 128:(h + 1) * 128]
                        rk = "ACh"
                    else:
                        rhs = VTOK.rearrange("p (b tc m) -> p b tc m", tc=2, m=D)[
                            :, half * 4:half * 4 + 4, kc, h * 128:(h + 1) * 128]
                        rk = "R0"
                    rks = ("AC", "ACt", "ACt2") if rk == "ACh" else (rk,)
                    s.op("tensor", lambda e, rhs=rhs, h=h, r=r, pp=pp, po=po,
                         j=j: e.matmul(
                        pp[:, po:po + 512],
                        mk["pbt"][:, (h * 3 + r) * 128:(h * 3 + r + 1) * 128],
                        rhs, start=(j == 0), stop=(j == 2)),
                        r=rks + ("R2",), w=(ppk,), fuse=(j == 1))
            for half in range(2):
                pp, ppk, po = ph[half]
                dst = ATTOK.rearrange("p (tc b m) -> p tc b m", b=8, m=D)[
                    :, tc, half * 4:half * 4 + 4, h * 128:(h + 1) * 128]
                s.op("scalar", lambda e, dst=dst, pp=pp, po=po: e.activation(
                    dst, pp[:, po:po + 512], AF.Identity), r=(ppk,), w=("R1",))

    s.dma("gpsimd", lambda e: e.dma_start(
        out=mk["WOT"].rearrange("p (c m) -> p c m", m=D),
        in_=mk["W_outT_d"].rearrange("(c p) m -> p c m", p=128)),
        r=(), w=("R2b", "R2"))
    XT = mk["XT"]
    ATTT = mk["attt"]
    idb = mk["identb"]
    for i in range(NT):
        b, tc = i // 2, i % 2
        src = ATTOK.rearrange("p (tc b m) -> p tc b m", b=8, m=D)[:, tc, b, :]
        pst = mk["ptb0"] if i % 2 == 0 else mk["ptb1"]
        pbk = "ptb0" if i % 2 == 0 else "ptb1"
        for c in range(8):
            s.op("tensor", lambda e, c=c, src=src, pst=pst: e.transpose(
                pst[:, c * 128:(c + 1) * 128], src[:, c * 128:(c + 1) * 128], idb[:]),
                r=("R1", "IDB"), w=(pbk,), fuse=(c != 0))
        s.op("vector", lambda e, pst=pst: e.tensor_copy(out=ATTT[:], in_=pst[:, 0:D]),
             r=(pbk,), w=("R2c", "R2"))
        pd, pdk = (mk["pt2"], "pt2") if i % 2 == 0 else (mk["pt3"], "pt3")
        for half in range(2):
            for c in range(8):
                s.op("tensor", lambda e, c=c, half=half, pd=pd: e.matmul(
                    pd[:, 0:512], ATTT[:, c * 128:(c + 1) * 128],
                    mk["WOT"][:, c * D + half * 512: c * D + (half + 1) * 512],
                    start=(c == 0), stop=(c == 7)),
                    r=("R2c", "R2b"), w=(pdk,), fuse=(c != 0))
            dstx = XT[:, i * D + half * 512: i * D + (half + 1) * 512]
            s.op("vector", lambda e, dstx=dstx, pd=pd: e.tensor_tensor(
                out=dstx, in0=pd[:, 0:512], in1=dstx, op=ALU.add),
                r=(pdk, f"x{i}"), w=(f"x{i}",))



INV32767 = 1.0 / 32767.0
TWO16 = 65536.0
INV216 = 1.0 / 65536.0


def sublayer_l1ffn(mk):
    s = mk.s
    SC = mk["scal"]
    layernorm_stats(mk)
    global_absmax_of_ln(mk)
    s.op("vector", lambda e: e.tensor_copy(out=mk["ccstage"][0:1, 0:1],
                                           in_=mk["st_g1"][0:1, 0:1]),
         r=("G1",), w=("CCS",))
    allreduce_max(mk, 1)
    s.op("vector", lambda e: e.tensor_scalar(SC[0:1, 1:2], mk["st_gmax"][0:1, 0:1],
                                             INV32767, 1e-12, ALU.mult, ALU.max),
         r=("GMAX",), w=("SC",))
    s.op("vector", lambda e: e.reciprocal(SC[0:1, 2:3], SC[0:1, 1:2]),
         r=("SC",), w=("SC",))
    broadcast_scalars(mk, SC[0:1, 1:3], ("SC",), "sxb", "SXB", 2)
    quantize_x_to_xqt(mk, mode="real16")

    XQTv = mk["XQT"].rearrange("p (c t) -> p c t", t=R)
    WG, WU = mk["wg_t"], mk["wu_t"]
    GATE = mk["GATE16"]
    upst = mk["upst"]
    UM, GM = mk["hmax"], mk["hmax2"]
    for ic in range(IC1):
        sl = ic % 2
        wgk, wuk = f"WGT{sl}", "WUT0"
        s.dma("sync", lambda e, ic=ic, sl=sl: e.dma_start(
            out=ap_of(WG)[:, sl * D:(sl + 1) * D].rearrange("p (c i) -> p c i", i=128),
            in_=mk["b_wg1"][ic].rearrange("c p i -> p c i")),
            r=("hb_b_wg1",), w=(wgk,))
        s.dma("sync", lambda e, ic=ic: e.dma_start(
            out=ap_of(WU)[:, 0:D].rearrange("p (c i) -> p c i", i=128),
            in_=mk["b_wu1"][ic].rearrange("c p i -> p c i")),
            r=("hb_b_wu1",), w=(wuk,))
        for tt in range(4):
            pg, pgk = (mk["pt0"], "pt0") if tt % 2 == 0 else (mk["pt1"], "pt1")
            pu, puk = (mk["pt2"], "pt2") if tt % 2 == 0 else (mk["pt3"], "pt3")
            for hc in range(8):
                s.op("tensor", lambda e, hc=hc, tt=tt, pu=pu: e.matmul(
                    pu[:, 0:512], WU[:, hc * 128:(hc + 1) * 128],
                    XQTv[:, hc, tt * 512:(tt + 1) * 512],
                    start=(hc == 0), stop=(hc == 7)),
                    r=(wuk, "AC", "ACt", "ACt2"), w=(puk,), fuse=(hc != 0))
            for hc in range(8):
                s.op("tensor", lambda e, hc=hc, tt=tt, sl=sl, pg=pg: e.matmul(
                    pg[:, 0:512], WG[:, sl * D + hc * 128:sl * D + (hc + 1) * 128],
                    XQTv[:, hc, tt * 512:(tt + 1) * 512],
                    start=(hc == 0), stop=(hc == 7)),
                    r=(wgk, "AC", "ACt", "ACt2"), w=(pgk,), fuse=(hc != 0))
            s.op("scalar", lambda e, pu=pu: e.activation(upst[:], pu[:, 0:512],
                                                         AF.Identity),
                 r=(puk,), w=("UPST",))
            s.dma("scalar", lambda e, ic=ic, tt=tt: e.dma_start(
                out=mk["up_hbm"][ic, :, tt * 512:(tt + 1) * 512], in_=upst[:]),
                r=("UPST",), w=(f"hup{ic}_{tt}",))
            s.op("vector", lambda e, ic=ic, tt=tt: e.reduce_max(
                UM[:, ic * 4 + tt:ic * 4 + tt + 1], upst[:], AX.X,
                apply_absolute_value=True), r=("UPST",), w=("HM",))
            gsl = GATE[:, ic * R + tt * 512: ic * R + (tt + 1) * 512]
            s.op("scalar", lambda e, gsl=gsl, pg=pg: e.activation(
                gsl, pg[:, 0:512], AF.Identity), r=(pgk,), w=ARN)
        gic = GATE[:, ic * R:(ic + 1) * R]
        s.op("vector", lambda e, gic=gic, ic=ic: e.reduce_max(
            GM[:, ic:ic + 1], gic, AX.X,
            apply_absolute_value=True), r=ARN, w=("HM2",))

    cross_partition_max(mk, UM, "HM", IC1 * 4, "G1")
    s.op("vector", lambda e: e.tensor_copy(out=mk["ccstage"][0:1, 0:1],
                                           in_=mk["st_g1"][0:1, 0:1]),
         r=("G1",), w=("CCS",))
    cross_partition_max(mk, GM, "HM2", IC1, "G1")
    s.op("vector", lambda e: e.tensor_copy(out=mk["ccstage"][0:1, 1:2],
                                           in_=mk["st_g1"][0:1, 0:1]),
         r=("G1",), w=("CCS",))
    allreduce_max(mk, 2)
    s.op("vector", lambda e: e.tensor_scalar(SC[0:1, 4:5], mk["st_gmax"][0:1, 0:1],
                                             INV32767, 1e-12, ALU.mult, ALU.max),
         r=("GMAX",), w=("SC",))
    s.op("vector", lambda e: e.reciprocal(SC[0:1, 5:6], SC[0:1, 4:5]),
         r=("SC",), w=("SC",))
    s.op("vector", lambda e: e.tensor_scalar(SC[0:1, 6:7], mk["st_gmax"][0:1, 1:2],
                                             INV32767, 1e-12, ALU.mult, ALU.max),
         r=("GMAX",), w=("SC",))
    s.op("vector", lambda e: e.reciprocal(SC[0:1, 7:8], SC[0:1, 6:7]),
         r=("SC",), w=("SC",))
    s.op("vector", lambda e: e.tensor_scalar(SC[0:1, 8:9], SC[0:1, 6:7],
                                             -MAGIC, None, ALU.mult),
         r=("SC",), w=("SC",))
    s.op("vector", lambda e: e.tensor_copy(out=SC[0:1, 9:10], in_=SC[0:1, 7:8]),
         r=("SC",), w=("SC",))
    s.op("vector", lambda e: e.tensor_copy(out=SC[0:1, 10:11], in_=SC[0:1, 6:7]),
         r=("SC",), w=("SC",))
    s.op("vector", lambda e: e.tensor_copy(out=SC[0:1, 11:12], in_=SC[0:1, 8:9]),
         r=("SC",), w=("SC",))
    s.op("vector", lambda e: e.tensor_copy(out=SC[0:1, 12:13], in_=SC[0:1, 5:6]),
         r=("SC",), w=("SC",))
    broadcast_scalars(mk, SC[0:1, 9:13], ("SC",), "bsc", "BSC", 4)

    silg, upr, sc512 = mk["silg"], mk["upr"], mk["sc512"]
    PM = mk["hmax"]
    for ic in range(IC1):
        for tt in range(4):
            par = (ic * 4 + tt) % 2
            if par == 0:
                S, U, C = silg, upr, sc512
                Sx, Ux, Cx = ("SILG",), ("UPR",), ("SC512", "ACt")
            else:
                S, U, C = mk["silgB"], mk["uprB"], mk["sc512B"]
                Sx, Ux, Cx = ("SILGB", "ACt2"), ("UPRB", "ACt2"), ("SC512B", "ACt2")
            gsl = GATE[:, ic * R + tt * 512: ic * R + (tt + 1) * 512]
            s.op("scalar", lambda e, gsl=gsl, S=S: e.activation(
                S[:, 0:512], gsl, AF.Identity, scale=mk["bsc"][:, 0:1],
                bias=mk["rcst"][:, 0:1]),
                r=ARN + ("BSC", "RCST"), w=Sx)
            s.op("scalar", lambda e, S=S, U=U: e.activation(
                U[:, 0:512], S[:, 0:512], AF.Sigmoid, scale=mk["bsc"][:, 1:2],
                bias=mk["bsc"][:, 2:3]), r=Sx + ("BSC",), w=Ux)
            s.op("scalar", lambda e, U=U, C=C: e.activation(
                C[:, 0:512], U[:, 0:512], AF.Identity, scale=mk["rcst"][:, 1:2],
                bias=mk["rcst"][:, 0:1]),
                r=Ux + ("RCST",), w=Cx)
            s.op("vector", lambda e, U=U, C=C: e.tensor_scalar(
                U[:, 0:512], C[:, 0:512], MAGIC, 65535.0, ALU.subtract, ALU.min),
                r=Cx, w=Ux)
            s.op("vector", lambda e, S=S, C=C: e.tensor_scalar(
                C[:, 0:512], S[:, 0:512], MAGIC, None, ALU.subtract),
                r=Sx, w=Cx)
            s.op("vector", lambda e, gsl=gsl, C=C, U=U: e.scalar_tensor_tensor(
                gsl, C[:, 0:512], INV216, U[:, 0:512], ALU.mult, ALU.mult),
                r=Cx + Ux, w=ARN)
        gic = GATE[:, ic * R:(ic + 1) * R]
        s.op("vector", lambda e, gic=gic, ic=ic: e.reduce_max(
            PM[:, ic:ic + 1], gic, AX.X,
            apply_absolute_value=True), r=ARN, w=("HM",))
    cross_partition_max(mk, PM, "HM", IC1, "G1")
    s.op("vector", lambda e: e.tensor_copy(out=mk["ccstage"][0:1, 0:1],
                                           in_=mk["st_g1"][0:1, 0:1]),
         r=("G1",), w=("CCS",))
    allreduce_max(mk, 1)
    s.op("vector", lambda e: e.tensor_tensor(out=SC[0:1, 13:14],
                                             in0=mk["st_gmax"][0:1, 0:1],
                                             in1=SC[0:1, 6:7], op=ALU.mult),
         r=("GMAX", "SC"), w=("SC",))
    s.op("vector", lambda e: e.tensor_scalar(SC[0:1, 14:15], SC[0:1, 13:14],
                                             INV32767, 1e-12, ALU.mult, ALU.max),
         r=("SC",), w=("SC",))
    s.op("vector", lambda e: e.reciprocal(SC[0:1, 15:16], SC[0:1, 14:15]),
         r=("SC",), w=("SC",))
    s.op("vector", lambda e: e.tensor_tensor(out=SC[0:1, 9:10], in0=SC[0:1, 6:7],
                                             in1=SC[0:1, 15:16], op=ALU.mult),
         r=("SC",), w=("SC",))
    s.op("vector", lambda e: e.tensor_copy(out=SC[0:1, 10:11], in_=SC[0:1, 5:6]),
         r=("SC",), w=("SC",))
    broadcast_scalars(mk, SC[0:1, 9:11], ("SC",), "bsc2", "BSC2", 2)

    QM = mk["hmax2"]
    upstB = ap_of(mk.t["wg_t"])[:, 0:1024].bitcast(F16)[:, 0:512]
    for ic in range(IC1):
        for tt in range(4):
            par = (ic * 4 + tt) % 2
            if par == 0:
                S, U, C = silg, upr, sc512
                Sx, Ux, Cx = ("SILG",), ("UPR",), ("SC512", "ACt")
                UP, UPk = mk["upst"], "UPST"
            else:
                S, U, C = mk["silgB"], mk["uprB"], mk["sc512B"]
                Sx, Ux, Cx = ("SILGB", "ACt2"), ("UPRB", "ACt2"), ("SC512B", "ACt2")
                UP, UPk = upstB, "WGT0"
            gsl = GATE[:, ic * R + tt * 512: ic * R + (tt + 1) * 512]
            s.op("scalar", lambda e, gsl=gsl, S=S: e.activation(
                S[:, 0:512], gsl, AF.Identity, scale=mk["bsc2"][:, 0:1],
                bias=mk["rcst"][:, 0:1]),
                r=ARN + ("BSC2", "RCST"), w=Sx)
            s.op("scalar", lambda e, S=S, C=C: e.activation(
                C[:, 0:512], S[:, 0:512], AF.Identity, scale=mk["rcst"][:, 3:4],
                bias=mk["rcst"][:, 2:3]),
                r=Sx + ("RCST",), w=Cx)
            s.dma("scalar", lambda e, ic=ic, tt=tt, UP=UP: e.dma_start(
                out=UP[:, 0:512] if not hasattr(UP, "ap") else UP[:, 0:512],
                in_=mk["up_hbm"][ic, :, tt * 512:(tt + 1) * 512]),
                r=(f"hup{ic}_{tt}",), w=(UPk,))
            s.op("scalar", lambda e, U=U, UP=UP: e.activation(
                U[:, 0:512], UP[:, 0:512], AF.Identity, scale=mk["bsc2"][:, 1:2],
                bias=mk["rcst"][:, 0:1]),
                r=(UPk, "BSC2", "RCST"), w=Ux)
            s.op("vector", lambda e, S=S, U=U: e.tensor_scalar(
                S[:, 0:512], U[:, 0:512], MAGIC, None, ALU.subtract),
                r=Ux, w=Sx)
            s.op("vector", lambda e, gsl=gsl, C=C, S=S: e.scalar_tensor_tensor(
                gsl, C[:, 0:512], INV216, S[:, 0:512], ALU.mult, ALU.mult),
                r=Cx + Sx, w=ARN)
        gic = GATE[:, ic * R:(ic + 1) * R]
        s.op("vector", lambda e, gic=gic, ic=ic: e.reduce_max(
            QM[:, ic:ic + 1], gic, AX.X,
            apply_absolute_value=True), r=ARN, w=("HM2",))
    cross_partition_max(mk, QM, "HM2", IC1, "G1")
    s.op("vector", lambda e: e.tensor_copy(out=mk["ccstage"][0:1, 0:1],
                                           in_=mk["st_g1"][0:1, 0:1]),
         r=("G1",), w=("CCS",))
    allreduce_max(mk, 1)
    s.op("vector", lambda e: e.tensor_tensor(out=SC[0:1, 13:14], in0=SC[0:1, 14:15],
                                             in1=SC[0:1, 4:5], op=ALU.mult),
         r=("SC",), w=("SC",))
    s.op("vector", lambda e: e.tensor_scalar(SC[0:1, 13:14], SC[0:1, 13:14],
                                             TWO16, None, ALU.mult),
         r=("SC",), w=("SC",))
    s.op("vector", lambda e: e.tensor_tensor(out=SC[0:1, 14:15],
                                             in0=mk["st_gmax"][0:1, 0:1],
                                             in1=SC[0:1, 13:14], op=ALU.mult),
         r=("GMAX", "SC"), w=("SC",))
    s.op("vector", lambda e: e.tensor_scalar(SC[0:1, 14:15], SC[0:1, 14:15],
                                             INV32767, 1e-12, ALU.mult, ALU.max),
         r=("SC",), w=("SC",))
    s.op("vector", lambda e: e.reciprocal(SC[0:1, 15:16], SC[0:1, 14:15]),
         r=("SC",), w=("SC",))
    s.op("vector", lambda e: e.tensor_tensor(out=SC[0:1, 9:10], in0=SC[0:1, 13:14],
                                             in1=SC[0:1, 15:16], op=ALU.mult),
         r=("SC",), w=("SC",))
    s.op("vector", lambda e: e.tensor_copy(out=SC[0:1, 10:11], in_=SC[0:1, 14:15]),
         r=("SC",), w=("SC",))
    broadcast_scalars(mk, SC[0:1, 9:11], ("SC",), "bsc2", "BSC2", 2)

    OQ = mk["H24"]
    for ic in range(IC1):
        for tt in range(4):
            par = (ic * 4 + tt) % 2
            S = silg if par == 0 else mk["silgB"]
            Sx = ("SILG",) if par == 0 else ("SILGB", "ACt2")
            gsl = GATE[:, ic * R + tt * 512: ic * R + (tt + 1) * 512]
            osl = OQ[:, ic * R + tt * 512: ic * R + (tt + 1) * 512]
            s.op("scalar", lambda e, gsl=gsl, S=S: e.activation(
                S[:, 0:512], gsl, AF.Identity, scale=mk["bsc2"][:, 0:1],
                bias=mk["rcst"][:, 0:1]),
                r=ARN + ("BSC2", "RCST"), w=Sx)
            s.op("vector", lambda e, osl=osl, S=S: e.tensor_scalar(
                osl, S[:, 0:512], MAGIC, mk["bsc2"][:, 1:2], ALU.subtract, ALU.mult),
                r=Sx + ("BSC2",), w=ARN)

    XT = mk["XT"]
    WD = mk["wd_t1"]
    for half in range(2):
        for ic in range(IC1):
            srcw = mk["b_wd1"][half * 4:(half + 1) * 4, ic, :, :].rearrange("c i h -> i c h")
            s.dma("sync", lambda e, ic=ic, srcw=srcw: e.dma_start(
                out=WD[:, ic * 512:(ic + 1) * 512].rearrange("i (c h) -> i c h", h=128),
                in_=srcw), r=("hb_b_wd1",), w=("AC",))
        for i in range(NT):
            pd, pdk = (mk["pt0"], "pt0") if i % 2 == 0 else (mk["pt1"], "pt1")
            for ic in range(IC1):
                s.op("tensor", lambda e, i=i, ic=ic, pd=pd: e.matmul(
                    pd[:, 0:512], OQ[:, ic * R + i * 128: ic * R + (i + 1) * 128],
                    WD[:, ic * 512:(ic + 1) * 512],
                    start=(ic == 0), stop=(ic == IC1 - 1)),
                    r=ARN + ("AC",), w=(pdk,), fuse=(ic != 0))
            dstx = XT[:, i * D + half * 512: i * D + (half + 1) * 512]
            s.op("vector", lambda e, dstx=dstx, pd=pd: e.scalar_tensor_tensor(
                dstx, pd[:, 0:512], 0.5, dstx, ALU.mult, ALU.add),
                r=(pdk, f"x{i}"), w=(f"x{i}",))



def sublayer_l1attn(mk):
    s = mk.s
    layernorm_stats(mk)
    ln_to_actt(mk, reorder=True)
    LNXT = mk["XQT"].rearrange("p (c t) -> p c t", t=R)

    def dump(tag, ap):
        if tag in DEBUG["dumps"]:
            o = mk.dram("dbg_" + tag, [128, 8 * R], BF16, kind="ExternalOutput")
            mk.s.dma("sync", lambda e: e.dma_start(out=o, in_=ap),
                     r=ARN + ACK + ("R1b", "R2b", "R2c"), w=())
    dump("lnxt", mk["XQT"])

    VT, QF, KF = mk["l1_VT"], mk["l1_QF"], mk["l1_KF"]
    ATD = mk["l1_ATD"]
    VW = mk["l1_VW"]
    psa = mk["pt_a"]
    WQK = mk["wg_t"]

    s.dma("gpsimd", lambda e: e.dma_start(out=mk["l1_brow"], in_=mk["bqkv_row_d"]),
          r=(), w=("R2",))
    s.dma("sync", lambda e: e.dma_start(out=mk["bqkv"][:], in_=mk["bqkv_d"]),
          r=(), w=("BQKV",))
    s.dma("sync", lambda e: e.dma_start(out=mk["msk"][:], in_=mk["msk_d"]),
          r=(), w=("MSK",))
    s.dma("sync", lambda e: e.dma_start(out=mk["onesb"][:], in_=mk["onesb_d"]),
          r=(), w=("ONESB",))
    s.op("vector", lambda e: e.tensor_scalar(mk["p_am"][:, 0:8], mk["bqkv"][:, 0:8],
                                             0.125, None, ALU.mult),
         r=("BQKV",), w=("HM2",))

    for hc in range(8):
        srcw = mk["inprojT_d"].rearrange("(c p) m -> p c m", p=128)[:, hc, 2048:3072]
        s.dma("gpsimd", lambda e, hc=hc, srcw=srcw: e.dma_start(
            out=VW[:, hc * D:(hc + 1) * D], in_=srcw), r=(), w=("R1",))
    for g in range(NT):
        if g % 2 == 0:
            ph = ((mk["pt0"], "pt0"), (mk["pt1"], "pt1"))
        else:
            ph = ((mk["pt2"], "pt2"), (mk["pt3"], "pt3"))
        for half in range(2):
            pp, ppk = ph[half]
            for hc in range(8):
                s.op("tensor", lambda e, g=g, half=half, hc=hc, pp=pp: e.matmul(
                    pp[:, 0:512],
                    LNXT[:, hc, g * 128:(g + 1) * 128],
                    VW[:, hc * D + half * 512: hc * D + (half + 1) * 512],
                    start=(hc == 0), stop=False),
                    r=("AC", "ACt", "ACt2", "R1"), w=(ppk,), fuse=(hc != 0))
            s.op("tensor", lambda e, half=half, pp=pp: e.matmul(
                pp[:, 0:512], mk["onesb"][0:1, :],
                mk["l1_brow"][0:1, 2048 + half * 512:2048 + (half + 1) * 512],
                start=False, stop=True),
                r=("ONESB", "R2"), w=(ppk,), fuse=True)
            dst = VT[:, g * D + half * 512:g * D + (half + 1) * 512]
            s.op("scalar", lambda e, dst=dst, pp=pp: e.activation(
                dst, pp[:, 0:512], AF.Identity), r=(ppk,), w=("R0",))

    dump("vt", mk["l1_VT"])
    for part, dstt, scl, dk in (("q", QF, 0.125, "R1"), ("k", KF, 1.0, "R2")):
        off = 0 if part == "q" else 1024
        for fc in range(8):
            sl = fc % 2
            wk = f"WGT{sl}"
            srcw = mk["inprojT_d"].rearrange("(c p) m -> p c m", p=128)[
                :, :, off + fc * 128: off + (fc + 1) * 128]
            s.dma("gpsimd", lambda e, srcw=srcw, sl=sl: e.dma_start(
                out=ap_of(WQK)[:, sl * D:(sl + 1) * D].rearrange(
                    "p (c i) -> p c i", i=128), in_=srcw),
                r=(), w=(wk,))
            for gg in range(4):
                pp, ppk = (mk["pt0"], "pt0") if gg % 2 == 0 else (mk["pt1"], "pt1")
                for hc in range(8):
                    rhs = LNXT[:, hc, gg * 512:(gg + 1) * 512]
                    s.op("tensor", lambda e, rhs=rhs, hc=hc, sl=sl, pp=pp: e.matmul(
                        pp[:, 0:512], WQK[:, sl * D + hc * 128:sl * D + (hc + 1) * 128],
                        rhs, start=(hc == 0), stop=(hc == 7)),
                        r=(wk, "AC", "ACt", "ACt2"), w=(ppk,), fuse=(hc != 0))
                if part == "q":
                    bias = mk["p_am"][:, fc:fc + 1]
                    bk = "HM2"
                else:
                    bias = mk["bqkv"][:, 8 + fc:8 + fc + 1]
                    bk = "BQKV"
                s.op("scalar", lambda e, dstt=dstt, fc=fc, gg=gg, scl=scl, bias=bias,
                     pp=pp: e.activation(
                         dstt[:, fc * R + gg * 512: fc * R + (gg + 1) * 512],
                         pp[:, 0:512], AF.Identity, bias=bias, scale=scl),
                     r=(ppk, bk), w=(dk,))

    dump("qf", mk["l1_QF"])
    dump("kf", mk["l1_KF"])
    attb = mk["wg_t"]
    nmx, ssum, rs = mk["st_red1"], mk["st_amax"], mk["st_rs"]
    silg, upr = mk["silg"], mk["upr"]
    idb = mk["identb"]
    for g in range(NT):
        for h in range(16):
            hh, fc = h % 2, h // 2
            psc, psck = (mk["pt2"], "pt2") if h % 2 == 0 else (mk["pt3"], "pt3")
            pst, pbk = (mk["ptb0"], "ptb0") if h % 2 == 0 else (mk["ptb1"], "ptb1")
            qv = ap_of(QF)[hh * 64:(hh + 1) * 64, fc * R + g * 128: fc * R + (g + 1) * 128]
            kv = ap_of(KF)[hh * 64:(hh + 1) * 64, fc * R + g * 128: fc * R + (g + 1) * 128]
            s.op("tensor", lambda e, qv=qv, kv=kv, psc=psc: e.matmul(
                psc[:, 0:128], qv, kv, start=True, stop=True),
                r=("R1", "R2"), w=(psck,))
            s.op("vector", lambda e, psc=psc: e.tensor_tensor(
                out=silg[:, 0:128], in0=psc[:, 0:128], in1=mk["msk"][:], op=ALU.add),
                r=(psck, "MSK"), w=("SILG",))
            s.op("vector", lambda e: e.reduce_max(nmx[:, 0:1], silg[:, 0:128], AX.X,
                                                  negate=True),
                 r=("SILG",), w=("RED1",))
            s.op("scalar", lambda e: e.activation(upr[:, 0:128], silg[:, 0:128],
                                                  AF.Exp, bias=nmx[:, 0:1]),
                 r=("SILG", "RED1"), w=("UPR",))
            s.op("vector", lambda e: e.reduce_sum(ssum[:, 0:1], upr[:, 0:128], AX.X),
                 r=("UPR",), w=("AMAX",))
            s.op("vector", lambda e: e.reciprocal(rs[:, 0:1], ssum[:, 0:1]),
                 r=("AMAX",), w=("RS",))
            s.op("vector", lambda e: e.tensor_scalar(attb[:, 0:128], upr[:, 0:128],
                                                     rs[:, 0:1], None, ALU.mult),
                 r=("UPR", "RS"), w=("WGT0",))
            s.op("tensor", lambda e, pst=pst: e.transpose(pst[:, 0:128], attb[:, 0:128],
                                                          idb[:]),
                 r=("WGT0", "IDB"), w=(pbk,))
            s.op("vector", lambda e, pst=pst: e.tensor_copy(out=attb[:, 128:256],
                                                            in_=pst[:, 0:128]),
                 r=(pbk,), w=("WGT1",))
            s.op("tensor", lambda e, g=g, h=h, hh=hh, fc=fc: e.matmul(
                psa[hh * 64:(hh + 1) * 64,
                    (h // 8) * 512 + (fc % 4) * 128:(h // 8) * 512 + (fc % 4 + 1) * 128],
                VT[:, g * D + h * 64: g * D + (h + 1) * 64],
                attb[:, 128:256], start=True, stop=True),
                r=("R0", "WGT1"), w=("pta",))
            if h in (7, 15):
                lo = 0 if h == 7 else 4
                po = (h // 8) * 512
                srcp = ap_of(psa)[:, po:po + 512].rearrange(
                    "p (sl j b) -> p sl j b", j=16, b=8)
                dst = ATD.rearrange("p (c b G j) -> p c b G j",
                                    c=8, b=8, j=16)[:, lo:lo + 4, :, g, :].rearrange(
                    "p c b j -> p c j b")
                s.op("vector", lambda e, srcp=srcp, dst=dst: e.tensor_copy(
                    out=dst, in_=srcp), r=("pta",), w=("AC", "ACt", "ACt2"))

    dump("atd", mk["l1_ATD"])
    s.dma("gpsimd", lambda e: e.dma_start(
        out=mk["l1_WOT"].rearrange("p (c m) -> p c m", m=D),
        in_=mk["outprojT1_d"].rearrange("(c p) m -> p c m", p=128)),
        r=("R1",), w=("R1",))
    s.dma("gpsimd", lambda e: e.dma_start(out=mk["l1_bout"], in_=mk["bout_row_d"]),
          r=(), w=("R1",))
    XT = mk["XT"]
    for i in range(NT):
        pd, pdk = (mk["pt0"], "pt0") if i % 2 == 0 else (mk["pt1"], "pt1")
        for half in range(2):
            for c in range(8):
                s.op("tensor", lambda e, c=c, half=half, i=i, pd=pd: e.matmul(
                    pd[:, 0:512], ATD[:, c * R + i * 128:c * R + (i + 1) * 128],
                    mk["l1_WOT"][:, c * D + half * 512: c * D + (half + 1) * 512],
                    start=(c == 0), stop=False),
                    r=("AC", "ACt", "ACt2", "R1"), w=(pdk,), fuse=(c != 0))
            s.op("tensor", lambda e, half=half, pd=pd: e.matmul(
                pd[:, 0:512], mk["onesb"][0:1, :],
                mk["l1_bout"][0:1, half * 512:(half + 1) * 512],
                start=False, stop=True),
                r=("ONESB", "R1"), w=(pdk,), fuse=True)
            dstx = XT[:, i * D + half * 512: i * D + (half + 1) * 512]
            s.op("vector", lambda e, dstx=dstx, pd=pd: e.tensor_tensor(
                out=dstx, in0=pd[:, 0:512], in1=dstx, op=ALU.add),
                r=(pdk, f"x{i}"), w=(f"x{i}",))



def alloc_common(mk, need):
    mk.dram("x_in", [R, D], F32, kind="ExternalInput")
    mk.dram("x_out", [R, D], F32, kind="ExternalOutput")
    mk.dram("identf_d", [128, 128], F32, kind="ExternalInput")
    mk.dram("identb_d", [128, 128], BF16, kind="ExternalInput")
    mk.dram("onesf_d", [1, 128], F32, kind="ExternalInput")
    if need["l0ffn"]:
        mk.dram("L0w_gate", [24 * 128, D], F32, kind="ExternalInput")
        mk.dram("L0w_up", [24 * 128, D], F32, kind="ExternalInput")
        mk.dram("L0w_down", [D, I0], F32, kind="ExternalInput")
        mk.dram("clipx", [1, 1], F32, kind="ExternalInput")
        mk.dram("clipd", [1, 1], F32, kind="ExternalInput")
        mk.dram("b_wg0", [24, 8, 128, 128], BF16, addr_space="Shared")
        mk.dram("b_wu0", [24, 8, 128, 128], BF16, addr_space="Shared")
        mk.dram("b_wd0", [8, 24, 128, 128], BF16, addr_space="Shared")
        mk.dram("b_swg0", [24, 128], F32, addr_space="Shared")
        mk.dram("b_swu0", [24, 128], F32, addr_space="Shared")
    if need["l1ffn"]:
        mk.dram("L1w_gate", [24 * 128, D], F32, kind="ExternalInput")
        mk.dram("L1w_up", [24 * 128, D], F32, kind="ExternalInput")
        mk.dram("L1w_down", [D, I1], F32, kind="ExternalInput")
        mk.dram("b_wg1", [24, 8, 128, 128], BF16, addr_space="Shared")
        mk.dram("b_wu1", [24, 8, 128, 128], BF16, addr_space="Shared")
        mk.dram("b_wd1", [8, 24, 128, 128], BF16, addr_space="Shared")
        mk.dram("up_hbm", [IC1, 128, R], F16)
    if need["l1attn"]:
        mk.dram("inprojT_d", [D, 3 * D], F32, kind="ExternalInput")
        mk.dram("bqkv_d", [128, 24], F32, kind="ExternalInput")
        mk.dram("bqkv_row_d", [1, 3 * D], F32, kind="ExternalInput")
        mk.dram("outprojT1_d", [D, D], F32, kind="ExternalInput")
        mk.dram("bout_row_d", [1, D], F32, kind="ExternalInput")
        mk.dram("msk_d", [128, 128], BF16, kind="ExternalInput")
        mk.dram("onesb_d", [1, 128], BF16, kind="ExternalInput")
    if need["l0attn"]:
        mk.dram("W_inT_d", [D, D], F32, kind="ExternalInput")
        mk.dram("W_outT_d", [D, D], F32, kind="ExternalInput")
        mk.dram("dm_d", [128, 1536], F32, kind="ExternalInput")
        mk.dram("v_eg_shard", [2, 8, 128, D], BF16)
        mk.dram("v_eg_g", [NCORES + 2, 2, 8, 128, D], BF16, addr_space="Shared")
    mk.dram("cc_in", [1, 16], F32)
    mk.dram("cc_out", [1, 16], F32, addr_space="Shared")

    mk.sb("XT", [128, NT * D], F32)
    arena = mk.sb("ARENA", [128, 24 * R], BF16)
    A = arena.ap()[:, :]
    mk.view("H", A[:, 0:IC0 * R])
    mk.view("Hf16", A.bitcast(F16)[:, 0:IC0 * R])
    Af32 = A.bitcast(F32)
    mk.view("p_wnat", Af32[:, 0:1024])
    mk.view("p_wnat3", Af32[:, 1024:4096])
    mk.view("p_scr", Af32[:, 4096:5120])
    mk.view("p_scr2", Af32[:, 5120:6144])
    mk.view("p_scr3", Af32[:, 6144:9216])
    mk.view("p_scr3b", Af32[:, 9216:12288])
    mk.view("scr1", Af32[:, 12288:14336])
    mk.view("scr2", Af32[:, 14336:16384])
    mk.view("p_stage", A[:, 32768:33792])
    mk.view("p_stage2", A[:, 33792:34816])
    mk.view("xqk", A[:, 34816:35840])
    mk.view("sc_tmp", Af32[:, 17920:18048])
    mk.view("p_amw", Af32[:, 18048:18144])
    mk.view("p_amw2", Af32[:, 18144:18240])
    actt = mk.sb("ACTT", [128, 8 * R], BF16)
    AT = actt.ap()[:, :]
    mk.view("XQT", AT)
    mk.view("wd_t", AT[:, 0:IC0 * 512])
    mk.view("wd_t1", AT[:, 0:IC1 * 512])
    mk.view("l1_ATD", AT[:, 0:NT * D])
    mk.view("sc512", AT[:, 12288:13312].bitcast(F32))
    mk.view("silgB", AT[:, 13312:14336].bitcast(F32))
    mk.view("uprB", AT[:, 14336:15360].bitcast(F32))
    mk.view("sc512B", AT[:, 15360:16384].bitcast(F32))
    mk.view("VTOK", A[:, 0:NT * D])
    mk.view("ATTOK", A[:, NT * D:2 * NT * D])
    mk.view("WOT", A[:, 2 * NT * D:2 * NT * D + 8 * D])
    mk.view("WIT", A[:, 2 * NT * D + 8 * D:2 * NT * D + 16 * D])
    mk.view("HALO", AT[:, 0:NT * D])
    mk.view("pbt", A[:, 2 * NT * D + 8 * D:2 * NT * D + 8 * D + 3072])
    mk.view("attt", A[:, 2 * NT * D + 8 * D + 3072:2 * NT * D + 8 * D + 4096])
    mk.view("GATE16", A.bitcast(F16))
    mk.view("H24", A)
    mk.view("l1_VT", A[:, 0:NT * D])
    mk.view("l1_QF", A[:, NT * D:2 * NT * D])
    mk.view("l1_KF", A[:, 2 * NT * D:3 * NT * D])
    mk.view("l1_VW", A[:, NT * D:NT * D + 8 * D])
    mk.view("l1_WOT", A[:, NT * D:NT * D + 8 * D])
    mk.view("l1_brow", A[0:1, 2 * NT * D:2 * NT * D + 3 * D])
    mk.view("l1_bout", A[0:1, NT * D + 8 * D:NT * D + 9 * D])
    mk.sb("identf", [128, 128], F32)
    mk.sb("identb", [128, 128], BF16)
    mk.sb("onesf", [1, 128], F32)
    for n in ("st_sum", "st_ssq", "st_mean", "st_var", "st_rstd", "st_nmr",
              "st_amax", "st_rs", "st_nmrs"):
        mk.sb(n, [128, NT], F32)
    mk.sb("st_red1", [128, 1], F32)
    mk.sb("st_g1", [1, 1], F32)
    mk.sb("st_gmax", [1, 16], F32)
    mk.sb("ccstage", [1, 16], F32)
    mk.sb("scal", [1, 16], F32)
    mk.sb("sxb", [128, 2], F32)
    mk.sb("shb", [128, 3], F32)
    mk.sb("clipx_t", [1, 1], F32)
    mk.sb("clipd_t", [1, 1], F32)
    mk.sb("swg", [128, 24], F32)
    mk.sb("swu", [128, 24], F32)
    mk.sb("wg_t", [128, 2048], BF16)
    mk.sb("wu_t", [128, 1024], BF16)
    mk.sb("silg", [128, 512], F32)
    mk.sb("upr", [128, 512], F32)
    mk.sb("hmax", [128, IC1 * 4], F32)
    mk.sb("hmax2", [128, IC1 * 4], F32)
    mk.view("p_am", ap_of(mk.t["hmax2"])[:, 0:32])
    mk.view("p_am2", ap_of(mk.t["hmax"])[:, 0:32])
    mk.sb("upst", [128, 512], F16)
    mk.sb("bsc", [128, 4], F32)
    mk.sb("bsc2", [128, 2], F32)
    mk.sb("rcst", [128, 4], F32)
    mk.sb("bqkv", [128, 24], F32)
    mk.sb("msk", [128, 128], BF16)
    mk.sb("onesb", [1, 128], BF16)

    mk.ps("ptb0", [128, 1024], BF16)
    mk.ps("ptb1", [128, 1024], BF16)
    mk.ps("pt_a", [128, 1024], F32)
    mk.ps("pt0", [128, 512], F32)
    mk.ps("pt1", [128, 512], F32)
    mk.ps("pt2", [128, 512], F32)
    mk.ps("pt3", [128, 512], F32)


def build_group(sublayers):
    nc = bass.Bass("TRN2", target_bir_lowering=False)
    s = Seq(nc)
    mk = MK(nc, s)
    need = {
        "l0ffn": any(x in (0, 2) for x in sublayers),
        "l0attn": 1 in sublayers,
        "l1ffn": any(x in (3, 5) for x in sublayers),
        "l1attn": 4 in sublayers,
    }
    alloc_common(mk, need)

    s.dma("sync", lambda e: e.dma_start(out=mk["identf"][:], in_=mk["identf_d"]),
          r=(), w=("IDF",))
    s.dma("sync", lambda e: e.dma_start(out=mk["identb"][:], in_=mk["identb_d"]),
          r=(), w=("IDB",))
    s.dma("sync", lambda e: e.dma_start(out=mk["onesf"][:], in_=mk["onesf_d"]),
          r=(), w=("ONESF",))
    s.op("vector", lambda e: e.memset(mk["rcst"][:, 0:1], MAGIC), r=(), w=("RCST",))
    s.op("vector", lambda e: e.memset(mk["rcst"][:, 1:2], TWO16), r=(), w=("RCST",))
    s.op("vector", lambda e: e.memset(mk["rcst"][:, 2:3], -MAGIC), r=(), w=("RCST",))
    s.op("vector", lambda e: e.memset(mk["rcst"][:, 3:4], 1.0), r=(), w=("RCST",))
    if need["l0ffn"]:
        s.dma("sync", lambda e: e.dma_start(out=mk["clipx_t"][:], in_=mk["clipx"]),
              r=(), w=("CLX",))
        s.dma("sync", lambda e: e.dma_start(out=mk["clipd_t"][:], in_=mk["clipd"]),
              r=(), w=("CLD",))
    xr = mk["x_in"].rearrange("(i p) m -> p i m", p=128)
    s.dma("sync", lambda e: e.dma_start(
        out=ap_of(mk.t["XT"])[:, :].rearrange("p (i m) -> p i m", m=D), in_=xr),
        r=(), w=xkeys())

    prep_all(mk, need["l0ffn"], need["l1ffn"])

    for sl in sublayers:
        if sl in (0, 2):
            sublayer_l0ffn(mk)
        elif sl == 1:
            sublayer_l0attn(mk)
        elif sl in (3, 5):
            sublayer_l1ffn(mk)
        elif sl == 4:
            sublayer_l1attn(mk)
        else:
            raise NotImplementedError(f"sublayer {sl}")

    xo = mk["x_out"].rearrange("(i p) m -> p i m", p=128)
    s.dma("sync", lambda e: e.dma_start(
        out=xo, in_=ap_of(mk.t["XT"])[:, :].rearrange("p (i m) -> p i m", m=D)),
        r=xkeys(), w=("hb_xout",))

    import contextlib
    with contextlib.ExitStack() as ctx:
        sems = {}
        for eng in ("scalar", "vector", "tensor", "gpsimd"):
            sems[(eng, False)] = ctx.enter_context(nc.semaphore(f"s_{eng}_c"))
        for eng in ("sync", "scalar", "gpsimd"):
            for sl in range(Seq.DMAK):
                sems[(eng, True, sl)] = ctx.enter_context(
                    nc.semaphore(f"s_{eng}_d{sl}"))
        block = ctx.enter_context(nc.Block())
        s.emit(block, sems)
    return nc



def _padw(w):
    out = np.zeros((24 * 128, w.shape[1]), np.float32)
    out[:w.shape[0]] = w
    return np.ascontiguousarray(out)


def _msk_const():
    r = np.arange(128)
    m = np.where((r[:, None] // 8) == (r[None, :] // 8), 0.0, -1e9)
    return np.ascontiguousarray(m.astype(np.float32))


def _dm_const():
    kk = np.arange(128.0)[:, None]
    rp = np.arange(384.0)[None, :]
    r, p = rp // 128, rp % 128
    dmi = kk - p + 128.0 * (r - 1.0)
    out = np.empty((128, 1536), np.float32)
    for sidx, std in enumerate((1.0, 2.0, 4.0, 8.0)):
        out[:, sidx * 384:(sidx + 1) * 384] = dmi / std + 1.0
    return out


def kernel(L0_attn_values, L0_attn_input_weights, L0_attn_output_weight,
           L0_ffn_clip_x, L0_ffn_clip_down, L0_ffn_w_gate, L0_ffn_w_up,
           L0_ffn_w_down, L1_attn_in_proj_w, L1_attn_in_proj_b,
           L1_attn_out_proj_w, L1_attn_out_proj_b, L1_ffn_w_gate,
           L1_ffn_w_up, L1_ffn_w_down):
    x = np.asarray(L0_attn_values, np.float32)
    feed = {
        "identf_d": np.eye(128, dtype=np.float32),
        "identb_d": np.eye(128, dtype=np.float32).astype(ml_dtypes.bfloat16),
        "onesf_d": np.ones((1, 128), np.float32),
        "L0w_gate": _padw(np.asarray(L0_ffn_w_gate, np.float32)),
        "L0w_up": _padw(np.asarray(L0_ffn_w_up, np.float32)),
        "L0w_down": np.ascontiguousarray(np.asarray(L0_ffn_w_down, np.float32)),
        "clipx": np.asarray(L0_ffn_clip_x, np.float32).reshape(1, 1),
        "clipd": np.asarray(L0_ffn_clip_down, np.float32).reshape(1, 1),
        "L1w_gate": _padw(np.asarray(L1_ffn_w_gate, np.float32)),
        "L1w_up": _padw(np.asarray(L1_ffn_w_up, np.float32)),
        "L1w_down": np.ascontiguousarray(np.asarray(L1_ffn_w_down, np.float32)),
        "W_inT_d": np.ascontiguousarray(np.asarray(L0_attn_input_weights, np.float32).T),
        "W_outT_d": np.ascontiguousarray(np.asarray(L0_attn_output_weight, np.float32).T),
        "dm_d": _dm_const(),
        "inprojT_d": np.ascontiguousarray(np.asarray(L1_attn_in_proj_w, np.float32).T),
        "bqkv_d": np.ascontiguousarray(
            np.asarray(L1_attn_in_proj_b, np.float32).reshape(24, 128).T),
        "bqkv_row_d": np.asarray(L1_attn_in_proj_b, np.float32).reshape(1, 3 * D),
        "outprojT1_d": np.ascontiguousarray(np.asarray(L1_attn_out_proj_w, np.float32).T),
        "bout_row_d": np.asarray(L1_attn_out_proj_b, np.float32).reshape(1, D),
        "msk_d": _msk_const().astype(ml_dtypes.bfloat16),
        "onesb_d": np.ones((1, 128), np.float32).astype(ml_dtypes.bfloat16),
    }

    cur = x
    for group in CONFIG["groups"]:
        nc = build_group(group)
        declared = {a.memorylocations[0].name
                    for a in nc.m.functions[0].allocations
                    if isinstance(a, mybir.MemoryLocationSet)
                    and a.kind == "ExternalInput"}
        in_maps = []
        for c in range(NCORES):
            xc = np.ascontiguousarray(
                cur[:, c * TLOC:(c + 1) * TLOC, :].reshape(R, D))
            m = {"x_in": xc}
            for k, v in feed.items():
                if k in declared:
                    m[k] = v
            in_maps.append(m)
        res = run_bass_kernel_spmd(nc, in_maps, core_ids=list(range(NCORES)))
        nxt = np.empty_like(cur)
        for c in range(NCORES):
            nxt[:, c * TLOC:(c + 1) * TLOC, :] = \
                res.results[c]["x_out"].reshape(B, TLOC, D)
        cur = nxt
    return cur


# version 07
# speedup vs baseline: 1.0573x; 1.0573x over previous
import numpy as np
import ml_dtypes
import concourse.bass as bass
import concourse.mybir as mybir
from concourse.bass_utils import run_bass_kernel_spmd

F32 = mybir.dt.float32
BF16 = mybir.dt.bfloat16
F16 = mybir.dt.float16
AF = mybir.ActivationFunctionType
ALU = mybir.AluOpType
AX = mybir.AxisListType
ET = mybir.EngineType

NCORES = 8
B = 8
S = 2048
D = 1024
TLOC = S // NCORES
R = B * TLOC
NT = R // 128
I0 = 2816
I1 = 3072
IC0 = I0 // 128
IC1 = I1 // 128
MAGIC = float(1.5 * 2.0 ** 23)
RG = [list(range(NCORES))]
ENGS = {"sync": ET.SP, "scalar": ET.Activation, "vector": ET.DVE,
        "tensor": ET.PE, "gpsimd": ET.Pool}

CONFIG = {"groups": [[0, 1, 2, 3, 4, 5]]}
DEBUG = {"dumps": []}

import os as _os
ELIDE_SAME_ENGINE = bool(int(_os.environ.get('MKELIDE', '0')))
ARN = ("R0", "R1", "R1b", "R2", "R2b", "R2c")
ACK = ("AC",)


class Seq:

    def __init__(self, nc):
        self.nc = nc
        self.ops = []
        self.pid = {}

    def op(self, eng, fn, r=(), w=(), fuse=False):
        if fuse and self.ops and self.ops[-1][0] == eng and not self.ops[-1][2]:
            p = self.ops[-1]
            self.ops[-1] = (eng, p[1] + [fn], False,
                            tuple(set(p[3]) | set(r)), tuple(set(p[4]) | set(w)))
        else:
            self.ops.append((eng, [fn], False, tuple(r), tuple(w)))

    def dma(self, eng, fn, r=(), w=()):
        self.ops.append((eng, [fn], True, tuple(r), tuple(w)))

    def core_id(self, name):
        if name not in self.pid:
            self.pid[name] = self.nc.partition_id(engines=[ENGS[name]])
        return self.pid[name]

    DMAK = 8

    def emit(self, block, sems):
        skey = []
        pos = []
        counts = {}
        for eng, fn, is_dma, r, w in self.ops:
            k = (eng, is_dma)
            skey.append(k)
            pos.append(counts.get(k, 0))
            counts[k] = counts.get(k, 0) + 1
        writers = {}
        readers = {}
        deps = [set() for _ in self.ops]
        for i, (eng, fn, is_dma, r, w) in enumerate(self.ops):
            for k in r:
                for j in writers.get(k, ()):
                    deps[i].add(j)
                readers.setdefault(k, []).append(i)
            for k in w:
                for j in writers.get(k, ()):
                    deps[i].add(j)
                for j in readers.get(k, ()):
                    deps[i].add(j)
                writers.setdefault(k, []).append(i)
        import os
        serial = bool(int(os.environ.get("MKSERIAL", "0")))
        def wtarget(k, p):
            if not k[1]:
                return (k, p + 1)
            return ((k[0], True, p % self.DMAK), 16 * (p // self.DMAK + 1))

        redeps = []
        for i in range(len(self.ops)):
            m = {}
            if serial:
                deps[i] = set(range(i))
            for j in deps[i]:
                if j == i:
                    continue
                kj = skey[j]
                if ELIDE_SAME_ENGINE and kj == skey[i] and not kj[1]:
                    continue
                sk, v = wtarget(kj, pos[j])
                m[sk] = max(m.get(sk, -1), v)
            redeps.append(m)

        finals = {}
        for k, n in counts.items():
            if not k[1]:
                finals[k] = n
            else:
                for sl in range(min(n, self.DMAK)):
                    uses = (n - sl + self.DMAK - 1) // self.DMAK
                    finals[(k[0], True, sl)] = 16 * uses

        def section(engobj, name):
            waited = {}

            def do_wait(sk, v):
                if v > waited.get(sk, 0):
                    engobj.wait_ge(sems[sk], v)
                    waited[sk] = v

            for i, (eng, fn, is_dma, r, w) in enumerate(self.ops):
                if eng != name:
                    continue
                if is_dma:
                    p = pos[i]
                    if p >= self.DMAK:
                        sk = (eng, True, p % self.DMAK)
                        do_wait(sk, 16 * (p // self.DMAK))
                for sk, v in sorted(redeps[i].items(), key=lambda x: str(x[0])):
                    do_wait(sk, v)
                if is_dma:
                    sk = (eng, True, pos[i] % self.DMAK)
                    fn[0](engobj).then_inc(sems[sk], 16)
                else:
                    for f in fn[:-1]:
                        f(engobj)
                    fn[-1](engobj).then_inc(sems[skey[i]], 1)
            for k, v in sorted(finals.items(), key=lambda x: str(x[0])):
                do_wait(k, v)

        for name in ("sync", "scalar", "vector", "tensor", "gpsimd"):
            getattr(block, name)(lambda e, n=name: section(e, n))
        return counts


class MK:
    def __init__(self, nc, s):
        self.nc = nc
        self.s = s
        self.t = {}

    def sb(self, name, shape, dt):
        self.t[name] = self.nc.alloc_sbuf_tensor(name, list(shape), dt)
        return self.t[name]

    def ps(self, name, shape, dt=F32):
        self.t[name] = self.nc.alloc_psum_tensor(name, list(shape), dt)
        return self.t[name]

    def dram(self, name, shape, dt, kind="Internal", addr_space="Local"):
        self.t[name] = self.nc.dram_tensor(name, list(shape), dt, kind=kind,
                                           addr_space=addr_space).ap()
        return self.t[name]

    def view(self, name, ap):
        self.t[name] = ap
        return ap

    def __getitem__(self, name):
        v = self.t[name]
        return v if isinstance(v, bass.AP) else v


def ap_of(x):
    return x if isinstance(x, bass.AP) else x.ap()[:, :]



def _lpbq_dequant_ops(mk, wv, ifull, scr, scr2, sg, isg, out_bf16, rkeys):
    s = mk.s
    ng = ifull // 32
    s.op("scalar", lambda e: e.activation(scr[:, 0:ifull], wv, AF.Abs),
         r=rkeys, w=("R0", "R1"))
    cur, oth = scr, scr2
    curw = 32
    while curw > 1:
        half = curw // 2
        v = cur[:, 0:ng * curw].rearrange("p (g w) -> p g w", w=curw)
        dv = oth[:, 0:ng * half].rearrange("p (g w) -> p g w", w=half)
        s.op("vector", lambda e, v=v, dv=dv, half=half: e.tensor_tensor(
            out=dv, in0=v[:, :, 0:half], in1=v[:, :, half:2 * half], op=ALU.max),
            r=("R0", "R1"), w=("R0", "R1"))
        cur, oth = oth, cur
        curw = half
    s.op("vector", lambda e, cur=cur: e.tensor_scalar(
        sg[:, 0:ng], cur[:, 0:ng], 1.0 / 7.0, 1e-12, ALU.mult, ALU.max),
        r=("R0", "R1"), w=("PAMW",))
    s.op("vector", lambda e: e.reciprocal(isg[:, 0:ng], sg[:, 0:ng]),
         r=("PAMW",), w=("PAMW",))
    sgb = sg[:, 0:ng].rearrange("p (g o) -> p g o", o=1).broadcast_to([128, ng, 32])
    isgb = isg[:, 0:ng].rearrange("p (g o) -> p g o", o=1).broadcast_to([128, ng, 32])
    wg_ = wv.rearrange("p (g w) -> p g w", w=32)
    scrg = scr[:, 0:ifull].rearrange("p (g w) -> p g w", w=32)
    scr2g = scr2[:, 0:ifull].rearrange("p (g w) -> p g w", w=32)
    s.op("vector", lambda e: e.tensor_tensor(out=scrg, in0=wg_, in1=isgb, op=ALU.mult),
         r=rkeys + ("PAMW",), w=("R0", "R1"))
    s.op("vector", lambda e: e.tensor_scalar(scr[:, 0:ifull], scr[:, 0:ifull],
                                             MAGIC, MAGIC, ALU.add, ALU.subtract),
         r=("R0", "R1"), w=("R0", "R1"))
    s.op("vector", lambda e: e.tensor_tensor(out=scr2g, in0=scrg, in1=sgb, op=ALU.mult),
         r=("R0", "R1", "PAMW"), w=("R0", "R1"))
    if out_bf16 is not None:
        s.op("vector", lambda e: e.tensor_copy(out=out_bf16, in_=scr2[:, 0:ifull]),
             r=("R0", "R1"), w=("R2",))


def prep_ffn_gateup(mk, wname, blobname, sblobname, lpbq):
    s = mk.s
    w = mk[wname]
    shard = mk.dram(blobname + "_shard", [3, 8, 128, 128], BF16)
    if sblobname:
        mk.dram(sblobname + "_shard", [3, 128], F32)
    wn, scr, scr2 = mk["p_wnat"], mk["p_scr"], mk["p_scr2"]
    am = mk["p_am"]
    stg, stg2 = mk["p_stage"], mk["p_stage2"]
    idb = mk["identb"]
    shk = "sh_" + blobname

    for k in range(3):
        def ld(e, k=k):
            pid = s.core_id("sync")
            return e.dma_start(out=wn[:, 0:D], in_=w[bass.ds(pid * 384 + k * 128, 128), :])
        s.dma("sync", ld, r=(), w=("R0",))
        if not lpbq:
            s.op("vector", lambda e: e.reduce_max(am[:, 0:1], wn[:, 0:D], AX.X,
                                                  apply_absolute_value=True),
                 r=("R0",), w=("HM2",))
            s.op("vector", lambda e: e.tensor_scalar(am[:, 1:2], am[:, 0:1],
                                                     1.0 / 127.0, 1e-30, ALU.mult, ALU.max),
                 r=("HM2",), w=("HM2",))
            s.op("vector", lambda e: e.reciprocal(am[:, 2:3], am[:, 1:2]),
                 r=("HM2",), w=("HM2",))
            s.dma("sync", lambda e, k=k: e.dma_start(out=mk[sblobname + "_shard"][k, :],
                                                     in_=am[:, 1:2]),
                  r=("HM2",), w=("sh_" + sblobname,))
            s.op("vector", lambda e: e.tensor_scalar(scr[:, 0:D], wn[:, 0:D],
                                                     am[:, 2:3], MAGIC, ALU.mult, ALU.add),
                 r=("R0", "HM2"), w=("R0",))
            s.op("vector", lambda e: e.tensor_scalar(stg[:, 0:D], scr[:, 0:D],
                                                     MAGIC, None, ALU.subtract),
                 r=("R0",), w=("R2",))
        else:
            _lpbq_dequant_ops(mk, wn[:, 0:D], D, scr, scr2, mk["p_am"], mk["p_am2"],
                              stg[:, 0:D], ("R0",))
        pst = mk["ptb0"] if k % 2 == 0 else mk["ptb1"]
        pbk = "ptb0" if k % 2 == 0 else "ptb1"
        for c in range(8):
            s.op("tensor", lambda e, c=c, pst=pst: e.transpose(
                pst[:, c * 128:(c + 1) * 128], stg[:, c * 128:(c + 1) * 128], idb[:]),
                r=("R2", "IDB"), w=(pbk,), fuse=(c != 0))
        s.op("vector", lambda e, pst=pst: e.tensor_copy(out=stg2[:, 0:D], in_=pst[:, 0:D]),
             r=(pbk,), w=("R2",))
        s.dma("sync", lambda e, k=k: e.dma_start(
            out=shard[k].rearrange("c h i -> h c i"),
            in_=stg2[:, 0:D].rearrange("h (c i) -> h c i", i=128)),
            r=("R2",), w=(shk,))


def prep_ffn_down(mk, wname, blobname, lpbq, ifull):
    s = mk.s
    w = mk[wname]
    shard = mk.dram(blobname + "_shard", [1, 24, 128, 128], BF16)
    wn, scr, scr2 = mk["p_wnat3"], mk["p_scr3"], mk["p_scr3b"]
    am = mk["p_am"]
    stg, stg2 = mk["p_stage"], mk["p_stage2"]
    idb = mk["identb"]
    shk = "sh_" + blobname

    def ld(e):
        pid = s.core_id("sync")
        return e.dma_start(out=wn[:, 0:ifull], in_=w[bass.ds(pid * 128, 128), :])
    s.dma("sync", ld, r=(), w=("R0",))
    if not lpbq:
        s.op("vector", lambda e: e.reduce_max(am[:, 0:1], wn[:, 0:ifull], AX.X,
                                              apply_absolute_value=True),
             r=("R0",), w=("HM2",))
        s.op("vector", lambda e: e.tensor_scalar(am[:, 1:2], am[:, 0:1],
                                                 1.0 / 127.0, 1e-30, ALU.mult, ALU.max),
             r=("HM2",), w=("HM2",))
        s.op("vector", lambda e: e.reciprocal(am[:, 2:3], am[:, 1:2]),
             r=("HM2",), w=("HM2",))
        s.op("vector", lambda e: e.tensor_scalar(scr[:, 0:ifull], wn[:, 0:ifull],
                                                 am[:, 2:3], MAGIC, ALU.mult, ALU.add),
             r=("R0", "HM2"), w=("R0", "R1"))
        s.op("vector", lambda e: e.tensor_scalar(scr2[:, 0:ifull], scr[:, 0:ifull],
                                                 MAGIC, am[:, 1:2], ALU.subtract, ALU.mult),
             r=("R0", "R1", "HM2"), w=("R1",))
    else:
        _lpbq_dequant_ops(mk, wn[:, 0:ifull], ifull, scr, scr2, mk["p_amw"],
                          mk["p_amw2"], None, ("R0",))
    nic_real = ifull // 128
    for ic in range(24):
        ic_src = min(ic, nic_real - 1)
        cc = ic % 8
        pst = mk["ptb0"] if (ic // 8) % 2 == 0 else mk["ptb1"]
        pbk = "ptb0" if (ic // 8) % 2 == 0 else "ptb1"
        s.op("vector", lambda e, ics=ic_src, cc=cc: e.tensor_copy(
            out=stg[:, cc * 128:(cc + 1) * 128],
            in_=scr2[:, ics * 128:(ics + 1) * 128]),
            r=("R0", "R1"), w=("R2",))
        s.op("tensor", lambda e, cc=cc, pst=pst: e.transpose(
            pst[:, cc * 128:(cc + 1) * 128], stg[:, cc * 128:(cc + 1) * 128], idb[:]),
            r=("R2", "IDB"), w=(pbk,))
        if cc == 7 or ic == 24 - 1:
            lo, n = ic - cc, cc + 1
            s.op("vector", lambda e, n=n, pst=pst: e.tensor_copy(
                out=stg2[:, 0:n * 128], in_=pst[:, 0:n * 128]),
                r=(pbk,), w=("R2",))
            s.dma("sync", lambda e, lo=lo, n=n: e.dma_start(
                out=shard[0, lo:lo + n].rearrange("c i h -> i c h"),
                in_=stg2[:, 0:n * 128].rearrange("i (c h) -> i c h", h=128)),
                r=("R2",), w=(shk,))


def prep_all(mk, want_l0, want_l1):
    s = mk.s
    pairs = []
    if want_l0:
        prep_ffn_gateup(mk, "L0w_gate", "b_wg0", "b_swg0", False)
        prep_ffn_gateup(mk, "L0w_up", "b_wu0", "b_swu0", False)
        prep_ffn_down(mk, "L0w_down", "b_wd0", False, I0)
        pairs += [("b_wg0", "sh_b_wg0"), ("b_swg0", "sh_b_swg0"),
                  ("b_wu0", "sh_b_wu0"), ("b_swu0", "sh_b_swu0"),
                  ("b_wd0", "sh_b_wd0")]
    if want_l1:
        prep_ffn_gateup(mk, "L1w_gate", "b_wg1", None, True)
        prep_ffn_gateup(mk, "L1w_up", "b_wu1", None, True)
        prep_ffn_down(mk, "L1w_down", "b_wd1", True, I1)
        pairs += [("b_wg1", "sh_b_wg1"), ("b_wu1", "sh_b_wu1"),
                  ("b_wd1", "sh_b_wd1")]
    for blob, shk in pairs:
        i_ = mk[blob + "_shard"][:]
        o_ = mk[blob][:]
        s.op("gpsimd", lambda e, i_=i_, o_=o_: e.collective_compute(
            "AllGather", ALU.bypass, replica_groups=RG, ins=[i_], outs=[o_]),
            r=(shk,), w=("hb_" + blob,))



def xkeys():
    return tuple(f"x{i}" for i in range(NT))


def layernorm_stats(mk):
    s = mk.s
    XT, scr = mk["XT"], mk["scr1"]
    SUMS, SSQ = mk["st_sum"], mk["st_ssq"]
    MEAN, RSTD, NMR, VAR = mk["st_mean"], mk["st_rstd"], mk["st_nmr"], mk["st_var"]
    scrB = mk["scr2"]
    for i in range(NT):
        X = XT[:, i * D:(i + 1) * D]
        SS, SSk = (scr, "R1") if i % 2 == 0 else (scrB, "R1b")
        s.op("scalar", lambda e, X=X, SS=SS: e.activation(SS[:, 0:D], X, AF.Square),
             r=(f"x{i}",), w=(SSk,))
        s.op("vector", lambda e, i=i, SS=SS: e.reduce_sum(SSQ[:, i:i + 1], SS[:, 0:D], AX.X),
             r=(SSk,), w=("SSQ",))
        s.op("vector", lambda e, X=X, i=i: e.reduce_sum(SUMS[:, i:i + 1], X, AX.X),
             r=(f"x{i}",), w=("SUMS",))
    s.op("vector", lambda e: e.tensor_scalar(MEAN[:], SUMS[:], 1.0 / D, None, ALU.mult),
         r=("SUMS",), w=("MEAN",))
    s.op("vector", lambda e: e.tensor_tensor(out=VAR[:], in0=MEAN[:], in1=MEAN[:],
                                             op=ALU.mult), r=("MEAN",), w=("VAR",))
    s.op("vector", lambda e: e.scalar_tensor_tensor(VAR[:], SSQ[:], 1.0 / D, VAR[:],
                                                    ALU.mult, ALU.subtract),
         r=("SSQ", "VAR"), w=("VAR",))
    s.op("vector", lambda e: e.tensor_scalar(SUMS[:], VAR[:], 1e-5, None, ALU.add),
         r=("VAR",), w=("SUMS",))
    s.op("scalar", lambda e: e.activation(RSTD[:], SUMS[:], AF.Sqrt),
         r=("SUMS",), w=("RSTD",))
    s.op("vector", lambda e: e.reciprocal(RSTD[:], RSTD[:]), r=("RSTD",), w=("RSTD",))
    s.op("vector", lambda e: e.scalar_tensor_tensor(NMR[:], MEAN[:], -1.0, RSTD[:],
                                                    ALU.mult, ALU.mult),
         r=("MEAN", "RSTD"), w=("NMR",))


def cross_partition_max(mk, AM, amkey, ncols, dstkey):
    s = mk.s
    red1, idf = mk["st_red1"], mk["identf"]
    pt = mk["pt2"]
    s.op("vector", lambda e: e.reduce_max(red1[:, 0:1], AM[:, 0:ncols], AX.X),
         r=(amkey,), w=("RED1",))
    s.op("tensor", lambda e: e.transpose(pt[0:1, 0:128], red1[:, 0:1], idf[:]),
         r=("RED1", "IDF"), w=("pt2",))
    s.op("vector", lambda e: e.reduce_max(mk["st_g1"][0:1, 0:1], pt[0:1, 0:128], AX.X),
         r=("pt2",), w=(dstkey,))


def global_absmax_of_ln(mk):
    s = mk.s
    XT, scr = mk["XT"], mk["scr1"]
    RSTD, NMR, AM = mk["st_rstd"], mk["st_nmr"], mk["st_amax"]
    scrB = mk["scr2"]
    for i in range(NT):
        X = XT[:, i * D:(i + 1) * D]
        SS, SSk = (scr, "R1") if i % 2 == 0 else (scrB, "R1b")
        s.op("scalar", lambda e, X=X, i=i, SS=SS: e.activation(
            SS[:, 0:D], X, AF.Abs, bias=NMR[:, i:i + 1], scale=RSTD[:, i:i + 1]),
            r=(f"x{i}", "RSTD", "NMR"), w=(SSk,))
        s.op("vector", lambda e, i=i, SS=SS: e.reduce_max(AM[:, i:i + 1], SS[:, 0:D], AX.X),
             r=(SSk,), w=("AMAX",))
    cross_partition_max(mk, AM, "AMAX", NT, "G1")


def allreduce_max(mk, n):
    s = mk.s
    cin, cout = mk["cc_in"], mk["cc_out"]
    s.dma("sync", lambda e: e.dma_start(out=cin[0:1, 0:n], in_=mk["ccstage"][0:1, 0:n]),
          r=("CCS",), w=("hb_cc",))
    s.op("gpsimd", lambda e: e.collective_compute(
        "AllReduce", ALU.max, replica_groups=RG,
        ins=[cin[0:1, 0:n]], outs=[cout[0:1, 0:n]]), r=("hb_cc",), w=("hb_cc2",))
    s.dma("sync", lambda e: e.dma_start(out=mk["st_gmax"][0:1, 0:n], in_=cout[0:1, 0:n]),
          r=("hb_cc2",), w=("GMAX",))


def broadcast_scalars(mk, src_ap, srckeys, dst, dkey, ncols):
    s = mk.s
    pt = mk["pt2"]
    s.op("tensor", lambda e: e.matmul(pt[:, 0:ncols], mk["onesf"][0:1, :], src_ap,
                                      start=True, stop=True),
         r=srckeys + ("ONESF",), w=("pt2",))
    s.op("vector", lambda e: e.tensor_copy(out=mk[dst][:, 0:ncols], in_=pt[:, 0:ncols]),
         r=("pt2",), w=(dkey,))


def quantize_x_to_xqt(mk, mode="int8"):
    s = mk.s
    XT, scr, scrb = mk["XT"], mk["scr1"], mk["scr2"]
    RSTD, NMR, RS, NMRS = mk["st_rstd"], mk["st_nmr"], mk["st_rs"], mk["st_nmrs"]
    XQK, XQT = mk["xqk"], mk["XQT"]
    idb = mk["identb"]
    s.op("vector", lambda e: e.tensor_scalar(RS[:], RSTD[:], mk["sxb"][:, 1:2],
                                             None, ALU.mult),
         r=("RSTD", "SXB"), w=("RS",))
    s.op("vector", lambda e: e.tensor_scalar(NMRS[:], NMR[:], mk["sxb"][:, 1:2],
                                             None, ALU.mult),
         r=("NMR", "SXB"), w=("NMRS",))
    for i in range(NT):
        X = XT[:, i * D:(i + 1) * D]
        if i % 2 == 0:
            SA, SB, XO = scr, scrb, XQK
            SAk, SBk, XOk = "R1", "R1b", "R2"
        else:
            SA, SB, XO = mk["p_scr"], mk["p_scr2"], mk["attt"]
            SAk, SBk, XOk = "R0", "R0", "R2c"
        s.op("scalar", lambda e, X=X, i=i, SA=SA: e.activation(
            SA[:, 0:D], X, AF.Identity, bias=NMRS[:, i:i + 1], scale=RS[:, i:i + 1]),
            r=(f"x{i}", "RS", "NMRS"), w=(SAk,))
        if mode == "int8":
            s.op("vector", lambda e, SA=SA, SB=SB: e.tensor_scalar(
                SB[:, 0:D], SA[:, 0:D], MAGIC, MAGIC, ALU.add, ALU.subtract),
                r=(SAk,), w=(SBk,))
            s.op("vector", lambda e, SB=SB, XO=XO: e.tensor_scalar(
                XO[:, 0:D], SB[:, 0:D], 127.0, -128.0, ALU.min, ALU.max),
                r=(SBk,), w=(XOk,))
        else:
            s.op("vector", lambda e, SA=SA, SB=SB: e.tensor_scalar(
                SB[:, 0:D], SA[:, 0:D], MAGIC, None, ALU.add),
                r=(SAk,), w=(SBk,))
            s.op("vector", lambda e, SB=SB, XO=XO: e.tensor_scalar(
                XO[:, 0:D], SB[:, 0:D], MAGIC, mk["sxb"][:, 0:1],
                ALU.subtract, ALU.mult),
                r=(SBk, "SXB"), w=(XOk,))
        pst = mk["ptb0"] if i % 2 == 0 else mk["ptb1"]
        pbk = "ptb0" if i % 2 == 0 else "ptb1"
        for c in range(8):
            s.op("tensor", lambda e, c=c, pst=pst, XO=XO: e.transpose(
                pst[:, c * 128:(c + 1) * 128], XO[:, c * 128:(c + 1) * 128], idb[:]),
                r=(XOk, "IDB"), w=(pbk,), fuse=(c != 0))
        dst = XQT.rearrange("p (c t) -> p c t", t=R)[:, :, i * 128:(i + 1) * 128]
        srcv = ap_of(pst)[:, 0:D].rearrange("p (c t) -> p c t", t=128)
        s.op("vector", lambda e, dst=dst, srcv=srcv: e.tensor_copy(out=dst, in_=srcv),
             r=(pbk,), w=("AC", "ACt", "ACt2"))


def load_scale_blob(mk, blobname, dstname, dkey, nic):
    s = mk.s
    tmp, idf = mk["sc_tmp"], mk["identf"]
    pt = mk["pt2"]
    s.dma("sync", lambda e: e.dma_start(out=tmp[0:24, :], in_=mk[blobname][:, :]),
          r=("hb_" + blobname,), w=("R2",))
    s.op("tensor", lambda e: e.transpose(pt[:, 0:24], tmp[0:24, :], idf[0:24, 0:24]),
         r=("R2", "IDF"), w=("pt2",))
    s.op("vector", lambda e: e.tensor_copy(out=mk[dstname][:, 0:nic], in_=pt[:, 0:nic]),
         r=("pt2",), w=(dkey,))



def sublayer_l0ffn(mk):
    s = mk.s

    def dump(tag, ap, shape, dt, rk):
        if tag in DEBUG["dumps"] and ("dbg_" + tag) not in mk.t:
            o = mk.dram("dbg_" + tag, list(shape), dt, kind="ExternalOutput")
            mk.s.dma("sync", lambda e: e.dma_start(out=o, in_=ap), r=rk, w=())
    layernorm_stats(mk)
    global_absmax_of_ln(mk)
    SC = mk["scal"]
    s.op("vector", lambda e: e.tensor_copy(out=mk["ccstage"][0:1, 0:1],
                                           in_=mk["st_g1"][0:1, 0:1]),
         r=("G1",), w=("CCS",))
    allreduce_max(mk, 1)
    s.op("scalar", lambda e: e.activation(SC[0:1, 0:1], mk["clipx_t"][0:1, 0:1],
                                          AF.Sigmoid), r=("CLX",), w=("SC",))
    s.op("vector", lambda e: e.scalar_tensor_tensor(
        SC[0:1, 1:2], SC[0:1, 0:1], 1.0 / 127.0, mk["st_gmax"][0:1, 0:1],
        ALU.mult, ALU.mult), r=("SC", "GMAX"), w=("SC",))
    s.op("vector", lambda e: e.reciprocal(SC[0:1, 2:3], SC[0:1, 1:2]),
         r=("SC",), w=("SC",))
    broadcast_scalars(mk, SC[0:1, 1:3], ("SC",), "sxb", "SXB", 2)
    dump("rstd", ap_of(mk.t["st_rstd"]), (128, 16), F32, ("RSTD",))
    dump("sxb", ap_of(mk.t["sxb"]), (128, 2), F32, ("SXB",))
    quantize_x_to_xqt(mk)
    dump("xqt", mk["XQT"], (128, 8 * R), BF16, ("AC", "ACt"))

    load_scale_blob(mk, "b_swg0", "swg", "SWG", IC0)
    load_scale_blob(mk, "b_swu0", "swu", "SWU", IC0)
    s.op("vector", lambda e: e.tensor_scalar(mk["swg"][:, 0:IC0], mk["swg"][:, 0:IC0],
                                             mk["sxb"][:, 0:1], None, ALU.mult),
         r=("SWG", "SXB"), w=("SWG",))
    s.op("vector", lambda e: e.tensor_scalar(mk["swu"][:, 0:IC0], mk["swu"][:, 0:IC0],
                                             mk["sxb"][:, 0:1], None, ALU.mult),
         r=("SWU", "SXB"), w=("SWU",))

    XQTv = mk["XQT"].rearrange("p (c t) -> p c t", t=R)
    WG, WU = mk["wg_t"], mk["wu_t"]
    silg, upr = mk["silg"], mk["upr"]
    Hf16 = mk["Hf16"]
    HM = mk["hmax"]
    for ic in range(IC0):
        sl = ic % 2
        wgk, wuk = f"WGT{sl}", "WUT0"
        s.dma("sync", lambda e, ic=ic, sl=sl: e.dma_start(
            out=ap_of(WG)[:, sl * D:(sl + 1) * D].rearrange("p (c i) -> p c i", i=128),
            in_=mk["b_wg0"][ic].rearrange("c p i -> p c i")),
            r=("hb_b_wg0", "hb_b_wd0"), w=(wgk,))
        s.dma("sync", lambda e, ic=ic: e.dma_start(
            out=ap_of(WU)[:, 0:D].rearrange("p (c i) -> p c i", i=128),
            in_=mk["b_wu0"][ic].rearrange("c p i -> p c i")),
            r=("hb_b_wu0", "hb_b_wd0"), w=(wuk,))
        for tt in range(4):
            pg, pgk = (mk["pt0"], "pt0") if tt % 2 == 0 else (mk["pt1"], "pt1")
            pu, puk = (mk["pt2"], "pt2") if tt % 2 == 0 else (mk["pt3"], "pt3")
            for hc in range(8):
                s.op("tensor", lambda e, hc=hc, tt=tt, sl=sl, pg=pg: e.matmul(
                    pg[:, 0:512], WG[:, sl * D + hc * 128:sl * D + (hc + 1) * 128],
                    XQTv[:, hc, tt * 512:(tt + 1) * 512],
                    start=(hc == 0), stop=(hc == 7)),
                    r=(wgk, "AC", "ACt", "ACt2"), w=(pgk,), fuse=(hc != 0))
            for hc in range(8):
                s.op("tensor", lambda e, hc=hc, tt=tt, pu=pu: e.matmul(
                    pu[:, 0:512], WU[:, hc * 128:(hc + 1) * 128],
                    XQTv[:, hc, tt * 512:(tt + 1) * 512],
                    start=(hc == 0), stop=(hc == 7)),
                    r=(wuk, "AC", "ACt", "ACt2"), w=(puk,), fuse=(hc != 0))
            s.op("scalar", lambda e, ic=ic, pg=pg: e.activation(
                silg[:], pg[:, 0:512], AF.Silu, scale=mk["swg"][:, ic:ic + 1]),
                r=(pgk, "SWG"), w=("SILG",))
            s.op("scalar", lambda e, ic=ic, pu=pu: e.activation(
                upr[:], pu[:, 0:512], AF.Identity, scale=mk["swu"][:, ic:ic + 1]),
                r=(puk, "SWU"), w=("UPR",))
            hv = Hf16[:, ic * R + tt * 512: ic * R + (tt + 1) * 512]
            s.op("vector", lambda e, hv=hv: e.tensor_tensor(out=hv, in0=silg[:],
                                                            in1=upr[:], op=ALU.mult),
                 r=("SILG", "UPR"), w=ARN)
        hic = Hf16[:, ic * R:(ic + 1) * R]
        s.op("vector", lambda e, hic=hic, ic=ic: e.reduce_max(
            HM[:, ic:ic + 1], hic, AX.X,
            apply_absolute_value=True), r=ARN, w=("HM",))

    dump("hf", mk["Hf16"], (128, IC0 * R), F16, ARN)
    dump("swg", ap_of(mk.t["swg"]), (128, 24), F32, ("SWG",))
    cross_partition_max(mk, HM, "HM", IC0, "G1")
    s.op("vector", lambda e: e.tensor_copy(out=mk["ccstage"][0:1, 0:1],
                                           in_=mk["st_g1"][0:1, 0:1]),
         r=("G1",), w=("CCS",))
    allreduce_max(mk, 1)
    s.op("scalar", lambda e: e.activation(SC[0:1, 0:1], mk["clipd_t"][0:1, 0:1],
                                          AF.Sigmoid), r=("CLD",), w=("SC",))
    s.op("vector", lambda e: e.scalar_tensor_tensor(
        SC[0:1, 1:2], SC[0:1, 0:1], 1.0 / 127.0, mk["st_gmax"][0:1, 0:1],
        ALU.mult, ALU.mult), r=("SC", "GMAX"), w=("SC",))
    s.op("vector", lambda e: e.reciprocal(SC[0:1, 2:3], SC[0:1, 1:2]),
         r=("SC",), w=("SC",))
    s.op("vector", lambda e: e.tensor_scalar(SC[0:1, 3:4], SC[0:1, 1:2], 0.5,
                                             None, ALU.mult), r=("SC",), w=("SC",))
    broadcast_scalars(mk, SC[0:1, 1:4], ("SC",), "shb", "SHB", 3)

    HQ = mk["H"]
    for ic in range(IC0):
        for tt in range(4):
            par = (ic * 4 + tt) % 2
            S, Sk = (silg, "SILG") if par == 0 else (mk["silgB"], "SILGB")
            U, Uk = (upr, "UPR") if par == 0 else (mk["uprB"], "UPRB")
            Sx = (Sk,) if par == 0 else (Sk, "ACt2")
            Ux = (Uk,) if par == 0 else (Uk, "ACt2")
            hv = Hf16[:, ic * R + tt * 512: ic * R + (tt + 1) * 512]
            hq = HQ[:, ic * R + tt * 512: ic * R + (tt + 1) * 512]
            s.op("vector", lambda e, hv=hv, S=S: e.tensor_scalar(
                S[:, 0:512], hv, mk["shb"][:, 1:2], MAGIC, ALU.mult, ALU.add),
                r=ARN + ("SHB",), w=Sx)
            s.op("vector", lambda e, S=S, U=U: e.tensor_scalar(
                U[:, 0:512], S[:, 0:512], MAGIC, 127.0, ALU.subtract, ALU.min),
                r=Sx, w=Ux)
            s.op("vector", lambda e, hq=hq, U=U: e.tensor_scalar(
                hq, U[:, 0:512], -128.0, None, ALU.max), r=Ux, w=ARN)

    dump("hq", mk["H"], (128, IC0 * R), BF16, ARN)
    XT = mk["XT"]
    WD = mk["wd_t"]
    for half in range(2):
        for ic in range(IC0):
            src = mk["b_wd0"][half * 4:(half + 1) * 4, ic, :, :].rearrange("c i h -> i c h")
            s.dma("sync", lambda e, ic=ic, src=src: e.dma_start(
                out=WD[:, ic * 512:(ic + 1) * 512].rearrange("i (c h) -> i c h", h=128),
                in_=src), r=("hb_b_wd0",), w=("AC",))
        for i in range(NT):
            pd, pdk = (mk["pt0"], "pt0") if i % 2 == 0 else (mk["pt1"], "pt1")
            for ic in range(IC0):
                s.op("tensor", lambda e, i=i, ic=ic, pd=pd: e.matmul(
                    pd[:, 0:512], HQ[:, ic * R + i * 128: ic * R + (i + 1) * 128],
                    WD[:, ic * 512:(ic + 1) * 512],
                    start=(ic == 0), stop=(ic == IC0 - 1)),
                    r=ARN + ("AC",), w=(pdk,), fuse=(ic != 0))
            dstx = XT[:, i * D + half * 512: i * D + (half + 1) * 512]
            s.op("vector", lambda e, dstx=dstx, pd=pd: e.scalar_tensor_tensor(
                dstx, pd[:, 0:512], mk["shb"][:, 2:3], dstx, ALU.mult, ALU.add),
                r=(pdk, "SHB", f"x{i}"), w=(f"x{i}",))



ATTN_STD = [1.0, 2.0, 4.0, 8.0, 1.0, 2.0, 4.0, 8.0]


def ln_to_actt(mk, reorder=False):
    s = mk.s
    XT = mk["XT"]
    XQK, XQT = mk["xqk"], mk["XQT"]
    idb = mk["identb"]
    for i in range(NT):
        X = XT[:, i * D:(i + 1) * D]
        XO, XOk = (XQK, "R2") if i % 2 == 0 else (mk["attt"], "R2c")
        s.op("scalar", lambda e, X=X, i=i, XO=XO: e.activation(
            XO[:, 0:D], X, AF.Identity, bias=mk["st_nmr"][:, i:i + 1],
            scale=mk["st_rstd"][:, i:i + 1]),
            r=(f"x{i}", "RSTD", "NMR"), w=(XOk,))
        pst = mk["ptb0"] if i % 2 == 0 else mk["ptb1"]
        pbk = "ptb0" if i % 2 == 0 else "ptb1"
        for c in range(8):
            s.op("tensor", lambda e, c=c, pst=pst, XO=XO: e.transpose(
                pst[:, c * 128:(c + 1) * 128], XO[:, c * 128:(c + 1) * 128], idb[:]),
                r=(XOk, "IDB"), w=(pbk,), fuse=(c != 0))
        if not reorder:
            dst = XQT.rearrange("p (c t) -> p c t", t=R)[:, :, i * 128:(i + 1) * 128]
            srcv = ap_of(pst)[:, 0:D].rearrange("p (c t) -> p c t", t=128)
        else:
            b, tc = i // 2, i % 2
            dst = XQT.rearrange("p (c G j bb) -> p c G j bb", c=8, G=16, j=16, bb=8)[
                :, :, tc * 8:(tc + 1) * 8, :, b]
            srcv = ap_of(pst)[:, 0:D].rearrange(
                "p (c gl j) -> p c gl j", c=8, gl=8, j=16)
        s.op("vector", lambda e, dst=dst, srcv=srcv: e.tensor_copy(out=dst, in_=srcv),
             r=(pbk,), w=("AC", "ACt", "ACt2"))


def sublayer_l0attn(mk):
    import math
    s = mk.s
    layernorm_stats(mk)
    ln_to_actt(mk)

    s.dma("gpsimd", lambda e: e.dma_start(
        out=mk["WIT"].rearrange("p (c m) -> p c m", m=D),
        in_=mk["W_inT_d"].rearrange("(c p) m -> p c m", p=128)),
        r=(), w=("R2", "R2c"))
    LNXT = mk["XQT"].rearrange("p (c t) -> p c t", t=R)
    VTOK = mk["VTOK"]
    for i in range(NT):
        if i % 2 == 0:
            ph = ((mk["pt0"], "pt0"), (mk["pt1"], "pt1"))
        else:
            ph = ((mk["pt2"], "pt2"), (mk["pt3"], "pt3"))
        for half in range(2):
            pp, ppk = ph[half]
            for hc in range(8):
                s.op("tensor", lambda e, i=i, half=half, hc=hc, pp=pp: e.matmul(
                    pp[:, 0:512],
                    LNXT[:, hc, i * 128:(i + 1) * 128],
                    mk["WIT"][:, hc * D + half * 512: hc * D + (half + 1) * 512],
                    start=(hc == 0), stop=(hc == 7)),
                    r=("AC", "ACt", "ACt2", "R2"), w=(ppk,), fuse=(hc != 0))
            dst = VTOK[:, i * D + half * 512:i * D + (half + 1) * 512]
            s.op("scalar", lambda e, dst=dst, pp=pp: e.activation(
                dst, pp[:, 0:512], AF.Identity), r=(ppk,), w=("R0",))

    PBT = mk["pbt"]
    dm = mk["scr1"]
    s.dma("sync", lambda e: e.dma_start(out=dm[:, 0:1536], in_=mk["dm_d"]),
          r=(), w=("R1",))
    for h in range(8):
        std = ATTN_STD[h]
        sidx = {1.0: 0, 2.0: 1, 4.0: 2, 8.0: 3}[std]
        inv_norm = 1.0 / (std * math.sqrt(2.0 * math.pi))
        s.op("scalar", lambda e, sidx=sidx: e.activation(
            mk["scr2"][:, 0:384], dm[:, sidx * 384:(sidx + 1) * 384], AF.Square),
            r=("R1",), w=("R1b",))
        s.op("scalar", lambda e: e.activation(
            mk["scr2"][:, 384:768], mk["scr2"][:, 0:384], AF.Exp, scale=-0.5),
            r=("R1b",), w=("R1b",))
        s.op("vector", lambda e, h=h, inv_norm=inv_norm: e.tensor_scalar(
            PBT[:, h * 384:(h + 1) * 384], mk["scr2"][:, 384:768],
            float(inv_norm), None, ALU.mult), r=("R1b",), w=("R2",))

    vshard = mk["v_eg_shard"]
    for side in range(2):
        src = VTOK.rearrange("p (b tc m) -> p b tc m", tc=2, m=D)[:, :, side, :]
        s.dma("sync", lambda e, side=side, src=src: e.dma_start(
            out=vshard[side].rearrange("b p m -> p b m"), in_=src),
            r=("R0",), w=("hb_vsh",))
    HALO = mk["HALO"]
    s.op("vector", lambda e: e.memset(HALO, 0.0), r=("AC", "ACt"), w=("AC", "ACt", "ACt2"))
    zsrc = HALO.rearrange("p (s b m) -> p s b m", s=2, m=D)
    for g in (0, NCORES + 1):
        s.dma("sync", lambda e, g=g: e.dma_start(
            out=mk["v_eg_g"][g].rearrange("s b p m -> p s b m"), in_=zsrc),
            r=("AC", "ACt", "ACt2"), w=("hb_veg",))
    s.op("gpsimd", lambda e: e.collective_compute(
        "AllGather", ALU.bypass, replica_groups=RG,
        ins=[vshard[:]], outs=[mk["v_eg_g"][1:NCORES + 1]]),
        r=("hb_vsh", "hb_veg"), w=("hb_veg",))

    def ld_halo(e, hside):
        pid = s.core_id("sync")
        slot = pid if hside == 0 else pid + 2
        src = mk["v_eg_g"][bass.ds(slot, 1), 1 - hside].rearrange("o b p m -> p (o b) m")
        dst = HALO.rearrange("p (g m) -> p g m", m=D)[:, hside * 8:(hside + 1) * 8, :]
        return e.dma_start(out=dst, in_=src)
    s.dma("sync", lambda e: ld_halo(e, 0), r=("hb_veg",), w=("AC", "ACt", "ACt2"))
    s.dma("sync", lambda e: ld_halo(e, 1), r=("hb_veg",), w=("AC", "ACt", "ACt2"))

    ATTOK = mk["ATTOK"]
    psa = mk["pt_a"]
    for tc in range(2):
        for h in range(8):
            if h % 2 == 0:
                ph = ((psa, "pta", 0), (psa, "pta", 512))
            else:
                ph = ((mk["pt0"], "pt0", 0), (mk["pt1"], "pt1", 0))
            for half in range(2):
                pp, ppk, po = ph[half]
                order = [r for r in range(3) if 0 <= tc + r - 1 <= 1] + \
                        [r for r in range(3) if not (0 <= tc + r - 1 <= 1)]
                for j, r in enumerate(order):
                    kc = tc + r - 1
                    if kc == -1:
                        rhs = HALO.rearrange("p (g m) -> p g m", m=D)[
                            :, half * 4:half * 4 + 4, h * 128:(h + 1) * 128]
                        rk = "ACh"
                    elif kc == 2:
                        rhs = HALO.rearrange("p (g m) -> p g m", m=D)[
                            :, 8 + half * 4:8 + half * 4 + 4, h * 128:(h + 1) * 128]
                        rk = "ACh"
                    else:
                        rhs = VTOK.rearrange("p (b tc m) -> p b tc m", tc=2, m=D)[
                            :, half * 4:half * 4 + 4, kc, h * 128:(h + 1) * 128]
                        rk = "R0"
                    rks = ("AC", "ACt", "ACt2") if rk == "ACh" else (rk,)
                    s.op("tensor", lambda e, rhs=rhs, h=h, r=r, pp=pp, po=po,
                         j=j: e.matmul(
                        pp[:, po:po + 512],
                        mk["pbt"][:, (h * 3 + r) * 128:(h * 3 + r + 1) * 128],
                        rhs, start=(j == 0), stop=(j == 2)),
                        r=rks + ("R2",), w=(ppk,), fuse=(j == 1))
            for half in range(2):
                pp, ppk, po = ph[half]
                dst = ATTOK.rearrange("p (tc b m) -> p tc b m", b=8, m=D)[
                    :, tc, half * 4:half * 4 + 4, h * 128:(h + 1) * 128]
                s.op("scalar", lambda e, dst=dst, pp=pp, po=po: e.activation(
                    dst, pp[:, po:po + 512], AF.Identity), r=(ppk,), w=("R1",))

    s.dma("gpsimd", lambda e: e.dma_start(
        out=mk["WOT"].rearrange("p (c m) -> p c m", m=D),
        in_=mk["W_outT_d"].rearrange("(c p) m -> p c m", p=128)),
        r=(), w=("R2b", "R2"))
    XT = mk["XT"]
    ATTT = mk["attt"]
    idb = mk["identb"]
    for i in range(NT):
        b, tc = i // 2, i % 2
        src = ATTOK.rearrange("p (tc b m) -> p tc b m", b=8, m=D)[:, tc, b, :]
        pst = mk["ptb0"] if i % 2 == 0 else mk["ptb1"]
        pbk = "ptb0" if i % 2 == 0 else "ptb1"
        for c in range(8):
            s.op("tensor", lambda e, c=c, src=src, pst=pst: e.transpose(
                pst[:, c * 128:(c + 1) * 128], src[:, c * 128:(c + 1) * 128], idb[:]),
                r=("R1", "IDB"), w=(pbk,), fuse=(c != 0))
        s.op("vector", lambda e, pst=pst: e.tensor_copy(out=ATTT[:], in_=pst[:, 0:D]),
             r=(pbk,), w=("R2c", "R2"))
        pd, pdk = (mk["pt2"], "pt2") if i % 2 == 0 else (mk["pt3"], "pt3")
        for half in range(2):
            for c in range(8):
                s.op("tensor", lambda e, c=c, half=half, pd=pd: e.matmul(
                    pd[:, 0:512], ATTT[:, c * 128:(c + 1) * 128],
                    mk["WOT"][:, c * D + half * 512: c * D + (half + 1) * 512],
                    start=(c == 0), stop=(c == 7)),
                    r=("R2c", "R2b"), w=(pdk,), fuse=(c != 0))
            dstx = XT[:, i * D + half * 512: i * D + (half + 1) * 512]
            s.op("vector", lambda e, dstx=dstx, pd=pd: e.tensor_tensor(
                out=dstx, in0=pd[:, 0:512], in1=dstx, op=ALU.add),
                r=(pdk, f"x{i}"), w=(f"x{i}",))



INV32767 = 1.0 / 32767.0
TWO16 = 65536.0
INV216 = 1.0 / 65536.0


def sublayer_l1ffn(mk):
    s = mk.s
    SC = mk["scal"]
    layernorm_stats(mk)
    global_absmax_of_ln(mk)
    s.op("vector", lambda e: e.tensor_copy(out=mk["ccstage"][0:1, 0:1],
                                           in_=mk["st_g1"][0:1, 0:1]),
         r=("G1",), w=("CCS",))
    allreduce_max(mk, 1)
    s.op("vector", lambda e: e.tensor_scalar(SC[0:1, 1:2], mk["st_gmax"][0:1, 0:1],
                                             INV32767, 1e-12, ALU.mult, ALU.max),
         r=("GMAX",), w=("SC",))
    s.op("vector", lambda e: e.reciprocal(SC[0:1, 2:3], SC[0:1, 1:2]),
         r=("SC",), w=("SC",))
    broadcast_scalars(mk, SC[0:1, 1:3], ("SC",), "sxb", "SXB", 2)
    quantize_x_to_xqt(mk, mode="real16")

    XQTv = mk["XQT"].rearrange("p (c t) -> p c t", t=R)
    WG, WU = mk["wg_t"], mk["wu_t"]
    GATE = mk["GATE16"]
    upst = mk["upst"]
    UM, GM = mk["hmax"], mk["hmax2"]
    for ic in range(IC1):
        sl = ic % 2
        wgk, wuk = f"WGT{sl}", "WUT0"
        s.dma("sync", lambda e, ic=ic, sl=sl: e.dma_start(
            out=ap_of(WG)[:, sl * D:(sl + 1) * D].rearrange("p (c i) -> p c i", i=128),
            in_=mk["b_wg1"][ic].rearrange("c p i -> p c i")),
            r=("hb_b_wg1",), w=(wgk,))
        s.dma("sync", lambda e, ic=ic: e.dma_start(
            out=ap_of(WU)[:, 0:D].rearrange("p (c i) -> p c i", i=128),
            in_=mk["b_wu1"][ic].rearrange("c p i -> p c i")),
            r=("hb_b_wu1",), w=(wuk,))
        for tt in range(4):
            pg, pgk = (mk["pt0"], "pt0") if tt % 2 == 0 else (mk["pt1"], "pt1")
            pu, puk = (mk["pt2"], "pt2") if tt % 2 == 0 else (mk["pt3"], "pt3")
            for hc in range(8):
                s.op("tensor", lambda e, hc=hc, tt=tt, pu=pu: e.matmul(
                    pu[:, 0:512], WU[:, hc * 128:(hc + 1) * 128],
                    XQTv[:, hc, tt * 512:(tt + 1) * 512],
                    start=(hc == 0), stop=(hc == 7)),
                    r=(wuk, "AC", "ACt", "ACt2"), w=(puk,), fuse=(hc != 0))
            for hc in range(8):
                s.op("tensor", lambda e, hc=hc, tt=tt, sl=sl, pg=pg: e.matmul(
                    pg[:, 0:512], WG[:, sl * D + hc * 128:sl * D + (hc + 1) * 128],
                    XQTv[:, hc, tt * 512:(tt + 1) * 512],
                    start=(hc == 0), stop=(hc == 7)),
                    r=(wgk, "AC", "ACt", "ACt2"), w=(pgk,), fuse=(hc != 0))
            s.op("scalar", lambda e, pu=pu: e.activation(upst[:], pu[:, 0:512],
                                                         AF.Identity),
                 r=(puk,), w=("UPST",))
            s.dma("scalar", lambda e, ic=ic, tt=tt: e.dma_start(
                out=mk["up_hbm"][ic, :, tt * 512:(tt + 1) * 512], in_=upst[:]),
                r=("UPST",), w=(f"hup{ic}_{tt}",))
            s.op("vector", lambda e, ic=ic, tt=tt: e.reduce_max(
                UM[:, ic * 4 + tt:ic * 4 + tt + 1], upst[:], AX.X,
                apply_absolute_value=True), r=("UPST",), w=("HM",))
            gsl = GATE[:, ic * R + tt * 512: ic * R + (tt + 1) * 512]
            s.op("scalar", lambda e, gsl=gsl, pg=pg: e.activation(
                gsl, pg[:, 0:512], AF.Identity), r=(pgk,), w=ARN)
        gic = GATE[:, ic * R:(ic + 1) * R]
        s.op("vector", lambda e, gic=gic, ic=ic: e.reduce_max(
            GM[:, ic:ic + 1], gic, AX.X,
            apply_absolute_value=True), r=ARN, w=("HM2",))

    cross_partition_max(mk, UM, "HM", IC1 * 4, "G1")
    s.op("vector", lambda e: e.tensor_copy(out=mk["ccstage"][0:1, 0:1],
                                           in_=mk["st_g1"][0:1, 0:1]),
         r=("G1",), w=("CCS",))
    cross_partition_max(mk, GM, "HM2", IC1, "G1")
    s.op("vector", lambda e: e.tensor_copy(out=mk["ccstage"][0:1, 1:2],
                                           in_=mk["st_g1"][0:1, 0:1]),
         r=("G1",), w=("CCS",))
    allreduce_max(mk, 2)
    s.op("vector", lambda e: e.tensor_scalar(SC[0:1, 4:5], mk["st_gmax"][0:1, 0:1],
                                             INV32767, 1e-12, ALU.mult, ALU.max),
         r=("GMAX",), w=("SC",))
    s.op("vector", lambda e: e.reciprocal(SC[0:1, 5:6], SC[0:1, 4:5]),
         r=("SC",), w=("SC",))
    s.op("vector", lambda e: e.tensor_scalar(SC[0:1, 6:7], mk["st_gmax"][0:1, 1:2],
                                             INV32767, 1e-12, ALU.mult, ALU.max),
         r=("GMAX",), w=("SC",))
    s.op("vector", lambda e: e.reciprocal(SC[0:1, 7:8], SC[0:1, 6:7]),
         r=("SC",), w=("SC",))
    s.op("vector", lambda e: e.tensor_scalar(SC[0:1, 8:9], SC[0:1, 6:7],
                                             -MAGIC, None, ALU.mult),
         r=("SC",), w=("SC",))
    s.op("vector", lambda e: e.tensor_copy(out=SC[0:1, 9:10], in_=SC[0:1, 7:8]),
         r=("SC",), w=("SC",))
    s.op("vector", lambda e: e.tensor_copy(out=SC[0:1, 10:11], in_=SC[0:1, 6:7]),
         r=("SC",), w=("SC",))
    s.op("vector", lambda e: e.tensor_copy(out=SC[0:1, 11:12], in_=SC[0:1, 8:9]),
         r=("SC",), w=("SC",))
    s.op("vector", lambda e: e.tensor_copy(out=SC[0:1, 12:13], in_=SC[0:1, 5:6]),
         r=("SC",), w=("SC",))
    broadcast_scalars(mk, SC[0:1, 9:13], ("SC",), "bsc", "BSC", 4)

    silg, upr, sc512 = mk["silg"], mk["upr"], mk["sc512"]
    PM = mk["hmax"]
    for ic in range(IC1):
        for tt in range(4):
            par = (ic * 4 + tt) % 2
            if par == 0:
                S, U, C = silg, upr, sc512
                Sx, Ux, Cx = ("SILG",), ("UPR",), ("SC512", "ACt")
            else:
                S, U, C = mk["silgB"], mk["uprB"], mk["sc512B"]
                Sx, Ux, Cx = ("SILGB", "ACt2"), ("UPRB", "ACt2"), ("SC512B", "ACt2")
            gsl = GATE[:, ic * R + tt * 512: ic * R + (tt + 1) * 512]
            s.op("vector", lambda e, gsl=gsl, S=S: e.tensor_scalar(
                S[:, 0:512], gsl, mk["bsc"][:, 0:1], MAGIC, ALU.mult, ALU.add),
                r=ARN + ("BSC",), w=Sx)
            s.op("scalar", lambda e, S=S, U=U: e.activation(
                U[:, 0:512], S[:, 0:512], AF.Sigmoid, scale=mk["bsc"][:, 1:2],
                bias=mk["bsc"][:, 2:3]), r=Sx + ("BSC",), w=Ux)
            s.op("vector", lambda e, U=U, C=C: e.tensor_scalar(
                C[:, 0:512], U[:, 0:512], TWO16, MAGIC, ALU.mult, ALU.add),
                r=Ux, w=Cx)
            s.op("vector", lambda e, U=U, C=C: e.tensor_scalar(
                U[:, 0:512], C[:, 0:512], MAGIC, 65535.0, ALU.subtract, ALU.min),
                r=Cx, w=Ux)
            s.op("vector", lambda e, S=S, C=C: e.tensor_scalar(
                C[:, 0:512], S[:, 0:512], MAGIC, None, ALU.subtract),
                r=Sx, w=Cx)
            s.op("vector", lambda e, gsl=gsl, C=C, U=U: e.scalar_tensor_tensor(
                gsl, C[:, 0:512], INV216, U[:, 0:512], ALU.mult, ALU.mult),
                r=Cx + Ux, w=ARN)
        gic = GATE[:, ic * R:(ic + 1) * R]
        s.op("vector", lambda e, gic=gic, ic=ic: e.reduce_max(
            PM[:, ic:ic + 1], gic, AX.X,
            apply_absolute_value=True), r=ARN, w=("HM",))
    cross_partition_max(mk, PM, "HM", IC1, "G1")
    s.op("vector", lambda e: e.tensor_copy(out=mk["ccstage"][0:1, 0:1],
                                           in_=mk["st_g1"][0:1, 0:1]),
         r=("G1",), w=("CCS",))
    allreduce_max(mk, 1)
    s.op("vector", lambda e: e.tensor_tensor(out=SC[0:1, 13:14],
                                             in0=mk["st_gmax"][0:1, 0:1],
                                             in1=SC[0:1, 6:7], op=ALU.mult),
         r=("GMAX", "SC"), w=("SC",))
    s.op("vector", lambda e: e.tensor_scalar(SC[0:1, 14:15], SC[0:1, 13:14],
                                             INV32767, 1e-12, ALU.mult, ALU.max),
         r=("SC",), w=("SC",))
    s.op("vector", lambda e: e.reciprocal(SC[0:1, 15:16], SC[0:1, 14:15]),
         r=("SC",), w=("SC",))
    s.op("vector", lambda e: e.tensor_tensor(out=SC[0:1, 9:10], in0=SC[0:1, 6:7],
                                             in1=SC[0:1, 15:16], op=ALU.mult),
         r=("SC",), w=("SC",))
    s.op("vector", lambda e: e.tensor_copy(out=SC[0:1, 10:11], in_=SC[0:1, 5:6]),
         r=("SC",), w=("SC",))
    broadcast_scalars(mk, SC[0:1, 9:11], ("SC",), "bsc2", "BSC2", 2)

    QM = mk["hmax2"]
    upstB = ap_of(mk.t["wg_t"])[:, 0:1024].bitcast(F16)[:, 0:512]
    for ic in range(IC1):
        for tt in range(4):
            par = (ic * 4 + tt) % 2
            if par == 0:
                S, U, C = silg, upr, sc512
                Sx, Ux, Cx = ("SILG",), ("UPR",), ("SC512", "ACt")
                UP, UPk = mk["upst"], "UPST"
            else:
                S, U, C = mk["silgB"], mk["uprB"], mk["sc512B"]
                Sx, Ux, Cx = ("SILGB", "ACt2"), ("UPRB", "ACt2"), ("SC512B", "ACt2")
                UP, UPk = upstB, "WGT0"
            gsl = GATE[:, ic * R + tt * 512: ic * R + (tt + 1) * 512]
            s.op("vector", lambda e, gsl=gsl, S=S: e.tensor_scalar(
                S[:, 0:512], gsl, mk["bsc2"][:, 0:1], MAGIC, ALU.mult, ALU.add),
                r=ARN + ("BSC2",), w=Sx)
            s.op("vector", lambda e, S=S, C=C: e.tensor_scalar(
                C[:, 0:512], S[:, 0:512], MAGIC, None, ALU.subtract),
                r=Sx, w=Cx)
            s.dma("scalar", lambda e, ic=ic, tt=tt, UP=UP: e.dma_start(
                out=UP[:, 0:512] if not hasattr(UP, "ap") else UP[:, 0:512],
                in_=mk["up_hbm"][ic, :, tt * 512:(tt + 1) * 512]),
                r=(f"hup{ic}_{tt}",), w=(UPk,))
            s.op("vector", lambda e, U=U, UP=UP: e.tensor_scalar(
                U[:, 0:512], UP[:, 0:512], mk["bsc2"][:, 1:2], MAGIC, ALU.mult, ALU.add),
                r=(UPk, "BSC2"), w=Ux)
            s.op("vector", lambda e, S=S, U=U: e.tensor_scalar(
                S[:, 0:512], U[:, 0:512], MAGIC, None, ALU.subtract),
                r=Ux, w=Sx)
            s.op("vector", lambda e, gsl=gsl, C=C, S=S: e.scalar_tensor_tensor(
                gsl, C[:, 0:512], INV216, S[:, 0:512], ALU.mult, ALU.mult),
                r=Cx + Sx, w=ARN)
        gic = GATE[:, ic * R:(ic + 1) * R]
        s.op("vector", lambda e, gic=gic, ic=ic: e.reduce_max(
            QM[:, ic:ic + 1], gic, AX.X,
            apply_absolute_value=True), r=ARN, w=("HM2",))
    cross_partition_max(mk, QM, "HM2", IC1, "G1")
    s.op("vector", lambda e: e.tensor_copy(out=mk["ccstage"][0:1, 0:1],
                                           in_=mk["st_g1"][0:1, 0:1]),
         r=("G1",), w=("CCS",))
    allreduce_max(mk, 1)
    s.op("vector", lambda e: e.tensor_tensor(out=SC[0:1, 13:14], in0=SC[0:1, 14:15],
                                             in1=SC[0:1, 4:5], op=ALU.mult),
         r=("SC",), w=("SC",))
    s.op("vector", lambda e: e.tensor_scalar(SC[0:1, 13:14], SC[0:1, 13:14],
                                             TWO16, None, ALU.mult),
         r=("SC",), w=("SC",))
    s.op("vector", lambda e: e.tensor_tensor(out=SC[0:1, 14:15],
                                             in0=mk["st_gmax"][0:1, 0:1],
                                             in1=SC[0:1, 13:14], op=ALU.mult),
         r=("GMAX", "SC"), w=("SC",))
    s.op("vector", lambda e: e.tensor_scalar(SC[0:1, 14:15], SC[0:1, 14:15],
                                             INV32767, 1e-12, ALU.mult, ALU.max),
         r=("SC",), w=("SC",))
    s.op("vector", lambda e: e.reciprocal(SC[0:1, 15:16], SC[0:1, 14:15]),
         r=("SC",), w=("SC",))
    s.op("vector", lambda e: e.tensor_tensor(out=SC[0:1, 9:10], in0=SC[0:1, 13:14],
                                             in1=SC[0:1, 15:16], op=ALU.mult),
         r=("SC",), w=("SC",))
    s.op("vector", lambda e: e.tensor_copy(out=SC[0:1, 10:11], in_=SC[0:1, 14:15]),
         r=("SC",), w=("SC",))
    broadcast_scalars(mk, SC[0:1, 9:11], ("SC",), "bsc2", "BSC2", 2)

    OQ = mk["H24"]
    for ic in range(IC1):
        for tt in range(4):
            par = (ic * 4 + tt) % 2
            S = silg if par == 0 else mk["silgB"]
            Sx = ("SILG",) if par == 0 else ("SILGB", "ACt2")
            gsl = GATE[:, ic * R + tt * 512: ic * R + (tt + 1) * 512]
            osl = OQ[:, ic * R + tt * 512: ic * R + (tt + 1) * 512]
            s.op("vector", lambda e, gsl=gsl, S=S: e.tensor_scalar(
                S[:, 0:512], gsl, mk["bsc2"][:, 0:1], MAGIC, ALU.mult, ALU.add),
                r=ARN + ("BSC2",), w=Sx)
            s.op("vector", lambda e, osl=osl, S=S: e.tensor_scalar(
                osl, S[:, 0:512], MAGIC, mk["bsc2"][:, 1:2], ALU.subtract, ALU.mult),
                r=Sx + ("BSC2",), w=ARN)

    XT = mk["XT"]
    WD = mk["wd_t1"]
    for half in range(2):
        for ic in range(IC1):
            srcw = mk["b_wd1"][half * 4:(half + 1) * 4, ic, :, :].rearrange("c i h -> i c h")
            s.dma("sync", lambda e, ic=ic, srcw=srcw: e.dma_start(
                out=WD[:, ic * 512:(ic + 1) * 512].rearrange("i (c h) -> i c h", h=128),
                in_=srcw), r=("hb_b_wd1",), w=("AC",))
        for i in range(NT):
            pd, pdk = (mk["pt0"], "pt0") if i % 2 == 0 else (mk["pt1"], "pt1")
            for ic in range(IC1):
                s.op("tensor", lambda e, i=i, ic=ic, pd=pd: e.matmul(
                    pd[:, 0:512], OQ[:, ic * R + i * 128: ic * R + (i + 1) * 128],
                    WD[:, ic * 512:(ic + 1) * 512],
                    start=(ic == 0), stop=(ic == IC1 - 1)),
                    r=ARN + ("AC",), w=(pdk,), fuse=(ic != 0))
            dstx = XT[:, i * D + half * 512: i * D + (half + 1) * 512]
            s.op("vector", lambda e, dstx=dstx, pd=pd: e.scalar_tensor_tensor(
                dstx, pd[:, 0:512], 0.5, dstx, ALU.mult, ALU.add),
                r=(pdk, f"x{i}"), w=(f"x{i}",))



def sublayer_l1attn(mk):
    s = mk.s
    layernorm_stats(mk)
    ln_to_actt(mk, reorder=True)
    LNXT = mk["XQT"].rearrange("p (c t) -> p c t", t=R)

    def dump(tag, ap):
        if tag in DEBUG["dumps"]:
            o = mk.dram("dbg_" + tag, [128, 8 * R], BF16, kind="ExternalOutput")
            mk.s.dma("sync", lambda e: e.dma_start(out=o, in_=ap),
                     r=ARN + ACK + ("R1b", "R2b", "R2c"), w=())
    dump("lnxt", mk["XQT"])

    VT, QF, KF = mk["l1_VT"], mk["l1_QF"], mk["l1_KF"]
    ATD = mk["l1_ATD"]
    VW = mk["l1_VW"]
    psa = mk["pt_a"]
    WQK = mk["wg_t"]

    s.dma("gpsimd", lambda e: e.dma_start(out=mk["l1_brow"], in_=mk["bqkv_row_d"]),
          r=(), w=("R2",))
    s.dma("sync", lambda e: e.dma_start(out=mk["bqkv"][:], in_=mk["bqkv_d"]),
          r=(), w=("BQKV",))
    s.dma("sync", lambda e: e.dma_start(out=mk["msk"][:], in_=mk["msk_d"]),
          r=(), w=("MSK",))
    s.dma("sync", lambda e: e.dma_start(out=mk["onesb"][:], in_=mk["onesb_d"]),
          r=(), w=("ONESB",))
    s.op("vector", lambda e: e.tensor_scalar(mk["p_am"][:, 0:8], mk["bqkv"][:, 0:8],
                                             0.125, None, ALU.mult),
         r=("BQKV",), w=("HM2",))

    for hc in range(8):
        srcw = mk["inprojT_d"].rearrange("(c p) m -> p c m", p=128)[:, hc, 2048:3072]
        s.dma("gpsimd", lambda e, hc=hc, srcw=srcw: e.dma_start(
            out=VW[:, hc * D:(hc + 1) * D], in_=srcw), r=(), w=("R1",))
    for g in range(NT):
        if g % 2 == 0:
            ph = ((mk["pt0"], "pt0"), (mk["pt1"], "pt1"))
        else:
            ph = ((mk["pt2"], "pt2"), (mk["pt3"], "pt3"))
        for half in range(2):
            pp, ppk = ph[half]
            for hc in range(8):
                s.op("tensor", lambda e, g=g, half=half, hc=hc, pp=pp: e.matmul(
                    pp[:, 0:512],
                    LNXT[:, hc, g * 128:(g + 1) * 128],
                    VW[:, hc * D + half * 512: hc * D + (half + 1) * 512],
                    start=(hc == 0), stop=False),
                    r=("AC", "ACt", "ACt2", "R1"), w=(ppk,), fuse=(hc != 0))
            s.op("tensor", lambda e, half=half, pp=pp: e.matmul(
                pp[:, 0:512], mk["onesb"][0:1, :],
                mk["l1_brow"][0:1, 2048 + half * 512:2048 + (half + 1) * 512],
                start=False, stop=True),
                r=("ONESB", "R2"), w=(ppk,), fuse=True)
            dst = VT[:, g * D + half * 512:g * D + (half + 1) * 512]
            s.op("scalar", lambda e, dst=dst, pp=pp: e.activation(
                dst, pp[:, 0:512], AF.Identity), r=(ppk,), w=("R0",))

    dump("vt", mk["l1_VT"])
    for part, dstt, scl, dk in (("q", QF, 0.125, "R1"), ("k", KF, 1.0, "R2")):
        off = 0 if part == "q" else 1024
        for fc in range(8):
            sl = fc % 2
            wk = f"WGT{sl}"
            srcw = mk["inprojT_d"].rearrange("(c p) m -> p c m", p=128)[
                :, :, off + fc * 128: off + (fc + 1) * 128]
            s.dma("gpsimd", lambda e, srcw=srcw, sl=sl: e.dma_start(
                out=ap_of(WQK)[:, sl * D:(sl + 1) * D].rearrange(
                    "p (c i) -> p c i", i=128), in_=srcw),
                r=(), w=(wk,))
            for gg in range(4):
                pp, ppk = (mk["pt0"], "pt0") if gg % 2 == 0 else (mk["pt1"], "pt1")
                for hc in range(8):
                    rhs = LNXT[:, hc, gg * 512:(gg + 1) * 512]
                    s.op("tensor", lambda e, rhs=rhs, hc=hc, sl=sl, pp=pp: e.matmul(
                        pp[:, 0:512], WQK[:, sl * D + hc * 128:sl * D + (hc + 1) * 128],
                        rhs, start=(hc == 0), stop=(hc == 7)),
                        r=(wk, "AC", "ACt", "ACt2"), w=(ppk,), fuse=(hc != 0))
                if part == "q":
                    bias = mk["p_am"][:, fc:fc + 1]
                    bk = "HM2"
                else:
                    bias = mk["bqkv"][:, 8 + fc:8 + fc + 1]
                    bk = "BQKV"
                s.op("scalar", lambda e, dstt=dstt, fc=fc, gg=gg, scl=scl, bias=bias,
                     pp=pp: e.activation(
                         dstt[:, fc * R + gg * 512: fc * R + (gg + 1) * 512],
                         pp[:, 0:512], AF.Identity, bias=bias, scale=scl),
                     r=(ppk, bk), w=(dk,))

    dump("qf", mk["l1_QF"])
    dump("kf", mk["l1_KF"])
    attb = mk["wg_t"]
    nmx, ssum, rs = mk["st_red1"], mk["st_amax"], mk["st_rs"]
    silg, upr = mk["silg"], mk["upr"]
    idb = mk["identb"]
    for g in range(NT):
        for h in range(16):
            hh, fc = h % 2, h // 2
            psc, psck = (mk["pt2"], "pt2") if h % 2 == 0 else (mk["pt3"], "pt3")
            pst, pbk = (mk["ptb0"], "ptb0") if h % 2 == 0 else (mk["ptb1"], "ptb1")
            qv = ap_of(QF)[hh * 64:(hh + 1) * 64, fc * R + g * 128: fc * R + (g + 1) * 128]
            kv = ap_of(KF)[hh * 64:(hh + 1) * 64, fc * R + g * 128: fc * R + (g + 1) * 128]
            s.op("tensor", lambda e, qv=qv, kv=kv, psc=psc: e.matmul(
                psc[:, 0:128], qv, kv, start=True, stop=True),
                r=("R1", "R2"), w=(psck,))
            s.op("vector", lambda e, psc=psc: e.tensor_tensor(
                out=silg[:, 0:128], in0=psc[:, 0:128], in1=mk["msk"][:], op=ALU.add),
                r=(psck, "MSK"), w=("SILG",))
            s.op("vector", lambda e: e.reduce_max(nmx[:, 0:1], silg[:, 0:128], AX.X,
                                                  negate=True),
                 r=("SILG",), w=("RED1",))
            s.op("scalar", lambda e: e.activation(upr[:, 0:128], silg[:, 0:128],
                                                  AF.Exp, bias=nmx[:, 0:1]),
                 r=("SILG", "RED1"), w=("UPR",))
            s.op("vector", lambda e: e.reduce_sum(ssum[:, 0:1], upr[:, 0:128], AX.X),
                 r=("UPR",), w=("AMAX",))
            s.op("vector", lambda e: e.reciprocal(rs[:, 0:1], ssum[:, 0:1]),
                 r=("AMAX",), w=("RS",))
            s.op("vector", lambda e: e.tensor_scalar(attb[:, 0:128], upr[:, 0:128],
                                                     rs[:, 0:1], None, ALU.mult),
                 r=("UPR", "RS"), w=("WGT0",))
            s.op("tensor", lambda e, pst=pst: e.transpose(pst[:, 0:128], attb[:, 0:128],
                                                          idb[:]),
                 r=("WGT0", "IDB"), w=(pbk,))
            s.op("vector", lambda e, pst=pst: e.tensor_copy(out=attb[:, 128:256],
                                                            in_=pst[:, 0:128]),
                 r=(pbk,), w=("WGT1",))
            s.op("tensor", lambda e, g=g, h=h, hh=hh, fc=fc: e.matmul(
                psa[hh * 64:(hh + 1) * 64,
                    (h // 8) * 512 + (fc % 4) * 128:(h // 8) * 512 + (fc % 4 + 1) * 128],
                VT[:, g * D + h * 64: g * D + (h + 1) * 64],
                attb[:, 128:256], start=True, stop=True),
                r=("R0", "WGT1"), w=("pta",))
            if h in (7, 15):
                lo = 0 if h == 7 else 4
                po = (h // 8) * 512
                srcp = ap_of(psa)[:, po:po + 512].rearrange(
                    "p (sl j b) -> p sl j b", j=16, b=8)
                dst = ATD.rearrange("p (c b G j) -> p c b G j",
                                    c=8, b=8, j=16)[:, lo:lo + 4, :, g, :].rearrange(
                    "p c b j -> p c j b")
                s.op("vector", lambda e, srcp=srcp, dst=dst: e.tensor_copy(
                    out=dst, in_=srcp), r=("pta",), w=("AC", "ACt", "ACt2"))

    dump("atd", mk["l1_ATD"])
    s.dma("gpsimd", lambda e: e.dma_start(
        out=mk["l1_WOT"].rearrange("p (c m) -> p c m", m=D),
        in_=mk["outprojT1_d"].rearrange("(c p) m -> p c m", p=128)),
        r=("R1",), w=("R1",))
    s.dma("gpsimd", lambda e: e.dma_start(out=mk["l1_bout"], in_=mk["bout_row_d"]),
          r=(), w=("R1",))
    XT = mk["XT"]
    for i in range(NT):
        pd, pdk = (mk["pt0"], "pt0") if i % 2 == 0 else (mk["pt1"], "pt1")
        for half in range(2):
            for c in range(8):
                s.op("tensor", lambda e, c=c, half=half, i=i, pd=pd: e.matmul(
                    pd[:, 0:512], ATD[:, c * R + i * 128:c * R + (i + 1) * 128],
                    mk["l1_WOT"][:, c * D + half * 512: c * D + (half + 1) * 512],
                    start=(c == 0), stop=False),
                    r=("AC", "ACt", "ACt2", "R1"), w=(pdk,), fuse=(c != 0))
            s.op("tensor", lambda e, half=half, pd=pd: e.matmul(
                pd[:, 0:512], mk["onesb"][0:1, :],
                mk["l1_bout"][0:1, half * 512:(half + 1) * 512],
                start=False, stop=True),
                r=("ONESB", "R1"), w=(pdk,), fuse=True)
            dstx = XT[:, i * D + half * 512: i * D + (half + 1) * 512]
            s.op("vector", lambda e, dstx=dstx, pd=pd: e.tensor_tensor(
                out=dstx, in0=pd[:, 0:512], in1=dstx, op=ALU.add),
                r=(pdk, f"x{i}"), w=(f"x{i}",))



def alloc_common(mk, need):
    mk.dram("x_in", [R, D], F32, kind="ExternalInput")
    mk.dram("x_out", [R, D], F32, kind="ExternalOutput")
    mk.dram("identf_d", [128, 128], F32, kind="ExternalInput")
    mk.dram("identb_d", [128, 128], BF16, kind="ExternalInput")
    mk.dram("onesf_d", [1, 128], F32, kind="ExternalInput")
    if need["l0ffn"]:
        mk.dram("L0w_gate", [24 * 128, D], F32, kind="ExternalInput")
        mk.dram("L0w_up", [24 * 128, D], F32, kind="ExternalInput")
        mk.dram("L0w_down", [D, I0], F32, kind="ExternalInput")
        mk.dram("clipx", [1, 1], F32, kind="ExternalInput")
        mk.dram("clipd", [1, 1], F32, kind="ExternalInput")
        mk.dram("b_wg0", [24, 8, 128, 128], BF16, addr_space="Shared")
        mk.dram("b_wu0", [24, 8, 128, 128], BF16, addr_space="Shared")
        mk.dram("b_wd0", [8, 24, 128, 128], BF16, addr_space="Shared")
        mk.dram("b_swg0", [24, 128], F32, addr_space="Shared")
        mk.dram("b_swu0", [24, 128], F32, addr_space="Shared")
    if need["l1ffn"]:
        mk.dram("L1w_gate", [24 * 128, D], F32, kind="ExternalInput")
        mk.dram("L1w_up", [24 * 128, D], F32, kind="ExternalInput")
        mk.dram("L1w_down", [D, I1], F32, kind="ExternalInput")
        mk.dram("b_wg1", [24, 8, 128, 128], BF16, addr_space="Shared")
        mk.dram("b_wu1", [24, 8, 128, 128], BF16, addr_space="Shared")
        mk.dram("b_wd1", [8, 24, 128, 128], BF16, addr_space="Shared")
        mk.dram("up_hbm", [IC1, 128, R], F16)
    if need["l1attn"]:
        mk.dram("inprojT_d", [D, 3 * D], F32, kind="ExternalInput")
        mk.dram("bqkv_d", [128, 24], F32, kind="ExternalInput")
        mk.dram("bqkv_row_d", [1, 3 * D], F32, kind="ExternalInput")
        mk.dram("outprojT1_d", [D, D], F32, kind="ExternalInput")
        mk.dram("bout_row_d", [1, D], F32, kind="ExternalInput")
        mk.dram("msk_d", [128, 128], BF16, kind="ExternalInput")
        mk.dram("onesb_d", [1, 128], BF16, kind="ExternalInput")
    if need["l0attn"]:
        mk.dram("W_inT_d", [D, D], F32, kind="ExternalInput")
        mk.dram("W_outT_d", [D, D], F32, kind="ExternalInput")
        mk.dram("dm_d", [128, 1536], F32, kind="ExternalInput")
        mk.dram("v_eg_shard", [2, 8, 128, D], BF16)
        mk.dram("v_eg_g", [NCORES + 2, 2, 8, 128, D], BF16, addr_space="Shared")
    mk.dram("cc_in", [1, 16], F32)
    mk.dram("cc_out", [1, 16], F32, addr_space="Shared")

    mk.sb("XT", [128, NT * D], F32)
    arena = mk.sb("ARENA", [128, 24 * R], BF16)
    A = arena.ap()[:, :]
    mk.view("H", A[:, 0:IC0 * R])
    mk.view("Hf16", A.bitcast(F16)[:, 0:IC0 * R])
    Af32 = A.bitcast(F32)
    mk.view("p_wnat", Af32[:, 0:1024])
    mk.view("p_wnat3", Af32[:, 1024:4096])
    mk.view("p_scr", Af32[:, 4096:5120])
    mk.view("p_scr2", Af32[:, 5120:6144])
    mk.view("p_scr3", Af32[:, 6144:9216])
    mk.view("p_scr3b", Af32[:, 9216:12288])
    mk.view("scr1", Af32[:, 12288:14336])
    mk.view("scr2", Af32[:, 14336:16384])
    mk.view("p_stage", A[:, 32768:33792])
    mk.view("p_stage2", A[:, 33792:34816])
    mk.view("xqk", A[:, 34816:35840])
    mk.view("sc_tmp", Af32[:, 17920:18048])
    mk.view("p_amw", Af32[:, 18048:18144])
    mk.view("p_amw2", Af32[:, 18144:18240])
    actt = mk.sb("ACTT", [128, 8 * R], BF16)
    AT = actt.ap()[:, :]
    mk.view("XQT", AT)
    mk.view("wd_t", AT[:, 0:IC0 * 512])
    mk.view("wd_t1", AT[:, 0:IC1 * 512])
    mk.view("l1_ATD", AT[:, 0:NT * D])
    mk.view("sc512", AT[:, 12288:13312].bitcast(F32))
    mk.view("silgB", AT[:, 13312:14336].bitcast(F32))
    mk.view("uprB", AT[:, 14336:15360].bitcast(F32))
    mk.view("sc512B", AT[:, 15360:16384].bitcast(F32))
    mk.view("VTOK", A[:, 0:NT * D])
    mk.view("ATTOK", A[:, NT * D:2 * NT * D])
    mk.view("WOT", A[:, 2 * NT * D:2 * NT * D + 8 * D])
    mk.view("WIT", A[:, 2 * NT * D + 8 * D:2 * NT * D + 16 * D])
    mk.view("HALO", AT[:, 0:NT * D])
    mk.view("pbt", A[:, 2 * NT * D + 8 * D:2 * NT * D + 8 * D + 3072])
    mk.view("attt", A[:, 2 * NT * D + 8 * D + 3072:2 * NT * D + 8 * D + 4096])
    mk.view("GATE16", A.bitcast(F16))
    mk.view("H24", A)
    mk.view("l1_VT", A[:, 0:NT * D])
    mk.view("l1_QF", A[:, NT * D:2 * NT * D])
    mk.view("l1_KF", A[:, 2 * NT * D:3 * NT * D])
    mk.view("l1_VW", A[:, NT * D:NT * D + 8 * D])
    mk.view("l1_WOT", A[:, NT * D:NT * D + 8 * D])
    mk.view("l1_brow", A[0:1, 2 * NT * D:2 * NT * D + 3 * D])
    mk.view("l1_bout", A[0:1, NT * D + 8 * D:NT * D + 9 * D])
    mk.sb("identf", [128, 128], F32)
    mk.sb("identb", [128, 128], BF16)
    mk.sb("onesf", [1, 128], F32)
    for n in ("st_sum", "st_ssq", "st_mean", "st_var", "st_rstd", "st_nmr",
              "st_amax", "st_rs", "st_nmrs"):
        mk.sb(n, [128, NT], F32)
    mk.sb("st_red1", [128, 1], F32)
    mk.sb("st_g1", [1, 1], F32)
    mk.sb("st_gmax", [1, 16], F32)
    mk.sb("ccstage", [1, 16], F32)
    mk.sb("scal", [1, 16], F32)
    mk.sb("sxb", [128, 2], F32)
    mk.sb("shb", [128, 3], F32)
    mk.sb("clipx_t", [1, 1], F32)
    mk.sb("clipd_t", [1, 1], F32)
    mk.sb("swg", [128, 24], F32)
    mk.sb("swu", [128, 24], F32)
    mk.sb("wg_t", [128, 2048], BF16)
    mk.sb("wu_t", [128, 1024], BF16)
    mk.sb("silg", [128, 512], F32)
    mk.sb("upr", [128, 512], F32)
    mk.sb("hmax", [128, IC1 * 4], F32)
    mk.sb("hmax2", [128, IC1 * 4], F32)
    mk.view("p_am", ap_of(mk.t["hmax2"])[:, 0:32])
    mk.view("p_am2", ap_of(mk.t["hmax"])[:, 0:32])
    mk.sb("upst", [128, 512], F16)
    mk.sb("bsc", [128, 4], F32)
    mk.sb("bsc2", [128, 2], F32)
    mk.sb("rcst", [128, 4], F32)
    mk.sb("bqkv", [128, 24], F32)
    mk.sb("msk", [128, 128], BF16)
    mk.sb("onesb", [1, 128], BF16)

    mk.ps("ptb0", [128, 1024], BF16)
    mk.ps("ptb1", [128, 1024], BF16)
    mk.ps("pt_a", [128, 1024], F32)
    mk.ps("pt0", [128, 512], F32)
    mk.ps("pt1", [128, 512], F32)
    mk.ps("pt2", [128, 512], F32)
    mk.ps("pt3", [128, 512], F32)


def build_group(sublayers):
    nc = bass.Bass("TRN2", target_bir_lowering=False)
    s = Seq(nc)
    mk = MK(nc, s)
    need = {
        "l0ffn": any(x in (0, 2) for x in sublayers),
        "l0attn": 1 in sublayers,
        "l1ffn": any(x in (3, 5) for x in sublayers),
        "l1attn": 4 in sublayers,
    }
    alloc_common(mk, need)

    s.dma("sync", lambda e: e.dma_start(out=mk["identf"][:], in_=mk["identf_d"]),
          r=(), w=("IDF",))
    s.dma("sync", lambda e: e.dma_start(out=mk["identb"][:], in_=mk["identb_d"]),
          r=(), w=("IDB",))
    s.dma("sync", lambda e: e.dma_start(out=mk["onesf"][:], in_=mk["onesf_d"]),
          r=(), w=("ONESF",))
    s.op("vector", lambda e: e.memset(mk["rcst"][:, 0:1], MAGIC), r=(), w=("RCST",))
    s.op("vector", lambda e: e.memset(mk["rcst"][:, 1:2], TWO16), r=(), w=("RCST",))
    s.op("vector", lambda e: e.memset(mk["rcst"][:, 2:3], -MAGIC), r=(), w=("RCST",))
    s.op("vector", lambda e: e.memset(mk["rcst"][:, 3:4], 1.0), r=(), w=("RCST",))
    if need["l0ffn"]:
        s.dma("sync", lambda e: e.dma_start(out=mk["clipx_t"][:], in_=mk["clipx"]),
              r=(), w=("CLX",))
        s.dma("sync", lambda e: e.dma_start(out=mk["clipd_t"][:], in_=mk["clipd"]),
              r=(), w=("CLD",))
    xr = mk["x_in"].rearrange("(i p) m -> p i m", p=128)
    s.dma("sync", lambda e: e.dma_start(
        out=ap_of(mk.t["XT"])[:, :].rearrange("p (i m) -> p i m", m=D), in_=xr),
        r=(), w=xkeys())

    prep_all(mk, need["l0ffn"], need["l1ffn"])

    for sl in sublayers:
        if sl in (0, 2):
            sublayer_l0ffn(mk)
        elif sl == 1:
            sublayer_l0attn(mk)
        elif sl in (3, 5):
            sublayer_l1ffn(mk)
        elif sl == 4:
            sublayer_l1attn(mk)
        else:
            raise NotImplementedError(f"sublayer {sl}")

    xo = mk["x_out"].rearrange("(i p) m -> p i m", p=128)
    s.dma("sync", lambda e: e.dma_start(
        out=xo, in_=ap_of(mk.t["XT"])[:, :].rearrange("p (i m) -> p i m", m=D)),
        r=xkeys(), w=("hb_xout",))

    import contextlib
    with contextlib.ExitStack() as ctx:
        sems = {}
        for eng in ("scalar", "vector", "tensor", "gpsimd"):
            sems[(eng, False)] = ctx.enter_context(nc.semaphore(f"s_{eng}_c"))
        for eng in ("sync", "scalar", "gpsimd"):
            for sl in range(Seq.DMAK):
                sems[(eng, True, sl)] = ctx.enter_context(
                    nc.semaphore(f"s_{eng}_d{sl}"))
        block = ctx.enter_context(nc.Block())
        s.emit(block, sems)
    return nc



def _padw(w):
    out = np.zeros((24 * 128, w.shape[1]), np.float32)
    out[:w.shape[0]] = w
    return np.ascontiguousarray(out)


def _msk_const():
    r = np.arange(128)
    m = np.where((r[:, None] // 8) == (r[None, :] // 8), 0.0, -1e9)
    return np.ascontiguousarray(m.astype(np.float32))


def _dm_const():
    kk = np.arange(128.0)[:, None]
    rp = np.arange(384.0)[None, :]
    r, p = rp // 128, rp % 128
    dmi = kk - p + 128.0 * (r - 1.0)
    out = np.empty((128, 1536), np.float32)
    for sidx, std in enumerate((1.0, 2.0, 4.0, 8.0)):
        out[:, sidx * 384:(sidx + 1) * 384] = dmi / std + 1.0
    return out


def kernel(L0_attn_values, L0_attn_input_weights, L0_attn_output_weight,
           L0_ffn_clip_x, L0_ffn_clip_down, L0_ffn_w_gate, L0_ffn_w_up,
           L0_ffn_w_down, L1_attn_in_proj_w, L1_attn_in_proj_b,
           L1_attn_out_proj_w, L1_attn_out_proj_b, L1_ffn_w_gate,
           L1_ffn_w_up, L1_ffn_w_down):
    x = np.asarray(L0_attn_values, np.float32)
    feed = {
        "identf_d": np.eye(128, dtype=np.float32),
        "identb_d": np.eye(128, dtype=np.float32).astype(ml_dtypes.bfloat16),
        "onesf_d": np.ones((1, 128), np.float32),
        "L0w_gate": _padw(np.asarray(L0_ffn_w_gate, np.float32)),
        "L0w_up": _padw(np.asarray(L0_ffn_w_up, np.float32)),
        "L0w_down": np.ascontiguousarray(np.asarray(L0_ffn_w_down, np.float32)),
        "clipx": np.asarray(L0_ffn_clip_x, np.float32).reshape(1, 1),
        "clipd": np.asarray(L0_ffn_clip_down, np.float32).reshape(1, 1),
        "L1w_gate": _padw(np.asarray(L1_ffn_w_gate, np.float32)),
        "L1w_up": _padw(np.asarray(L1_ffn_w_up, np.float32)),
        "L1w_down": np.ascontiguousarray(np.asarray(L1_ffn_w_down, np.float32)),
        "W_inT_d": np.ascontiguousarray(np.asarray(L0_attn_input_weights, np.float32).T),
        "W_outT_d": np.ascontiguousarray(np.asarray(L0_attn_output_weight, np.float32).T),
        "dm_d": _dm_const(),
        "inprojT_d": np.ascontiguousarray(np.asarray(L1_attn_in_proj_w, np.float32).T),
        "bqkv_d": np.ascontiguousarray(
            np.asarray(L1_attn_in_proj_b, np.float32).reshape(24, 128).T),
        "bqkv_row_d": np.asarray(L1_attn_in_proj_b, np.float32).reshape(1, 3 * D),
        "outprojT1_d": np.ascontiguousarray(np.asarray(L1_attn_out_proj_w, np.float32).T),
        "bout_row_d": np.asarray(L1_attn_out_proj_b, np.float32).reshape(1, D),
        "msk_d": _msk_const().astype(ml_dtypes.bfloat16),
        "onesb_d": np.ones((1, 128), np.float32).astype(ml_dtypes.bfloat16),
    }

    cur = x
    for group in CONFIG["groups"]:
        nc = build_group(group)
        declared = {a.memorylocations[0].name
                    for a in nc.m.functions[0].allocations
                    if isinstance(a, mybir.MemoryLocationSet)
                    and a.kind == "ExternalInput"}
        in_maps = []
        for c in range(NCORES):
            xc = np.ascontiguousarray(
                cur[:, c * TLOC:(c + 1) * TLOC, :].reshape(R, D))
            m = {"x_in": xc}
            for k, v in feed.items():
                if k in declared:
                    m[k] = v
            in_maps.append(m)
        res = run_bass_kernel_spmd(nc, in_maps, core_ids=list(range(NCORES)))
        nxt = np.empty_like(cur)
        for c in range(NCORES):
            nxt[:, c * TLOC:(c + 1) * TLOC, :] = \
                res.results[c]["x_out"].reshape(B, TLOC, D)
        cur = nxt
    return cur


# version 09 (final)
# speedup vs baseline: 1.1175x; 1.0570x over previous
import numpy as np
import ml_dtypes
import concourse.bass as bass
import concourse.mybir as mybir
from concourse.bass_utils import run_bass_kernel_spmd

F32 = mybir.dt.float32
BF16 = mybir.dt.bfloat16
F16 = mybir.dt.float16
AF = mybir.ActivationFunctionType
ALU = mybir.AluOpType
AX = mybir.AxisListType
ET = mybir.EngineType

NCORES = 8
B = 8
S = 2048
D = 1024
TLOC = S // NCORES
R = B * TLOC
NT = R // 128
I0 = 2816
I1 = 3072
IC0 = I0 // 128
IC1 = I1 // 128
MAGIC = float(1.5 * 2.0 ** 23)
RG = [list(range(NCORES))]
ENGS = {"sync": ET.SP, "scalar": ET.Activation, "vector": ET.DVE,
        "tensor": ET.PE, "gpsimd": ET.Pool}

CONFIG = {"groups": [[0, 1, 2, 3, 4, 5]]}
DEBUG = {"dumps": []}

import os as _os
ELIDE_SAME_ENGINE = bool(int(_os.environ.get('MKELIDE', '0')))
ARN = ("R0", "R1", "R1b", "R2", "R2b", "R2c")
ACK = ("AC",)


class Seq:

    def __init__(self, nc):
        self.nc = nc
        self.ops = []
        self.pid = {}

    def op(self, eng, fn, r=(), w=(), fuse=False):
        if fuse and self.ops and self.ops[-1][0] == eng and not self.ops[-1][2]:
            p = self.ops[-1]
            self.ops[-1] = (eng, p[1] + [fn], False,
                            tuple(set(p[3]) | set(r)), tuple(set(p[4]) | set(w)))
        else:
            self.ops.append((eng, [fn], False, tuple(r), tuple(w)))

    def dma(self, eng, fn, r=(), w=()):
        self.ops.append((eng, [fn], True, tuple(r), tuple(w)))

    def core_id(self, name):
        if name not in self.pid:
            self.pid[name] = self.nc.partition_id(engines=[ENGS[name]])
        return self.pid[name]

    DMAK = 8

    def emit(self, block, sems):
        skey = []
        pos = []
        counts = {}
        for eng, fn, is_dma, r, w in self.ops:
            k = (eng, is_dma)
            skey.append(k)
            pos.append(counts.get(k, 0))
            counts[k] = counts.get(k, 0) + 1
        writers = {}
        readers = {}
        deps = [set() for _ in self.ops]
        for i, (eng, fn, is_dma, r, w) in enumerate(self.ops):
            for k in r:
                for j in writers.get(k, ()):
                    deps[i].add(j)
                readers.setdefault(k, []).append(i)
            for k in w:
                for j in writers.get(k, ()):
                    deps[i].add(j)
                for j in readers.get(k, ()):
                    deps[i].add(j)
                writers.setdefault(k, []).append(i)
        import os
        serial = bool(int(os.environ.get("MKSERIAL", "0")))
        def wtarget(k, p):
            if not k[1]:
                return (k, p + 1)
            return ((k[0], True, p % self.DMAK), 16 * (p // self.DMAK + 1))

        redeps = []
        for i in range(len(self.ops)):
            m = {}
            if serial:
                deps[i] = set(range(i))
            for j in deps[i]:
                if j == i:
                    continue
                kj = skey[j]
                if ELIDE_SAME_ENGINE and kj == skey[i] and not kj[1]:
                    continue
                sk, v = wtarget(kj, pos[j])
                m[sk] = max(m.get(sk, -1), v)
            redeps.append(m)

        finals = {}
        for k, n in counts.items():
            if not k[1]:
                finals[k] = n
            else:
                for sl in range(min(n, self.DMAK)):
                    uses = (n - sl + self.DMAK - 1) // self.DMAK
                    finals[(k[0], True, sl)] = 16 * uses

        def section(engobj, name):
            waited = {}

            def do_wait(sk, v):
                if v > waited.get(sk, 0):
                    engobj.wait_ge(sems[sk], v)
                    waited[sk] = v

            for i, (eng, fn, is_dma, r, w) in enumerate(self.ops):
                if eng != name:
                    continue
                if is_dma:
                    p = pos[i]
                    if p >= self.DMAK:
                        sk = (eng, True, p % self.DMAK)
                        do_wait(sk, 16 * (p // self.DMAK))
                for sk, v in sorted(redeps[i].items(), key=lambda x: str(x[0])):
                    do_wait(sk, v)
                if is_dma:
                    sk = (eng, True, pos[i] % self.DMAK)
                    fn[0](engobj).then_inc(sems[sk], 16)
                else:
                    for f in fn[:-1]:
                        f(engobj)
                    fn[-1](engobj).then_inc(sems[skey[i]], 1)
            for k, v in sorted(finals.items(), key=lambda x: str(x[0])):
                do_wait(k, v)

        for name in ("sync", "scalar", "vector", "tensor", "gpsimd"):
            getattr(block, name)(lambda e, n=name: section(e, n))
        return counts


class MK:
    def __init__(self, nc, s):
        self.nc = nc
        self.s = s
        self.t = {}

    def sb(self, name, shape, dt):
        self.t[name] = self.nc.alloc_sbuf_tensor(name, list(shape), dt)
        return self.t[name]

    def ps(self, name, shape, dt=F32):
        self.t[name] = self.nc.alloc_psum_tensor(name, list(shape), dt)
        return self.t[name]

    def dram(self, name, shape, dt, kind="Internal", addr_space="Local"):
        self.t[name] = self.nc.dram_tensor(name, list(shape), dt, kind=kind,
                                           addr_space=addr_space).ap()
        return self.t[name]

    def view(self, name, ap):
        self.t[name] = ap
        return ap

    def __getitem__(self, name):
        v = self.t[name]
        return v if isinstance(v, bass.AP) else v


def ap_of(x):
    return x if isinstance(x, bass.AP) else x.ap()[:, :]



def _lpbq_dequant_ops(mk, wv, ifull, scr, scr2, sg, isg, out_bf16, rkeys):
    s = mk.s
    ng = ifull // 32
    s.op("scalar", lambda e: e.activation(scr[:, 0:ifull], wv, AF.Abs),
         r=rkeys, w=("R0", "R1"))
    cur, oth = scr, scr2
    curw = 32
    while curw > 1:
        half = curw // 2
        v = cur[:, 0:ng * curw].rearrange("p (g w) -> p g w", w=curw)
        dv = oth[:, 0:ng * half].rearrange("p (g w) -> p g w", w=half)
        s.op("vector", lambda e, v=v, dv=dv, half=half: e.tensor_tensor(
            out=dv, in0=v[:, :, 0:half], in1=v[:, :, half:2 * half], op=ALU.max),
            r=("R0", "R1"), w=("R0", "R1"))
        cur, oth = oth, cur
        curw = half
    s.op("vector", lambda e, cur=cur: e.tensor_scalar(
        sg[:, 0:ng], cur[:, 0:ng], 1.0 / 7.0, 1e-12, ALU.mult, ALU.max),
        r=("R0", "R1"), w=("PAMW",))
    s.op("vector", lambda e: e.reciprocal(isg[:, 0:ng], sg[:, 0:ng]),
         r=("PAMW",), w=("PAMW",))
    sgb = sg[:, 0:ng].rearrange("p (g o) -> p g o", o=1).broadcast_to([128, ng, 32])
    isgb = isg[:, 0:ng].rearrange("p (g o) -> p g o", o=1).broadcast_to([128, ng, 32])
    wg_ = wv.rearrange("p (g w) -> p g w", w=32)
    scrg = scr[:, 0:ifull].rearrange("p (g w) -> p g w", w=32)
    scr2g = scr2[:, 0:ifull].rearrange("p (g w) -> p g w", w=32)
    s.op("vector", lambda e: e.tensor_tensor(out=scrg, in0=wg_, in1=isgb, op=ALU.mult),
         r=rkeys + ("PAMW",), w=("R0", "R1"))
    s.op("vector", lambda e: e.tensor_scalar(scr[:, 0:ifull], scr[:, 0:ifull],
                                             MAGIC, MAGIC, ALU.add, ALU.subtract),
         r=("R0", "R1"), w=("R0", "R1"))
    s.op("vector", lambda e: e.tensor_tensor(out=scr2g, in0=scrg, in1=sgb, op=ALU.mult),
         r=("R0", "R1", "PAMW"), w=("R0", "R1"))
    if out_bf16 is not None:
        s.op("vector", lambda e: e.tensor_copy(out=out_bf16, in_=scr2[:, 0:ifull]),
             r=("R0", "R1"), w=("R2",))


def prep_ffn_gateup(mk, wname, blobname, sblobname, lpbq):
    s = mk.s
    w = mk[wname]
    shard = mk.dram(blobname + "_shard", [3, 8, 128, 128], BF16)
    if sblobname:
        mk.dram(sblobname + "_shard", [3, 128], F32)
    wn, scr, scr2 = mk["p_wnat"], mk["p_scr"], mk["p_scr2"]
    am = mk["p_am"]
    stg, stg2 = mk["p_stage"], mk["p_stage2"]
    idb = mk["identb"]
    shk = "sh_" + blobname

    for k in range(3):
        def ld(e, k=k):
            pid = s.core_id("sync")
            return e.dma_start(out=wn[:, 0:D], in_=w[bass.ds(pid * 384 + k * 128, 128), :])
        s.dma("sync", ld, r=(), w=("R0",))
        if not lpbq:
            s.op("vector", lambda e: e.reduce_max(am[:, 0:1], wn[:, 0:D], AX.X,
                                                  apply_absolute_value=True),
                 r=("R0",), w=("HM2",))
            s.op("vector", lambda e: e.tensor_scalar(am[:, 1:2], am[:, 0:1],
                                                     1.0 / 127.0, 1e-30, ALU.mult, ALU.max),
                 r=("HM2",), w=("HM2",))
            s.op("vector", lambda e: e.reciprocal(am[:, 2:3], am[:, 1:2]),
                 r=("HM2",), w=("HM2",))
            s.dma("sync", lambda e, k=k: e.dma_start(out=mk[sblobname + "_shard"][k, :],
                                                     in_=am[:, 1:2]),
                  r=("HM2",), w=("sh_" + sblobname,))
            s.op("vector", lambda e: e.tensor_scalar(scr[:, 0:D], wn[:, 0:D],
                                                     am[:, 2:3], MAGIC, ALU.mult, ALU.add),
                 r=("R0", "HM2"), w=("R0",))
            s.op("vector", lambda e: e.tensor_scalar(stg[:, 0:D], scr[:, 0:D],
                                                     MAGIC, None, ALU.subtract),
                 r=("R0",), w=("R2",))
        else:
            _lpbq_dequant_ops(mk, wn[:, 0:D], D, scr, scr2, mk["p_am"], mk["p_am2"],
                              stg[:, 0:D], ("R0",))
        pst = mk["ptb0"] if k % 2 == 0 else mk["ptb1"]
        pbk = "ptb0" if k % 2 == 0 else "ptb1"
        for c in range(8):
            s.op("tensor", lambda e, c=c, pst=pst: e.transpose(
                pst[:, c * 128:(c + 1) * 128], stg[:, c * 128:(c + 1) * 128], idb[:]),
                r=("R2", "IDB"), w=(pbk,), fuse=(c != 0))
        s.op("vector", lambda e, pst=pst: e.tensor_copy(out=stg2[:, 0:D], in_=pst[:, 0:D]),
             r=(pbk,), w=("R2",))
        s.dma("sync", lambda e, k=k: e.dma_start(
            out=shard[k].rearrange("c h i -> h c i"),
            in_=stg2[:, 0:D].rearrange("h (c i) -> h c i", i=128)),
            r=("R2",), w=(shk,))


def prep_ffn_down(mk, wname, blobname, lpbq, ifull):
    s = mk.s
    w = mk[wname]
    shard = mk.dram(blobname + "_shard", [1, 24, 128, 128], BF16)
    wn, scr, scr2 = mk["p_wnat3"], mk["p_scr3"], mk["p_scr3b"]
    am = mk["p_am"]
    stg, stg2 = mk["p_stage"], mk["p_stage2"]
    idb = mk["identb"]
    shk = "sh_" + blobname

    def ld(e):
        pid = s.core_id("sync")
        return e.dma_start(out=wn[:, 0:ifull], in_=w[bass.ds(pid * 128, 128), :])
    s.dma("sync", ld, r=(), w=("R0",))
    if not lpbq:
        s.op("vector", lambda e: e.reduce_max(am[:, 0:1], wn[:, 0:ifull], AX.X,
                                              apply_absolute_value=True),
             r=("R0",), w=("HM2",))
        s.op("vector", lambda e: e.tensor_scalar(am[:, 1:2], am[:, 0:1],
                                                 1.0 / 127.0, 1e-30, ALU.mult, ALU.max),
             r=("HM2",), w=("HM2",))
        s.op("vector", lambda e: e.reciprocal(am[:, 2:3], am[:, 1:2]),
             r=("HM2",), w=("HM2",))
        s.op("vector", lambda e: e.tensor_scalar(scr[:, 0:ifull], wn[:, 0:ifull],
                                                 am[:, 2:3], MAGIC, ALU.mult, ALU.add),
             r=("R0", "HM2"), w=("R0", "R1"))
        s.op("vector", lambda e: e.tensor_scalar(scr2[:, 0:ifull], scr[:, 0:ifull],
                                                 MAGIC, am[:, 1:2], ALU.subtract, ALU.mult),
             r=("R0", "R1", "HM2"), w=("R1",))
    else:
        _lpbq_dequant_ops(mk, wn[:, 0:ifull], ifull, scr, scr2, mk["p_amw"],
                          mk["p_amw2"], None, ("R0",))
    nic_real = ifull // 128
    for ic in range(24):
        ic_src = min(ic, nic_real - 1)
        cc = ic % 8
        pst = mk["ptb0"] if (ic // 8) % 2 == 0 else mk["ptb1"]
        pbk = "ptb0" if (ic // 8) % 2 == 0 else "ptb1"
        s.op("vector", lambda e, ics=ic_src, cc=cc: e.tensor_copy(
            out=stg[:, cc * 128:(cc + 1) * 128],
            in_=scr2[:, ics * 128:(ics + 1) * 128]),
            r=("R0", "R1"), w=("R2",))
        s.op("tensor", lambda e, cc=cc, pst=pst: e.transpose(
            pst[:, cc * 128:(cc + 1) * 128], stg[:, cc * 128:(cc + 1) * 128], idb[:]),
            r=("R2", "IDB"), w=(pbk,))
        if cc == 7 or ic == 24 - 1:
            lo, n = ic - cc, cc + 1
            s.op("vector", lambda e, n=n, pst=pst: e.tensor_copy(
                out=stg2[:, 0:n * 128], in_=pst[:, 0:n * 128]),
                r=(pbk,), w=("R2",))
            s.dma("sync", lambda e, lo=lo, n=n: e.dma_start(
                out=shard[0, lo:lo + n].rearrange("c i h -> i c h"),
                in_=stg2[:, 0:n * 128].rearrange("i (c h) -> i c h", h=128)),
                r=("R2",), w=(shk,))


def prep_all(mk, want_l0, want_l1):
    s = mk.s
    pairs = []
    if want_l0:
        prep_ffn_gateup(mk, "L0w_gate", "b_wg0", "b_swg0", False)
        prep_ffn_gateup(mk, "L0w_up", "b_wu0", "b_swu0", False)
        prep_ffn_down(mk, "L0w_down", "b_wd0", False, I0)
        pairs += [("b_wg0", "sh_b_wg0"), ("b_swg0", "sh_b_swg0"),
                  ("b_wu0", "sh_b_wu0"), ("b_swu0", "sh_b_swu0"),
                  ("b_wd0", "sh_b_wd0")]
    if want_l1:
        prep_ffn_gateup(mk, "L1w_gate", "b_wg1", None, True)
        prep_ffn_gateup(mk, "L1w_up", "b_wu1", None, True)
        prep_ffn_down(mk, "L1w_down", "b_wd1", True, I1)
        pairs += [("b_wg1", "sh_b_wg1"), ("b_wu1", "sh_b_wu1"),
                  ("b_wd1", "sh_b_wd1")]
    for blob, shk in pairs:
        i_ = mk[blob + "_shard"][:]
        o_ = mk[blob][:]
        s.op("gpsimd", lambda e, i_=i_, o_=o_: e.collective_compute(
            "AllGather", ALU.bypass, replica_groups=RG, ins=[i_], outs=[o_]),
            r=(shk,), w=("hb_" + blob,))



def xkeys():
    return tuple(f"x{i}" for i in range(NT))


def layernorm_stats(mk):
    s = mk.s
    XT, scr = mk["XT"], mk["scr1"]
    SUMS, SSQ = mk["st_sum"], mk["st_ssq"]
    MEAN, RSTD, NMR, VAR = mk["st_mean"], mk["st_rstd"], mk["st_nmr"], mk["st_var"]
    scrB = mk["scr2"]
    for i in range(NT):
        X = XT[:, i * D:(i + 1) * D]
        SS, SSk = (scr, "R1") if i % 2 == 0 else (scrB, "R1b")
        s.op("scalar", lambda e, X=X, SS=SS: e.activation(SS[:, 0:D], X, AF.Square),
             r=(f"x{i}",), w=(SSk,))
        s.op("vector", lambda e, i=i, SS=SS: e.reduce_sum(SSQ[:, i:i + 1], SS[:, 0:D], AX.X),
             r=(SSk,), w=("SSQ",))
        s.op("vector", lambda e, X=X, i=i: e.reduce_sum(SUMS[:, i:i + 1], X, AX.X),
             r=(f"x{i}",), w=("SUMS",))
    s.op("vector", lambda e: e.tensor_scalar(MEAN[:], SUMS[:], 1.0 / D, None, ALU.mult),
         r=("SUMS",), w=("MEAN",))
    s.op("vector", lambda e: e.tensor_tensor(out=VAR[:], in0=MEAN[:], in1=MEAN[:],
                                             op=ALU.mult), r=("MEAN",), w=("VAR",))
    s.op("vector", lambda e: e.scalar_tensor_tensor(VAR[:], SSQ[:], 1.0 / D, VAR[:],
                                                    ALU.mult, ALU.subtract),
         r=("SSQ", "VAR"), w=("VAR",))
    s.op("vector", lambda e: e.tensor_scalar(SUMS[:], VAR[:], 1e-5, None, ALU.add),
         r=("VAR",), w=("SUMS",))
    s.op("scalar", lambda e: e.activation(RSTD[:], SUMS[:], AF.Sqrt),
         r=("SUMS",), w=("RSTD",))
    s.op("vector", lambda e: e.reciprocal(RSTD[:], RSTD[:]), r=("RSTD",), w=("RSTD",))
    s.op("vector", lambda e: e.scalar_tensor_tensor(NMR[:], MEAN[:], -1.0, RSTD[:],
                                                    ALU.mult, ALU.mult),
         r=("MEAN", "RSTD"), w=("NMR",))


def cross_partition_max(mk, AM, amkey, ncols, dstkey):
    s = mk.s
    red1, idf = mk["st_red1"], mk["identf"]
    pt = mk["pt2"]
    s.op("vector", lambda e: e.reduce_max(red1[:, 0:1], AM[:, 0:ncols], AX.X),
         r=(amkey,), w=("RED1",))
    s.op("tensor", lambda e: e.transpose(pt[0:1, 0:128], red1[:, 0:1], idf[:]),
         r=("RED1", "IDF"), w=("pt2",))
    s.op("vector", lambda e: e.reduce_max(mk["st_g1"][0:1, 0:1], pt[0:1, 0:128], AX.X),
         r=("pt2",), w=(dstkey,))


def global_absmax_of_ln(mk):
    s = mk.s
    XT, scr = mk["XT"], mk["scr1"]
    RSTD, NMR, AM = mk["st_rstd"], mk["st_nmr"], mk["st_amax"]
    scrB = mk["scr2"]
    for i in range(NT):
        X = XT[:, i * D:(i + 1) * D]
        SS, SSk = (scr, "R1") if i % 2 == 0 else (scrB, "R1b")
        s.op("scalar", lambda e, X=X, i=i, SS=SS: e.activation(
            SS[:, 0:D], X, AF.Abs, bias=NMR[:, i:i + 1], scale=RSTD[:, i:i + 1]),
            r=(f"x{i}", "RSTD", "NMR"), w=(SSk,))
        s.op("vector", lambda e, i=i, SS=SS: e.reduce_max(AM[:, i:i + 1], SS[:, 0:D], AX.X),
             r=(SSk,), w=("AMAX",))
    cross_partition_max(mk, AM, "AMAX", NT, "G1")


def allreduce_max(mk, n):
    s = mk.s
    cin, cout = mk["cc_in"], mk["cc_out"]
    s.dma("sync", lambda e: e.dma_start(out=cin[0:1, 0:n], in_=mk["ccstage"][0:1, 0:n]),
          r=("CCS",), w=("hb_cc",))
    s.op("gpsimd", lambda e: e.collective_compute(
        "AllReduce", ALU.max, replica_groups=RG,
        ins=[cin[0:1, 0:n]], outs=[cout[0:1, 0:n]]), r=("hb_cc",), w=("hb_cc2",))
    s.dma("sync", lambda e: e.dma_start(out=mk["st_gmax"][0:1, 0:n], in_=cout[0:1, 0:n]),
          r=("hb_cc2",), w=("GMAX",))


def broadcast_scalars(mk, src_ap, srckeys, dst, dkey, ncols):
    s = mk.s
    pt = mk["pt2"]
    s.op("tensor", lambda e: e.matmul(pt[:, 0:ncols], mk["onesf"][0:1, :], src_ap,
                                      start=True, stop=True),
         r=srckeys + ("ONESF",), w=("pt2",))
    s.op("vector", lambda e: e.tensor_copy(out=mk[dst][:, 0:ncols], in_=pt[:, 0:ncols]),
         r=("pt2",), w=(dkey,))


def quantize_x_to_xqt(mk, mode="int8"):
    s = mk.s
    XT, scr, scrb = mk["XT"], mk["scr1"], mk["scr2"]
    RSTD, NMR, RS, NMRS = mk["st_rstd"], mk["st_nmr"], mk["st_rs"], mk["st_nmrs"]
    XQK, XQT = mk["xqk"], mk["XQT"]
    idb = mk["identb"]
    s.op("vector", lambda e: e.tensor_scalar(RS[:], RSTD[:], mk["sxb"][:, 1:2],
                                             None, ALU.mult),
         r=("RSTD", "SXB"), w=("RS",))
    s.op("vector", lambda e: e.tensor_scalar(NMRS[:], NMR[:], mk["sxb"][:, 1:2],
                                             None, ALU.mult),
         r=("NMR", "SXB"), w=("NMRS",))
    for i in range(NT):
        X = XT[:, i * D:(i + 1) * D]
        if i % 2 == 0:
            SA, SB, XO = scr, scrb, XQK
            SAk, SBk, XOk = "R1", "R1b", "R2"
        else:
            SA, SB, XO = mk["p_scr"], mk["p_scr2"], mk["attt"]
            SAk, SBk, XOk = "R0", "R0", "R2c"
        s.op("scalar", lambda e, X=X, i=i, SA=SA: e.activation(
            SA[:, 0:D], X, AF.Identity, bias=NMRS[:, i:i + 1], scale=RS[:, i:i + 1]),
            r=(f"x{i}", "RS", "NMRS"), w=(SAk,))
        if mode == "int8":
            s.op("vector", lambda e, SA=SA, SB=SB: e.tensor_scalar(
                SB[:, 0:D], SA[:, 0:D], MAGIC, MAGIC, ALU.add, ALU.subtract),
                r=(SAk,), w=(SBk,))
            s.op("vector", lambda e, SB=SB, XO=XO: e.tensor_scalar(
                XO[:, 0:D], SB[:, 0:D], 127.0, -128.0, ALU.min, ALU.max),
                r=(SBk,), w=(XOk,), fuse=True)
        else:
            s.op("vector", lambda e, SA=SA, SB=SB: e.tensor_scalar(
                SB[:, 0:D], SA[:, 0:D], MAGIC, None, ALU.add),
                r=(SAk,), w=(SBk,))
            s.op("vector", lambda e, SB=SB, XO=XO: e.tensor_scalar(
                XO[:, 0:D], SB[:, 0:D], MAGIC, mk["sxb"][:, 0:1],
                ALU.subtract, ALU.mult),
                r=(SBk, "SXB"), w=(XOk,), fuse=True)
        pst = mk["ptb0"] if i % 2 == 0 else mk["ptb1"]
        pbk = "ptb0" if i % 2 == 0 else "ptb1"
        for c in range(8):
            s.op("tensor", lambda e, c=c, pst=pst, XO=XO: e.transpose(
                pst[:, c * 128:(c + 1) * 128], XO[:, c * 128:(c + 1) * 128], idb[:]),
                r=(XOk, "IDB"), w=(pbk,), fuse=(c != 0))
        dst = XQT.rearrange("p (c t) -> p c t", t=R)[:, :, i * 128:(i + 1) * 128]
        srcv = ap_of(pst)[:, 0:D].rearrange("p (c t) -> p c t", t=128)
        s.op("vector", lambda e, dst=dst, srcv=srcv: e.tensor_copy(out=dst, in_=srcv),
             r=(pbk,), w=("AC", "ACt", "ACt2"))


def load_scale_blob(mk, blobname, dstname, dkey, nic):
    s = mk.s
    tmp, idf = mk["sc_tmp"], mk["identf"]
    pt = mk["pt2"]
    s.dma("sync", lambda e: e.dma_start(out=tmp[0:24, :], in_=mk[blobname][:, :]),
          r=("hb_" + blobname,), w=("R2",))
    s.op("tensor", lambda e: e.transpose(pt[:, 0:24], tmp[0:24, :], idf[0:24, 0:24]),
         r=("R2", "IDF"), w=("pt2",))
    s.op("vector", lambda e: e.tensor_copy(out=mk[dstname][:, 0:nic], in_=pt[:, 0:nic]),
         r=("pt2",), w=(dkey,))



def sublayer_l0ffn(mk):
    s = mk.s

    def dump(tag, ap, shape, dt, rk):
        if tag in DEBUG["dumps"] and ("dbg_" + tag) not in mk.t:
            o = mk.dram("dbg_" + tag, list(shape), dt, kind="ExternalOutput")
            mk.s.dma("sync", lambda e: e.dma_start(out=o, in_=ap), r=rk, w=())
    layernorm_stats(mk)
    global_absmax_of_ln(mk)
    SC = mk["scal"]
    s.op("vector", lambda e: e.tensor_copy(out=mk["ccstage"][0:1, 0:1],
                                           in_=mk["st_g1"][0:1, 0:1]),
         r=("G1",), w=("CCS",))
    allreduce_max(mk, 1)
    s.op("scalar", lambda e: e.activation(SC[0:1, 0:1], mk["clipx_t"][0:1, 0:1],
                                          AF.Sigmoid), r=("CLX",), w=("SC",))
    s.op("vector", lambda e: e.scalar_tensor_tensor(
        SC[0:1, 1:2], SC[0:1, 0:1], 1.0 / 127.0, mk["st_gmax"][0:1, 0:1],
        ALU.mult, ALU.mult), r=("SC", "GMAX"), w=("SC",))
    s.op("vector", lambda e: e.reciprocal(SC[0:1, 2:3], SC[0:1, 1:2]),
         r=("SC",), w=("SC",))
    broadcast_scalars(mk, SC[0:1, 1:3], ("SC",), "sxb", "SXB", 2)
    dump("rstd", ap_of(mk.t["st_rstd"]), (128, 16), F32, ("RSTD",))
    dump("sxb", ap_of(mk.t["sxb"]), (128, 2), F32, ("SXB",))
    quantize_x_to_xqt(mk)
    dump("xqt", mk["XQT"], (128, 8 * R), BF16, ("AC", "ACt"))

    load_scale_blob(mk, "b_swg0", "swg", "SWG", IC0)
    load_scale_blob(mk, "b_swu0", "swu", "SWU", IC0)
    s.op("vector", lambda e: e.tensor_scalar(mk["swg"][:, 0:IC0], mk["swg"][:, 0:IC0],
                                             mk["sxb"][:, 0:1], None, ALU.mult),
         r=("SWG", "SXB"), w=("SWG",))
    s.op("vector", lambda e: e.tensor_scalar(mk["swu"][:, 0:IC0], mk["swu"][:, 0:IC0],
                                             mk["sxb"][:, 0:1], None, ALU.mult),
         r=("SWU", "SXB"), w=("SWU",))

    XQTv = mk["XQT"].rearrange("p (c t) -> p c t", t=R)
    WG, WU = mk["wg_t"], mk["wu_t"]
    silg, upr = mk["silg"], mk["upr"]
    Hf16 = mk["Hf16"]
    HM = mk["hmax"]
    for ic in range(IC0):
        sl = ic % 2
        wgk, wuk = f"WGT{sl}", "WUT0"
        s.dma("sync", lambda e, ic=ic, sl=sl: e.dma_start(
            out=ap_of(WG)[:, sl * D:(sl + 1) * D].rearrange("p (c i) -> p c i", i=128),
            in_=mk["b_wg0"][ic].rearrange("c p i -> p c i")),
            r=("hb_b_wg0", "hb_b_wd0"), w=(wgk,))
        s.dma("sync", lambda e, ic=ic: e.dma_start(
            out=ap_of(WU)[:, 0:D].rearrange("p (c i) -> p c i", i=128),
            in_=mk["b_wu0"][ic].rearrange("c p i -> p c i")),
            r=("hb_b_wu0", "hb_b_wd0"), w=(wuk,))
        for tt in range(4):
            pg, pgk = (mk["pt0"], "pt0") if tt % 2 == 0 else (mk["pt1"], "pt1")
            pu, puk = (mk["pt2"], "pt2") if tt % 2 == 0 else (mk["pt3"], "pt3")
            for hc in range(8):
                s.op("tensor", lambda e, hc=hc, tt=tt, sl=sl, pg=pg: e.matmul(
                    pg[:, 0:512], WG[:, sl * D + hc * 128:sl * D + (hc + 1) * 128],
                    XQTv[:, hc, tt * 512:(tt + 1) * 512],
                    start=(hc == 0), stop=(hc == 7)),
                    r=(wgk, "AC", "ACt", "ACt2"), w=(pgk,), fuse=(hc != 0))
            for hc in range(8):
                s.op("tensor", lambda e, hc=hc, tt=tt, pu=pu: e.matmul(
                    pu[:, 0:512], WU[:, hc * 128:(hc + 1) * 128],
                    XQTv[:, hc, tt * 512:(tt + 1) * 512],
                    start=(hc == 0), stop=(hc == 7)),
                    r=(wuk, "AC", "ACt", "ACt2"), w=(puk,), fuse=(hc != 0))
            s.op("scalar", lambda e, ic=ic, pg=pg: e.activation(
                silg[:], pg[:, 0:512], AF.Silu, scale=mk["swg"][:, ic:ic + 1]),
                r=(pgk, "SWG"), w=("SILG",))
            s.op("scalar", lambda e, ic=ic, pu=pu: e.activation(
                upr[:], pu[:, 0:512], AF.Identity, scale=mk["swu"][:, ic:ic + 1]),
                r=(puk, "SWU"), w=("UPR",), fuse=True)
            hv = Hf16[:, ic * R + tt * 512: ic * R + (tt + 1) * 512]
            s.op("vector", lambda e, hv=hv: e.tensor_tensor(out=hv, in0=silg[:],
                                                            in1=upr[:], op=ALU.mult),
                 r=("SILG", "UPR"), w=ARN)
        hic = Hf16[:, ic * R:(ic + 1) * R]
        s.op("vector", lambda e, hic=hic, ic=ic: e.reduce_max(
            HM[:, ic:ic + 1], hic, AX.X,
            apply_absolute_value=True), r=ARN, w=("HM",))

    dump("hf", mk["Hf16"], (128, IC0 * R), F16, ARN)
    dump("swg", ap_of(mk.t["swg"]), (128, 24), F32, ("SWG",))
    cross_partition_max(mk, HM, "HM", IC0, "G1")
    s.op("vector", lambda e: e.tensor_copy(out=mk["ccstage"][0:1, 0:1],
                                           in_=mk["st_g1"][0:1, 0:1]),
         r=("G1",), w=("CCS",))
    allreduce_max(mk, 1)
    s.op("scalar", lambda e: e.activation(SC[0:1, 0:1], mk["clipd_t"][0:1, 0:1],
                                          AF.Sigmoid), r=("CLD",), w=("SC",))
    s.op("vector", lambda e: e.scalar_tensor_tensor(
        SC[0:1, 1:2], SC[0:1, 0:1], 1.0 / 127.0, mk["st_gmax"][0:1, 0:1],
        ALU.mult, ALU.mult), r=("SC", "GMAX"), w=("SC",))
    s.op("vector", lambda e: e.reciprocal(SC[0:1, 2:3], SC[0:1, 1:2]),
         r=("SC",), w=("SC",))
    s.op("vector", lambda e: e.tensor_scalar(SC[0:1, 3:4], SC[0:1, 1:2], 0.5,
                                             None, ALU.mult), r=("SC",), w=("SC",))
    broadcast_scalars(mk, SC[0:1, 1:4], ("SC",), "shb", "SHB", 3)

    HQ = mk["H"]
    for ic in range(IC0):
        for tt in range(4):
            par = (ic * 4 + tt) % 2
            S, Sk = (silg, "SILG") if par == 0 else (mk["silgB"], "SILGB")
            U, Uk = (upr, "UPR") if par == 0 else (mk["uprB"], "UPRB")
            Sx = (Sk,) if par == 0 else (Sk, "ACt2")
            Ux = (Uk,) if par == 0 else (Uk, "ACt2")
            hv = Hf16[:, ic * R + tt * 512: ic * R + (tt + 1) * 512]
            hq = HQ[:, ic * R + tt * 512: ic * R + (tt + 1) * 512]
            s.op("vector", lambda e, hv=hv, S=S: e.tensor_scalar(
                S[:, 0:512], hv, mk["shb"][:, 1:2], MAGIC, ALU.mult, ALU.add),
                r=ARN + ("SHB",), w=Sx)
            s.op("vector", lambda e, S=S, U=U: e.tensor_scalar(
                U[:, 0:512], S[:, 0:512], MAGIC, 127.0, ALU.subtract, ALU.min),
                r=Sx, w=Ux, fuse=True)
            s.op("vector", lambda e, hq=hq, U=U: e.tensor_scalar(
                hq, U[:, 0:512], -128.0, None, ALU.max), r=Ux, w=ARN, fuse=True)

    dump("hq", mk["H"], (128, IC0 * R), BF16, ARN)
    XT = mk["XT"]
    WD = mk["wd_t"]
    for half in range(2):
        for ic in range(IC0):
            src = mk["b_wd0"][half * 4:(half + 1) * 4, ic, :, :].rearrange("c i h -> i c h")
            s.dma("sync", lambda e, ic=ic, src=src: e.dma_start(
                out=WD[:, ic * 512:(ic + 1) * 512].rearrange("i (c h) -> i c h", h=128),
                in_=src), r=("hb_b_wd0",), w=("AC",))
        for i in range(NT):
            pd, pdk = (mk["pt0"], "pt0") if i % 2 == 0 else (mk["pt1"], "pt1")
            for ic in range(IC0):
                s.op("tensor", lambda e, i=i, ic=ic, pd=pd: e.matmul(
                    pd[:, 0:512], HQ[:, ic * R + i * 128: ic * R + (i + 1) * 128],
                    WD[:, ic * 512:(ic + 1) * 512],
                    start=(ic == 0), stop=(ic == IC0 - 1)),
                    r=ARN + ("AC",), w=(pdk,), fuse=(ic != 0))
            dstx = XT[:, i * D + half * 512: i * D + (half + 1) * 512]
            s.op("vector", lambda e, dstx=dstx, pd=pd: e.scalar_tensor_tensor(
                dstx, pd[:, 0:512], mk["shb"][:, 2:3], dstx, ALU.mult, ALU.add),
                r=(pdk, "SHB", f"x{i}"), w=(f"x{i}",))



ATTN_STD = [1.0, 2.0, 4.0, 8.0, 1.0, 2.0, 4.0, 8.0]


def ln_to_actt(mk, reorder=False):
    s = mk.s
    XT = mk["XT"]
    XQK, XQT = mk["xqk"], mk["XQT"]
    idb = mk["identb"]
    for i in range(NT):
        X = XT[:, i * D:(i + 1) * D]
        XO, XOk = (XQK, "R2") if i % 2 == 0 else (mk["attt"], "R2c")
        s.op("scalar", lambda e, X=X, i=i, XO=XO: e.activation(
            XO[:, 0:D], X, AF.Identity, bias=mk["st_nmr"][:, i:i + 1],
            scale=mk["st_rstd"][:, i:i + 1]),
            r=(f"x{i}", "RSTD", "NMR"), w=(XOk,))
        pst = mk["ptb0"] if i % 2 == 0 else mk["ptb1"]
        pbk = "ptb0" if i % 2 == 0 else "ptb1"
        for c in range(8):
            s.op("tensor", lambda e, c=c, pst=pst, XO=XO: e.transpose(
                pst[:, c * 128:(c + 1) * 128], XO[:, c * 128:(c + 1) * 128], idb[:]),
                r=(XOk, "IDB"), w=(pbk,), fuse=(c != 0))
        if not reorder:
            dst = XQT.rearrange("p (c t) -> p c t", t=R)[:, :, i * 128:(i + 1) * 128]
            srcv = ap_of(pst)[:, 0:D].rearrange("p (c t) -> p c t", t=128)
        else:
            b, tc = i // 2, i % 2
            dst = XQT.rearrange("p (c G j bb) -> p c G j bb", c=8, G=16, j=16, bb=8)[
                :, :, tc * 8:(tc + 1) * 8, :, b]
            srcv = ap_of(pst)[:, 0:D].rearrange(
                "p (c gl j) -> p c gl j", c=8, gl=8, j=16)
        s.op("vector", lambda e, dst=dst, srcv=srcv: e.tensor_copy(out=dst, in_=srcv),
             r=(pbk,), w=("AC", "ACt", "ACt2"))


def sublayer_l0attn(mk):
    import math
    s = mk.s
    layernorm_stats(mk)
    ln_to_actt(mk)

    s.dma("gpsimd", lambda e: e.dma_start(
        out=mk["WIT"].rearrange("p (c m) -> p c m", m=D),
        in_=mk["W_inT_d"].rearrange("(c p) m -> p c m", p=128)),
        r=(), w=("R2", "R2c"))
    LNXT = mk["XQT"].rearrange("p (c t) -> p c t", t=R)
    VTOK = mk["VTOK"]
    for i in range(NT):
        if i % 2 == 0:
            ph = ((mk["pt0"], "pt0"), (mk["pt1"], "pt1"))
        else:
            ph = ((mk["pt2"], "pt2"), (mk["pt3"], "pt3"))
        for half in range(2):
            pp, ppk = ph[half]
            for hc in range(8):
                s.op("tensor", lambda e, i=i, half=half, hc=hc, pp=pp: e.matmul(
                    pp[:, 0:512],
                    LNXT[:, hc, i * 128:(i + 1) * 128],
                    mk["WIT"][:, hc * D + half * 512: hc * D + (half + 1) * 512],
                    start=(hc == 0), stop=(hc == 7)),
                    r=("AC", "ACt", "ACt2", "R2"), w=(ppk,), fuse=(hc != 0))
            dst = VTOK[:, i * D + half * 512:i * D + (half + 1) * 512]
            s.op("scalar", lambda e, dst=dst, pp=pp: e.activation(
                dst, pp[:, 0:512], AF.Identity), r=(ppk,), w=("R0",))

    PBT = mk["pbt"]
    dm = mk["scr1"]
    s.dma("sync", lambda e: e.dma_start(out=dm[:, 0:1536], in_=mk["dm_d"]),
          r=(), w=("R1",))
    for h in range(8):
        std = ATTN_STD[h]
        sidx = {1.0: 0, 2.0: 1, 4.0: 2, 8.0: 3}[std]
        inv_norm = 1.0 / (std * math.sqrt(2.0 * math.pi))
        s.op("scalar", lambda e, sidx=sidx: e.activation(
            mk["scr2"][:, 0:384], dm[:, sidx * 384:(sidx + 1) * 384], AF.Square),
            r=("R1",), w=("R1b",))
        s.op("scalar", lambda e: e.activation(
            mk["scr2"][:, 384:768], mk["scr2"][:, 0:384], AF.Exp, scale=-0.5),
            r=("R1b",), w=("R1b",))
        s.op("vector", lambda e, h=h, inv_norm=inv_norm: e.tensor_scalar(
            PBT[:, h * 384:(h + 1) * 384], mk["scr2"][:, 384:768],
            float(inv_norm), None, ALU.mult), r=("R1b",), w=("R2",))

    vshard = mk["v_eg_shard"]
    for side in range(2):
        src = VTOK.rearrange("p (b tc m) -> p b tc m", tc=2, m=D)[:, :, side, :]
        s.dma("sync", lambda e, side=side, src=src: e.dma_start(
            out=vshard[side].rearrange("b p m -> p b m"), in_=src),
            r=("R0",), w=("hb_vsh",))
    HALO = mk["HALO"]
    s.op("vector", lambda e: e.memset(HALO, 0.0), r=("AC", "ACt"), w=("AC", "ACt", "ACt2"))
    zsrc = HALO.rearrange("p (s b m) -> p s b m", s=2, m=D)
    for g in (0, NCORES + 1):
        s.dma("sync", lambda e, g=g: e.dma_start(
            out=mk["v_eg_g"][g].rearrange("s b p m -> p s b m"), in_=zsrc),
            r=("AC", "ACt", "ACt2"), w=("hb_veg",))
    s.op("gpsimd", lambda e: e.collective_compute(
        "AllGather", ALU.bypass, replica_groups=RG,
        ins=[vshard[:]], outs=[mk["v_eg_g"][1:NCORES + 1]]),
        r=("hb_vsh", "hb_veg"), w=("hb_veg",))

    def ld_halo(e, hside):
        pid = s.core_id("sync")
        slot = pid if hside == 0 else pid + 2
        src = mk["v_eg_g"][bass.ds(slot, 1), 1 - hside].rearrange("o b p m -> p (o b) m")
        dst = HALO.rearrange("p (g m) -> p g m", m=D)[:, hside * 8:(hside + 1) * 8, :]
        return e.dma_start(out=dst, in_=src)
    s.dma("sync", lambda e: ld_halo(e, 0), r=("hb_veg",), w=("AC", "ACt", "ACt2"))
    s.dma("sync", lambda e: ld_halo(e, 1), r=("hb_veg",), w=("AC", "ACt", "ACt2"))

    ATTOK = mk["ATTOK"]
    psa = mk["pt_a"]
    for tc in range(2):
        for h in range(8):
            if h % 2 == 0:
                ph = ((psa, "pta", 0), (psa, "pta", 512))
            else:
                ph = ((mk["pt0"], "pt0", 0), (mk["pt1"], "pt1", 0))
            for half in range(2):
                pp, ppk, po = ph[half]
                order = [r for r in range(3) if 0 <= tc + r - 1 <= 1] + \
                        [r for r in range(3) if not (0 <= tc + r - 1 <= 1)]
                for j, r in enumerate(order):
                    kc = tc + r - 1
                    if kc == -1:
                        rhs = HALO.rearrange("p (g m) -> p g m", m=D)[
                            :, half * 4:half * 4 + 4, h * 128:(h + 1) * 128]
                        rk = "ACh"
                    elif kc == 2:
                        rhs = HALO.rearrange("p (g m) -> p g m", m=D)[
                            :, 8 + half * 4:8 + half * 4 + 4, h * 128:(h + 1) * 128]
                        rk = "ACh"
                    else:
                        rhs = VTOK.rearrange("p (b tc m) -> p b tc m", tc=2, m=D)[
                            :, half * 4:half * 4 + 4, kc, h * 128:(h + 1) * 128]
                        rk = "R0"
                    rks = ("AC", "ACt", "ACt2") if rk == "ACh" else (rk,)
                    s.op("tensor", lambda e, rhs=rhs, h=h, r=r, pp=pp, po=po,
                         j=j: e.matmul(
                        pp[:, po:po + 512],
                        mk["pbt"][:, (h * 3 + r) * 128:(h * 3 + r + 1) * 128],
                        rhs, start=(j == 0), stop=(j == 2)),
                        r=rks + ("R2",), w=(ppk,), fuse=(j == 1))
            for half in range(2):
                pp, ppk, po = ph[half]
                dst = ATTOK.rearrange("p (tc b m) -> p tc b m", b=8, m=D)[
                    :, tc, half * 4:half * 4 + 4, h * 128:(h + 1) * 128]
                s.op("scalar", lambda e, dst=dst, pp=pp, po=po: e.activation(
                    dst, pp[:, po:po + 512], AF.Identity), r=(ppk,), w=("R1",))

    s.dma("gpsimd", lambda e: e.dma_start(
        out=mk["WOT"].rearrange("p (c m) -> p c m", m=D),
        in_=mk["W_outT_d"].rearrange("(c p) m -> p c m", p=128)),
        r=(), w=("R2b", "R2"))
    XT = mk["XT"]
    ATTT = mk["attt"]
    idb = mk["identb"]
    for i in range(NT):
        b, tc = i // 2, i % 2
        src = ATTOK.rearrange("p (tc b m) -> p tc b m", b=8, m=D)[:, tc, b, :]
        pst = mk["ptb0"] if i % 2 == 0 else mk["ptb1"]
        pbk = "ptb0" if i % 2 == 0 else "ptb1"
        for c in range(8):
            s.op("tensor", lambda e, c=c, src=src, pst=pst: e.transpose(
                pst[:, c * 128:(c + 1) * 128], src[:, c * 128:(c + 1) * 128], idb[:]),
                r=("R1", "IDB"), w=(pbk,), fuse=(c != 0))
        s.op("vector", lambda e, pst=pst: e.tensor_copy(out=ATTT[:], in_=pst[:, 0:D]),
             r=(pbk,), w=("R2c", "R2"))
        pd, pdk = (mk["pt2"], "pt2") if i % 2 == 0 else (mk["pt3"], "pt3")
        for half in range(2):
            for c in range(8):
                s.op("tensor", lambda e, c=c, half=half, pd=pd: e.matmul(
                    pd[:, 0:512], ATTT[:, c * 128:(c + 1) * 128],
                    mk["WOT"][:, c * D + half * 512: c * D + (half + 1) * 512],
                    start=(c == 0), stop=(c == 7)),
                    r=("R2c", "R2b"), w=(pdk,), fuse=(c != 0))
            dstx = XT[:, i * D + half * 512: i * D + (half + 1) * 512]
            s.op("vector", lambda e, dstx=dstx, pd=pd: e.tensor_tensor(
                out=dstx, in0=pd[:, 0:512], in1=dstx, op=ALU.add),
                r=(pdk, f"x{i}"), w=(f"x{i}",))



INV32767 = 1.0 / 32767.0
TWO16 = 65536.0
INV216 = 1.0 / 65536.0


def sublayer_l1ffn(mk):
    s = mk.s
    SC = mk["scal"]
    layernorm_stats(mk)
    global_absmax_of_ln(mk)
    s.op("vector", lambda e: e.tensor_copy(out=mk["ccstage"][0:1, 0:1],
                                           in_=mk["st_g1"][0:1, 0:1]),
         r=("G1",), w=("CCS",))
    allreduce_max(mk, 1)
    s.op("vector", lambda e: e.tensor_scalar(SC[0:1, 1:2], mk["st_gmax"][0:1, 0:1],
                                             INV32767, 1e-12, ALU.mult, ALU.max),
         r=("GMAX",), w=("SC",))
    s.op("vector", lambda e: e.reciprocal(SC[0:1, 2:3], SC[0:1, 1:2]),
         r=("SC",), w=("SC",))
    broadcast_scalars(mk, SC[0:1, 1:3], ("SC",), "sxb", "SXB", 2)
    quantize_x_to_xqt(mk, mode="real16")

    XQTv = mk["XQT"].rearrange("p (c t) -> p c t", t=R)
    WG, WU = mk["wg_t"], mk["wu_t"]
    GATE = mk["GATE16"]
    upst = mk["upst"]
    UM, GM = mk["hmax"], mk["hmax2"]
    for ic in range(IC1):
        sl = ic % 2
        wgk, wuk = f"WGT{sl}", "WUT0"
        s.dma("sync", lambda e, ic=ic, sl=sl: e.dma_start(
            out=ap_of(WG)[:, sl * D:(sl + 1) * D].rearrange("p (c i) -> p c i", i=128),
            in_=mk["b_wg1"][ic].rearrange("c p i -> p c i")),
            r=("hb_b_wg1",), w=(wgk,))
        s.dma("sync", lambda e, ic=ic: e.dma_start(
            out=ap_of(WU)[:, 0:D].rearrange("p (c i) -> p c i", i=128),
            in_=mk["b_wu1"][ic].rearrange("c p i -> p c i")),
            r=("hb_b_wu1",), w=(wuk,))
        for tt in range(4):
            pg, pgk = (mk["pt0"], "pt0") if tt % 2 == 0 else (mk["pt1"], "pt1")
            pu, puk = (mk["pt2"], "pt2") if tt % 2 == 0 else (mk["pt3"], "pt3")
            for hc in range(8):
                s.op("tensor", lambda e, hc=hc, tt=tt, pu=pu: e.matmul(
                    pu[:, 0:512], WU[:, hc * 128:(hc + 1) * 128],
                    XQTv[:, hc, tt * 512:(tt + 1) * 512],
                    start=(hc == 0), stop=(hc == 7)),
                    r=(wuk, "AC", "ACt", "ACt2"), w=(puk,), fuse=(hc != 0))
            for hc in range(8):
                s.op("tensor", lambda e, hc=hc, tt=tt, sl=sl, pg=pg: e.matmul(
                    pg[:, 0:512], WG[:, sl * D + hc * 128:sl * D + (hc + 1) * 128],
                    XQTv[:, hc, tt * 512:(tt + 1) * 512],
                    start=(hc == 0), stop=(hc == 7)),
                    r=(wgk, "AC", "ACt", "ACt2"), w=(pgk,), fuse=(hc != 0))
            s.op("scalar", lambda e, pu=pu: e.activation(upst[:], pu[:, 0:512],
                                                         AF.Identity),
                 r=(puk,), w=("UPST",))
            s.dma("scalar", lambda e, ic=ic, tt=tt: e.dma_start(
                out=mk["up_hbm"][ic, :, tt * 512:(tt + 1) * 512], in_=upst[:]),
                r=("UPST",), w=(f"hup{ic}_{tt}",))
            s.op("vector", lambda e, ic=ic, tt=tt: e.reduce_max(
                UM[:, ic * 4 + tt:ic * 4 + tt + 1], upst[:], AX.X,
                apply_absolute_value=True), r=("UPST",), w=("HM",))
            gsl = GATE[:, ic * R + tt * 512: ic * R + (tt + 1) * 512]
            s.op("scalar", lambda e, gsl=gsl, pg=pg: e.activation(
                gsl, pg[:, 0:512], AF.Identity), r=(pgk,), w=ARN)
        gic = GATE[:, ic * R:(ic + 1) * R]
        s.op("vector", lambda e, gic=gic, ic=ic: e.reduce_max(
            GM[:, ic:ic + 1], gic, AX.X,
            apply_absolute_value=True), r=ARN, w=("HM2",))

    cross_partition_max(mk, UM, "HM", IC1 * 4, "G1")
    s.op("vector", lambda e: e.tensor_copy(out=mk["ccstage"][0:1, 0:1],
                                           in_=mk["st_g1"][0:1, 0:1]),
         r=("G1",), w=("CCS",))
    cross_partition_max(mk, GM, "HM2", IC1, "G1")
    s.op("vector", lambda e: e.tensor_copy(out=mk["ccstage"][0:1, 1:2],
                                           in_=mk["st_g1"][0:1, 0:1]),
         r=("G1",), w=("CCS",))
    allreduce_max(mk, 2)
    s.op("vector", lambda e: e.tensor_scalar(SC[0:1, 4:5], mk["st_gmax"][0:1, 0:1],
                                             INV32767, 1e-12, ALU.mult, ALU.max),
         r=("GMAX",), w=("SC",))
    s.op("vector", lambda e: e.reciprocal(SC[0:1, 5:6], SC[0:1, 4:5]),
         r=("SC",), w=("SC",))
    s.op("vector", lambda e: e.tensor_scalar(SC[0:1, 6:7], mk["st_gmax"][0:1, 1:2],
                                             INV32767, 1e-12, ALU.mult, ALU.max),
         r=("GMAX",), w=("SC",))
    s.op("vector", lambda e: e.reciprocal(SC[0:1, 7:8], SC[0:1, 6:7]),
         r=("SC",), w=("SC",))
    s.op("vector", lambda e: e.tensor_scalar(SC[0:1, 8:9], SC[0:1, 6:7],
                                             -MAGIC, None, ALU.mult),
         r=("SC",), w=("SC",))
    s.op("vector", lambda e: e.tensor_copy(out=SC[0:1, 9:10], in_=SC[0:1, 7:8]),
         r=("SC",), w=("SC",))
    s.op("vector", lambda e: e.tensor_copy(out=SC[0:1, 10:11], in_=SC[0:1, 6:7]),
         r=("SC",), w=("SC",))
    s.op("vector", lambda e: e.tensor_copy(out=SC[0:1, 11:12], in_=SC[0:1, 8:9]),
         r=("SC",), w=("SC",))
    s.op("vector", lambda e: e.tensor_copy(out=SC[0:1, 12:13], in_=SC[0:1, 5:6]),
         r=("SC",), w=("SC",))
    broadcast_scalars(mk, SC[0:1, 9:13], ("SC",), "bsc", "BSC", 4)

    silg, upr, sc512 = mk["silg"], mk["upr"], mk["sc512"]
    PM = mk["hmax"]
    for ic in range(IC1):
        for tt in range(4):
            par = (ic * 4 + tt) % 2
            if par == 0:
                S, U, C = silg, upr, sc512
                Sx, Ux, Cx = ("SILG",), ("UPR",), ("SC512", "ACt")
            else:
                S, U, C = mk["silgB"], mk["uprB"], mk["sc512B"]
                Sx, Ux, Cx = ("SILGB", "ACt2"), ("UPRB", "ACt2"), ("SC512B", "ACt2")
            gsl = GATE[:, ic * R + tt * 512: ic * R + (tt + 1) * 512]
            s.op("vector", lambda e, gsl=gsl, S=S: e.tensor_scalar(
                S[:, 0:512], gsl, mk["bsc"][:, 0:1], MAGIC, ALU.mult, ALU.add),
                r=ARN + ("BSC",), w=Sx)
            s.op("scalar", lambda e, S=S, U=U: e.activation(
                U[:, 0:512], S[:, 0:512], AF.Sigmoid, scale=mk["bsc"][:, 1:2],
                bias=mk["bsc"][:, 2:3]), r=Sx + ("BSC",), w=Ux)
            s.op("vector", lambda e, U=U, C=C: e.tensor_scalar(
                C[:, 0:512], U[:, 0:512], TWO16, MAGIC, ALU.mult, ALU.add),
                r=Ux, w=Cx)
            s.op("vector", lambda e, U=U, C=C: e.tensor_scalar(
                U[:, 0:512], C[:, 0:512], MAGIC, 65535.0, ALU.subtract, ALU.min),
                r=Cx, w=Ux, fuse=True)
            s.op("vector", lambda e, S=S, C=C: e.tensor_scalar(
                C[:, 0:512], S[:, 0:512], MAGIC, None, ALU.subtract),
                r=Sx, w=Cx, fuse=True)
            s.op("vector", lambda e, gsl=gsl, C=C, U=U: e.scalar_tensor_tensor(
                gsl, C[:, 0:512], INV216, U[:, 0:512], ALU.mult, ALU.mult),
                r=Cx + Ux, w=ARN, fuse=True)
        gic = GATE[:, ic * R:(ic + 1) * R]
        s.op("vector", lambda e, gic=gic, ic=ic: e.reduce_max(
            PM[:, ic:ic + 1], gic, AX.X,
            apply_absolute_value=True), r=ARN, w=("HM",))
    cross_partition_max(mk, PM, "HM", IC1, "G1")
    s.op("vector", lambda e: e.tensor_copy(out=mk["ccstage"][0:1, 0:1],
                                           in_=mk["st_g1"][0:1, 0:1]),
         r=("G1",), w=("CCS",))
    allreduce_max(mk, 1)
    s.op("vector", lambda e: e.tensor_tensor(out=SC[0:1, 13:14],
                                             in0=mk["st_gmax"][0:1, 0:1],
                                             in1=SC[0:1, 6:7], op=ALU.mult),
         r=("GMAX", "SC"), w=("SC",))
    s.op("vector", lambda e: e.tensor_scalar(SC[0:1, 14:15], SC[0:1, 13:14],
                                             INV32767, 1e-12, ALU.mult, ALU.max),
         r=("SC",), w=("SC",))
    s.op("vector", lambda e: e.reciprocal(SC[0:1, 15:16], SC[0:1, 14:15]),
         r=("SC",), w=("SC",))
    s.op("vector", lambda e: e.tensor_tensor(out=SC[0:1, 9:10], in0=SC[0:1, 6:7],
                                             in1=SC[0:1, 15:16], op=ALU.mult),
         r=("SC",), w=("SC",))
    s.op("vector", lambda e: e.tensor_copy(out=SC[0:1, 10:11], in_=SC[0:1, 5:6]),
         r=("SC",), w=("SC",))
    broadcast_scalars(mk, SC[0:1, 9:11], ("SC",), "bsc2", "BSC2", 2)

    QM = mk["hmax2"]
    upstB = ap_of(mk.t["wg_t"])[:, 0:1024].bitcast(F16)[:, 0:512]
    for ic in range(IC1):
        for tt in range(4):
            par = (ic * 4 + tt) % 2
            if par == 0:
                S, U, C = silg, upr, sc512
                Sx, Ux, Cx = ("SILG",), ("UPR",), ("SC512", "ACt")
                UP, UPk = mk["upst"], "UPST"
            else:
                S, U, C = mk["silgB"], mk["uprB"], mk["sc512B"]
                Sx, Ux, Cx = ("SILGB", "ACt2"), ("UPRB", "ACt2"), ("SC512B", "ACt2")
                UP, UPk = upstB, "WGT0"
            gsl = GATE[:, ic * R + tt * 512: ic * R + (tt + 1) * 512]
            s.op("vector", lambda e, gsl=gsl, S=S: e.tensor_scalar(
                S[:, 0:512], gsl, mk["bsc2"][:, 0:1], MAGIC, ALU.mult, ALU.add),
                r=ARN + ("BSC2",), w=Sx)
            s.op("vector", lambda e, S=S, C=C: e.tensor_scalar(
                C[:, 0:512], S[:, 0:512], MAGIC, None, ALU.subtract),
                r=Sx, w=Cx, fuse=True)
            s.dma("scalar", lambda e, ic=ic, tt=tt, UP=UP: e.dma_start(
                out=UP[:, 0:512] if not hasattr(UP, "ap") else UP[:, 0:512],
                in_=mk["up_hbm"][ic, :, tt * 512:(tt + 1) * 512]),
                r=(f"hup{ic}_{tt}",), w=(UPk,))
            s.op("vector", lambda e, U=U, UP=UP: e.tensor_scalar(
                U[:, 0:512], UP[:, 0:512], mk["bsc2"][:, 1:2], MAGIC, ALU.mult, ALU.add),
                r=(UPk, "BSC2"), w=Ux)
            s.op("vector", lambda e, S=S, U=U: e.tensor_scalar(
                S[:, 0:512], U[:, 0:512], MAGIC, None, ALU.subtract),
                r=Ux, w=Sx, fuse=True)
            s.op("vector", lambda e, gsl=gsl, C=C, S=S: e.scalar_tensor_tensor(
                gsl, C[:, 0:512], INV216, S[:, 0:512], ALU.mult, ALU.mult),
                r=Cx + Sx, w=ARN, fuse=True)
        gic = GATE[:, ic * R:(ic + 1) * R]
        s.op("vector", lambda e, gic=gic, ic=ic: e.reduce_max(
            QM[:, ic:ic + 1], gic, AX.X,
            apply_absolute_value=True), r=ARN, w=("HM2",))
    cross_partition_max(mk, QM, "HM2", IC1, "G1")
    s.op("vector", lambda e: e.tensor_copy(out=mk["ccstage"][0:1, 0:1],
                                           in_=mk["st_g1"][0:1, 0:1]),
         r=("G1",), w=("CCS",))
    allreduce_max(mk, 1)
    s.op("vector", lambda e: e.tensor_tensor(out=SC[0:1, 13:14], in0=SC[0:1, 14:15],
                                             in1=SC[0:1, 4:5], op=ALU.mult),
         r=("SC",), w=("SC",))
    s.op("vector", lambda e: e.tensor_scalar(SC[0:1, 13:14], SC[0:1, 13:14],
                                             TWO16, None, ALU.mult),
         r=("SC",), w=("SC",))
    s.op("vector", lambda e: e.tensor_tensor(out=SC[0:1, 14:15],
                                             in0=mk["st_gmax"][0:1, 0:1],
                                             in1=SC[0:1, 13:14], op=ALU.mult),
         r=("GMAX", "SC"), w=("SC",))
    s.op("vector", lambda e: e.tensor_scalar(SC[0:1, 14:15], SC[0:1, 14:15],
                                             INV32767, 1e-12, ALU.mult, ALU.max),
         r=("SC",), w=("SC",))
    s.op("vector", lambda e: e.reciprocal(SC[0:1, 15:16], SC[0:1, 14:15]),
         r=("SC",), w=("SC",))
    s.op("vector", lambda e: e.tensor_tensor(out=SC[0:1, 9:10], in0=SC[0:1, 13:14],
                                             in1=SC[0:1, 15:16], op=ALU.mult),
         r=("SC",), w=("SC",))
    s.op("vector", lambda e: e.tensor_copy(out=SC[0:1, 10:11], in_=SC[0:1, 14:15]),
         r=("SC",), w=("SC",))
    broadcast_scalars(mk, SC[0:1, 9:11], ("SC",), "bsc2", "BSC2", 2)

    OQ = mk["H24"]
    for ic in range(IC1):
        for tt in range(4):
            par = (ic * 4 + tt) % 2
            S = silg if par == 0 else mk["silgB"]
            Sx = ("SILG",) if par == 0 else ("SILGB", "ACt2")
            gsl = GATE[:, ic * R + tt * 512: ic * R + (tt + 1) * 512]
            osl = OQ[:, ic * R + tt * 512: ic * R + (tt + 1) * 512]
            s.op("vector", lambda e, gsl=gsl, S=S: e.tensor_scalar(
                S[:, 0:512], gsl, mk["bsc2"][:, 0:1], MAGIC, ALU.mult, ALU.add),
                r=ARN + ("BSC2",), w=Sx)
            s.op("vector", lambda e, osl=osl, S=S: e.tensor_scalar(
                osl, S[:, 0:512], MAGIC, mk["bsc2"][:, 1:2], ALU.subtract, ALU.mult),
                r=Sx + ("BSC2",), w=ARN, fuse=True)

    XT = mk["XT"]
    WD = mk["wd_t1"]
    for half in range(2):
        for ic in range(IC1):
            srcw = mk["b_wd1"][half * 4:(half + 1) * 4, ic, :, :].rearrange("c i h -> i c h")
            s.dma("sync", lambda e, ic=ic, srcw=srcw: e.dma_start(
                out=WD[:, ic * 512:(ic + 1) * 512].rearrange("i (c h) -> i c h", h=128),
                in_=srcw), r=("hb_b_wd1",), w=("AC",))
        for i in range(NT):
            pd, pdk = (mk["pt0"], "pt0") if i % 2 == 0 else (mk["pt1"], "pt1")
            for ic in range(IC1):
                s.op("tensor", lambda e, i=i, ic=ic, pd=pd: e.matmul(
                    pd[:, 0:512], OQ[:, ic * R + i * 128: ic * R + (i + 1) * 128],
                    WD[:, ic * 512:(ic + 1) * 512],
                    start=(ic == 0), stop=(ic == IC1 - 1)),
                    r=ARN + ("AC",), w=(pdk,), fuse=(ic != 0))
            dstx = XT[:, i * D + half * 512: i * D + (half + 1) * 512]
            s.op("vector", lambda e, dstx=dstx, pd=pd: e.scalar_tensor_tensor(
                dstx, pd[:, 0:512], 0.5, dstx, ALU.mult, ALU.add),
                r=(pdk, f"x{i}"), w=(f"x{i}",))



def sublayer_l1attn(mk):
    s = mk.s
    layernorm_stats(mk)
    ln_to_actt(mk, reorder=True)
    LNXT = mk["XQT"].rearrange("p (c t) -> p c t", t=R)

    def dump(tag, ap):
        if tag in DEBUG["dumps"]:
            o = mk.dram("dbg_" + tag, [128, 8 * R], BF16, kind="ExternalOutput")
            mk.s.dma("sync", lambda e: e.dma_start(out=o, in_=ap),
                     r=ARN + ACK + ("R1b", "R2b", "R2c"), w=())
    dump("lnxt", mk["XQT"])

    VT, QF, KF = mk["l1_VT"], mk["l1_QF"], mk["l1_KF"]
    ATD = mk["l1_ATD"]
    VW = mk["l1_VW"]
    psa = mk["pt_a"]
    WQK = mk["wg_t"]

    s.dma("gpsimd", lambda e: e.dma_start(out=mk["l1_brow"], in_=mk["bqkv_row_d"]),
          r=(), w=("R2",))
    s.dma("sync", lambda e: e.dma_start(out=mk["bqkv"][:], in_=mk["bqkv_d"]),
          r=(), w=("BQKV",))
    s.dma("sync", lambda e: e.dma_start(out=mk["msk"][:], in_=mk["msk_d"]),
          r=(), w=("MSK",))
    s.dma("sync", lambda e: e.dma_start(out=mk["onesb"][:], in_=mk["onesb_d"]),
          r=(), w=("ONESB",))
    s.op("vector", lambda e: e.tensor_scalar(mk["p_am"][:, 0:8], mk["bqkv"][:, 0:8],
                                             0.125, None, ALU.mult),
         r=("BQKV",), w=("HM2",))

    for hc in range(8):
        srcw = mk["inprojT_d"].rearrange("(c p) m -> p c m", p=128)[:, hc, 2048:3072]
        s.dma("gpsimd", lambda e, hc=hc, srcw=srcw: e.dma_start(
            out=VW[:, hc * D:(hc + 1) * D], in_=srcw), r=(), w=("R1",))
    for g in range(NT):
        if g % 2 == 0:
            ph = ((mk["pt0"], "pt0"), (mk["pt1"], "pt1"))
        else:
            ph = ((mk["pt2"], "pt2"), (mk["pt3"], "pt3"))
        for half in range(2):
            pp, ppk = ph[half]
            for hc in range(8):
                s.op("tensor", lambda e, g=g, half=half, hc=hc, pp=pp: e.matmul(
                    pp[:, 0:512],
                    LNXT[:, hc, g * 128:(g + 1) * 128],
                    VW[:, hc * D + half * 512: hc * D + (half + 1) * 512],
                    start=(hc == 0), stop=False),
                    r=("AC", "ACt", "ACt2", "R1"), w=(ppk,), fuse=(hc != 0))
            s.op("tensor", lambda e, half=half, pp=pp: e.matmul(
                pp[:, 0:512], mk["onesb"][0:1, :],
                mk["l1_brow"][0:1, 2048 + half * 512:2048 + (half + 1) * 512],
                start=False, stop=True),
                r=("ONESB", "R2"), w=(ppk,), fuse=True)
            dst = VT[:, g * D + half * 512:g * D + (half + 1) * 512]
            s.op("scalar", lambda e, dst=dst, pp=pp: e.activation(
                dst, pp[:, 0:512], AF.Identity), r=(ppk,), w=("R0",))

    dump("vt", mk["l1_VT"])
    for part, dstt, scl, dk in (("q", QF, 0.125, "R1"), ("k", KF, 1.0, "R2")):
        off = 0 if part == "q" else 1024
        for fc in range(8):
            sl = fc % 2
            wk = f"WGT{sl}"
            srcw = mk["inprojT_d"].rearrange("(c p) m -> p c m", p=128)[
                :, :, off + fc * 128: off + (fc + 1) * 128]
            s.dma("gpsimd", lambda e, srcw=srcw, sl=sl: e.dma_start(
                out=ap_of(WQK)[:, sl * D:(sl + 1) * D].rearrange(
                    "p (c i) -> p c i", i=128), in_=srcw),
                r=(), w=(wk,))
            for gg in range(4):
                pp, ppk = (mk["pt0"], "pt0") if gg % 2 == 0 else (mk["pt1"], "pt1")
                for hc in range(8):
                    rhs = LNXT[:, hc, gg * 512:(gg + 1) * 512]
                    s.op("tensor", lambda e, rhs=rhs, hc=hc, sl=sl, pp=pp: e.matmul(
                        pp[:, 0:512], WQK[:, sl * D + hc * 128:sl * D + (hc + 1) * 128],
                        rhs, start=(hc == 0), stop=(hc == 7)),
                        r=(wk, "AC", "ACt", "ACt2"), w=(ppk,), fuse=(hc != 0))
                if part == "q":
                    bias = mk["p_am"][:, fc:fc + 1]
                    bk = "HM2"
                else:
                    bias = mk["bqkv"][:, 8 + fc:8 + fc + 1]
                    bk = "BQKV"
                s.op("scalar", lambda e, dstt=dstt, fc=fc, gg=gg, scl=scl, bias=bias,
                     pp=pp: e.activation(
                         dstt[:, fc * R + gg * 512: fc * R + (gg + 1) * 512],
                         pp[:, 0:512], AF.Identity, bias=bias, scale=scl),
                     r=(ppk, bk), w=(dk,))

    dump("qf", mk["l1_QF"])
    dump("kf", mk["l1_KF"])
    attb = mk["wg_t"]
    nmx, ssum, rs = mk["st_red1"], mk["st_amax"], mk["st_rs"]
    silg, upr = mk["silg"], mk["upr"]
    idb = mk["identb"]
    for g in range(NT):
        for h in range(16):
            hh, fc = h % 2, h // 2
            psc, psck = (mk["pt2"], "pt2") if h % 2 == 0 else (mk["pt3"], "pt3")
            pst, pbk = (mk["ptb0"], "ptb0") if h % 2 == 0 else (mk["ptb1"], "ptb1")
            qv = ap_of(QF)[hh * 64:(hh + 1) * 64, fc * R + g * 128: fc * R + (g + 1) * 128]
            kv = ap_of(KF)[hh * 64:(hh + 1) * 64, fc * R + g * 128: fc * R + (g + 1) * 128]
            s.op("tensor", lambda e, qv=qv, kv=kv, psc=psc: e.matmul(
                psc[:, 0:128], qv, kv, start=True, stop=True),
                r=("R1", "R2"), w=(psck,))
            s.op("vector", lambda e, psc=psc: e.tensor_tensor(
                out=silg[:, 0:128], in0=psc[:, 0:128], in1=mk["msk"][:], op=ALU.add),
                r=(psck, "MSK"), w=("SILG",))
            s.op("vector", lambda e: e.reduce_max(nmx[:, 0:1], silg[:, 0:128], AX.X,
                                                  negate=True),
                 r=("SILG",), w=("RED1",))
            s.op("scalar", lambda e: e.activation(upr[:, 0:128], silg[:, 0:128],
                                                  AF.Exp, bias=nmx[:, 0:1]),
                 r=("SILG", "RED1"), w=("UPR",))
            s.op("vector", lambda e: e.reduce_sum(ssum[:, 0:1], upr[:, 0:128], AX.X),
                 r=("UPR",), w=("AMAX",))
            s.op("vector", lambda e: e.reciprocal(rs[:, 0:1], ssum[:, 0:1]),
                 r=("AMAX",), w=("RS",))
            s.op("vector", lambda e: e.tensor_scalar(attb[:, 0:128], upr[:, 0:128],
                                                     rs[:, 0:1], None, ALU.mult),
                 r=("UPR", "RS"), w=("WGT0",))
            s.op("tensor", lambda e, pst=pst: e.transpose(pst[:, 0:128], attb[:, 0:128],
                                                          idb[:]),
                 r=("WGT0", "IDB"), w=(pbk,))
            s.op("vector", lambda e, pst=pst: e.tensor_copy(out=attb[:, 128:256],
                                                            in_=pst[:, 0:128]),
                 r=(pbk,), w=("WGT1",))
            s.op("tensor", lambda e, g=g, h=h, hh=hh, fc=fc: e.matmul(
                psa[hh * 64:(hh + 1) * 64,
                    (h // 8) * 512 + (fc % 4) * 128:(h // 8) * 512 + (fc % 4 + 1) * 128],
                VT[:, g * D + h * 64: g * D + (h + 1) * 64],
                attb[:, 128:256], start=True, stop=True),
                r=("R0", "WGT1"), w=("pta",))
            if h in (7, 15):
                lo = 0 if h == 7 else 4
                po = (h // 8) * 512
                srcp = ap_of(psa)[:, po:po + 512].rearrange(
                    "p (sl j b) -> p sl j b", j=16, b=8)
                dst = ATD.rearrange("p (c b G j) -> p c b G j",
                                    c=8, b=8, j=16)[:, lo:lo + 4, :, g, :].rearrange(
                    "p c b j -> p c j b")
                s.op("vector", lambda e, srcp=srcp, dst=dst: e.tensor_copy(
                    out=dst, in_=srcp), r=("pta",), w=("AC", "ACt", "ACt2"))

    dump("atd", mk["l1_ATD"])
    s.dma("gpsimd", lambda e: e.dma_start(
        out=mk["l1_WOT"].rearrange("p (c m) -> p c m", m=D),
        in_=mk["outprojT1_d"].rearrange("(c p) m -> p c m", p=128)),
        r=("R1",), w=("R1",))
    s.dma("gpsimd", lambda e: e.dma_start(out=mk["l1_bout"], in_=mk["bout_row_d"]),
          r=(), w=("R1",))
    XT = mk["XT"]
    for i in range(NT):
        pd, pdk = (mk["pt0"], "pt0") if i % 2 == 0 else (mk["pt1"], "pt1")
        for half in range(2):
            for c in range(8):
                s.op("tensor", lambda e, c=c, half=half, i=i, pd=pd: e.matmul(
                    pd[:, 0:512], ATD[:, c * R + i * 128:c * R + (i + 1) * 128],
                    mk["l1_WOT"][:, c * D + half * 512: c * D + (half + 1) * 512],
                    start=(c == 0), stop=False),
                    r=("AC", "ACt", "ACt2", "R1"), w=(pdk,), fuse=(c != 0))
            s.op("tensor", lambda e, half=half, pd=pd: e.matmul(
                pd[:, 0:512], mk["onesb"][0:1, :],
                mk["l1_bout"][0:1, half * 512:(half + 1) * 512],
                start=False, stop=True),
                r=("ONESB", "R1"), w=(pdk,), fuse=True)
            dstx = XT[:, i * D + half * 512: i * D + (half + 1) * 512]
            s.op("vector", lambda e, dstx=dstx, pd=pd: e.tensor_tensor(
                out=dstx, in0=pd[:, 0:512], in1=dstx, op=ALU.add),
                r=(pdk, f"x{i}"), w=(f"x{i}",))



def alloc_common(mk, need):
    mk.dram("x_in", [R, D], F32, kind="ExternalInput")
    mk.dram("x_out", [R, D], F32, kind="ExternalOutput")
    mk.dram("identf_d", [128, 128], F32, kind="ExternalInput")
    mk.dram("identb_d", [128, 128], BF16, kind="ExternalInput")
    mk.dram("onesf_d", [1, 128], F32, kind="ExternalInput")
    if need["l0ffn"]:
        mk.dram("L0w_gate", [24 * 128, D], F32, kind="ExternalInput")
        mk.dram("L0w_up", [24 * 128, D], F32, kind="ExternalInput")
        mk.dram("L0w_down", [D, I0], F32, kind="ExternalInput")
        mk.dram("clipx", [1, 1], F32, kind="ExternalInput")
        mk.dram("clipd", [1, 1], F32, kind="ExternalInput")
        mk.dram("b_wg0", [24, 8, 128, 128], BF16, addr_space="Shared")
        mk.dram("b_wu0", [24, 8, 128, 128], BF16, addr_space="Shared")
        mk.dram("b_wd0", [8, 24, 128, 128], BF16, addr_space="Shared")
        mk.dram("b_swg0", [24, 128], F32, addr_space="Shared")
        mk.dram("b_swu0", [24, 128], F32, addr_space="Shared")
    if need["l1ffn"]:
        mk.dram("L1w_gate", [24 * 128, D], F32, kind="ExternalInput")
        mk.dram("L1w_up", [24 * 128, D], F32, kind="ExternalInput")
        mk.dram("L1w_down", [D, I1], F32, kind="ExternalInput")
        mk.dram("b_wg1", [24, 8, 128, 128], BF16, addr_space="Shared")
        mk.dram("b_wu1", [24, 8, 128, 128], BF16, addr_space="Shared")
        mk.dram("b_wd1", [8, 24, 128, 128], BF16, addr_space="Shared")
        mk.dram("up_hbm", [IC1, 128, R], F16)
    if need["l1attn"]:
        mk.dram("inprojT_d", [D, 3 * D], F32, kind="ExternalInput")
        mk.dram("bqkv_d", [128, 24], F32, kind="ExternalInput")
        mk.dram("bqkv_row_d", [1, 3 * D], F32, kind="ExternalInput")
        mk.dram("outprojT1_d", [D, D], F32, kind="ExternalInput")
        mk.dram("bout_row_d", [1, D], F32, kind="ExternalInput")
        mk.dram("msk_d", [128, 128], BF16, kind="ExternalInput")
        mk.dram("onesb_d", [1, 128], BF16, kind="ExternalInput")
    if need["l0attn"]:
        mk.dram("W_inT_d", [D, D], F32, kind="ExternalInput")
        mk.dram("W_outT_d", [D, D], F32, kind="ExternalInput")
        mk.dram("dm_d", [128, 1536], F32, kind="ExternalInput")
        mk.dram("v_eg_shard", [2, 8, 128, D], BF16)
        mk.dram("v_eg_g", [NCORES + 2, 2, 8, 128, D], BF16, addr_space="Shared")
    mk.dram("cc_in", [1, 16], F32)
    mk.dram("cc_out", [1, 16], F32, addr_space="Shared")

    mk.sb("XT", [128, NT * D], F32)
    arena = mk.sb("ARENA", [128, 24 * R], BF16)
    A = arena.ap()[:, :]
    mk.view("H", A[:, 0:IC0 * R])
    mk.view("Hf16", A.bitcast(F16)[:, 0:IC0 * R])
    Af32 = A.bitcast(F32)
    mk.view("p_wnat", Af32[:, 0:1024])
    mk.view("p_wnat3", Af32[:, 1024:4096])
    mk.view("p_scr", Af32[:, 4096:5120])
    mk.view("p_scr2", Af32[:, 5120:6144])
    mk.view("p_scr3", Af32[:, 6144:9216])
    mk.view("p_scr3b", Af32[:, 9216:12288])
    mk.view("scr1", Af32[:, 12288:14336])
    mk.view("scr2", Af32[:, 14336:16384])
    mk.view("p_stage", A[:, 32768:33792])
    mk.view("p_stage2", A[:, 33792:34816])
    mk.view("xqk", A[:, 34816:35840])
    mk.view("sc_tmp", Af32[:, 17920:18048])
    mk.view("p_amw", Af32[:, 18048:18144])
    mk.view("p_amw2", Af32[:, 18144:18240])
    actt = mk.sb("ACTT", [128, 8 * R], BF16)
    AT = actt.ap()[:, :]
    mk.view("XQT", AT)
    mk.view("wd_t", AT[:, 0:IC0 * 512])
    mk.view("wd_t1", AT[:, 0:IC1 * 512])
    mk.view("l1_ATD", AT[:, 0:NT * D])
    mk.view("sc512", AT[:, 12288:13312].bitcast(F32))
    mk.view("silgB", AT[:, 13312:14336].bitcast(F32))
    mk.view("uprB", AT[:, 14336:15360].bitcast(F32))
    mk.view("sc512B", AT[:, 15360:16384].bitcast(F32))
    mk.view("VTOK", A[:, 0:NT * D])
    mk.view("ATTOK", A[:, NT * D:2 * NT * D])
    mk.view("WOT", A[:, 2 * NT * D:2 * NT * D + 8 * D])
    mk.view("WIT", A[:, 2 * NT * D + 8 * D:2 * NT * D + 16 * D])
    mk.view("HALO", AT[:, 0:NT * D])
    mk.view("pbt", A[:, 2 * NT * D + 8 * D:2 * NT * D + 8 * D + 3072])
    mk.view("attt", A[:, 2 * NT * D + 8 * D + 3072:2 * NT * D + 8 * D + 4096])
    mk.view("GATE16", A.bitcast(F16))
    mk.view("H24", A)
    mk.view("l1_VT", A[:, 0:NT * D])
    mk.view("l1_QF", A[:, NT * D:2 * NT * D])
    mk.view("l1_KF", A[:, 2 * NT * D:3 * NT * D])
    mk.view("l1_VW", A[:, NT * D:NT * D + 8 * D])
    mk.view("l1_WOT", A[:, NT * D:NT * D + 8 * D])
    mk.view("l1_brow", A[0:1, 2 * NT * D:2 * NT * D + 3 * D])
    mk.view("l1_bout", A[0:1, NT * D + 8 * D:NT * D + 9 * D])
    mk.sb("identf", [128, 128], F32)
    mk.sb("identb", [128, 128], BF16)
    mk.sb("onesf", [1, 128], F32)
    for n in ("st_sum", "st_ssq", "st_mean", "st_var", "st_rstd", "st_nmr",
              "st_amax", "st_rs", "st_nmrs"):
        mk.sb(n, [128, NT], F32)
    mk.sb("st_red1", [128, 1], F32)
    mk.sb("st_g1", [1, 1], F32)
    mk.sb("st_gmax", [1, 16], F32)
    mk.sb("ccstage", [1, 16], F32)
    mk.sb("scal", [1, 16], F32)
    mk.sb("sxb", [128, 2], F32)
    mk.sb("shb", [128, 3], F32)
    mk.sb("clipx_t", [1, 1], F32)
    mk.sb("clipd_t", [1, 1], F32)
    mk.sb("swg", [128, 24], F32)
    mk.sb("swu", [128, 24], F32)
    mk.sb("wg_t", [128, 2048], BF16)
    mk.sb("wu_t", [128, 1024], BF16)
    mk.sb("silg", [128, 512], F32)
    mk.sb("upr", [128, 512], F32)
    mk.sb("hmax", [128, IC1 * 4], F32)
    mk.sb("hmax2", [128, IC1 * 4], F32)
    mk.view("p_am", ap_of(mk.t["hmax2"])[:, 0:32])
    mk.view("p_am2", ap_of(mk.t["hmax"])[:, 0:32])
    mk.sb("upst", [128, 512], F16)
    mk.sb("bsc", [128, 4], F32)
    mk.sb("bsc2", [128, 2], F32)
    mk.sb("rcst", [128, 4], F32)
    mk.sb("bqkv", [128, 24], F32)
    mk.sb("msk", [128, 128], BF16)
    mk.sb("onesb", [1, 128], BF16)

    mk.ps("ptb0", [128, 1024], BF16)
    mk.ps("ptb1", [128, 1024], BF16)
    mk.ps("pt_a", [128, 1024], F32)
    mk.ps("pt0", [128, 512], F32)
    mk.ps("pt1", [128, 512], F32)
    mk.ps("pt2", [128, 512], F32)
    mk.ps("pt3", [128, 512], F32)


def build_group(sublayers):
    nc = bass.Bass("TRN2", target_bir_lowering=False)
    s = Seq(nc)
    mk = MK(nc, s)
    need = {
        "l0ffn": any(x in (0, 2) for x in sublayers),
        "l0attn": 1 in sublayers,
        "l1ffn": any(x in (3, 5) for x in sublayers),
        "l1attn": 4 in sublayers,
    }
    alloc_common(mk, need)

    s.dma("sync", lambda e: e.dma_start(out=mk["identf"][:], in_=mk["identf_d"]),
          r=(), w=("IDF",))
    s.dma("sync", lambda e: e.dma_start(out=mk["identb"][:], in_=mk["identb_d"]),
          r=(), w=("IDB",))
    s.dma("sync", lambda e: e.dma_start(out=mk["onesf"][:], in_=mk["onesf_d"]),
          r=(), w=("ONESF",))
    s.op("vector", lambda e: e.memset(mk["rcst"][:, 0:1], MAGIC), r=(), w=("RCST",))
    s.op("vector", lambda e: e.memset(mk["rcst"][:, 1:2], TWO16), r=(), w=("RCST",))
    s.op("vector", lambda e: e.memset(mk["rcst"][:, 2:3], -MAGIC), r=(), w=("RCST",))
    s.op("vector", lambda e: e.memset(mk["rcst"][:, 3:4], 1.0), r=(), w=("RCST",))
    if need["l0ffn"]:
        s.dma("sync", lambda e: e.dma_start(out=mk["clipx_t"][:], in_=mk["clipx"]),
              r=(), w=("CLX",))
        s.dma("sync", lambda e: e.dma_start(out=mk["clipd_t"][:], in_=mk["clipd"]),
              r=(), w=("CLD",))
    xr = mk["x_in"].rearrange("(i p) m -> p i m", p=128)
    s.dma("sync", lambda e: e.dma_start(
        out=ap_of(mk.t["XT"])[:, :].rearrange("p (i m) -> p i m", m=D), in_=xr),
        r=(), w=xkeys())

    prep_all(mk, need["l0ffn"], need["l1ffn"])

    for sl in sublayers:
        if sl in (0, 2):
            sublayer_l0ffn(mk)
        elif sl == 1:
            sublayer_l0attn(mk)
        elif sl in (3, 5):
            sublayer_l1ffn(mk)
        elif sl == 4:
            sublayer_l1attn(mk)
        else:
            raise NotImplementedError(f"sublayer {sl}")

    xo = mk["x_out"].rearrange("(i p) m -> p i m", p=128)
    s.dma("sync", lambda e: e.dma_start(
        out=xo, in_=ap_of(mk.t["XT"])[:, :].rearrange("p (i m) -> p i m", m=D)),
        r=xkeys(), w=("hb_xout",))

    import contextlib
    with contextlib.ExitStack() as ctx:
        sems = {}
        for eng in ("scalar", "vector", "tensor", "gpsimd"):
            sems[(eng, False)] = ctx.enter_context(nc.semaphore(f"s_{eng}_c"))
        for eng in ("sync", "scalar", "gpsimd"):
            for sl in range(Seq.DMAK):
                sems[(eng, True, sl)] = ctx.enter_context(
                    nc.semaphore(f"s_{eng}_d{sl}"))
        block = ctx.enter_context(nc.Block())
        s.emit(block, sems)
    return nc



def _padw(w):
    out = np.zeros((24 * 128, w.shape[1]), np.float32)
    out[:w.shape[0]] = w
    return np.ascontiguousarray(out)


def _msk_const():
    r = np.arange(128)
    m = np.where((r[:, None] // 8) == (r[None, :] // 8), 0.0, -1e9)
    return np.ascontiguousarray(m.astype(np.float32))


def _dm_const():
    kk = np.arange(128.0)[:, None]
    rp = np.arange(384.0)[None, :]
    r, p = rp // 128, rp % 128
    dmi = kk - p + 128.0 * (r - 1.0)
    out = np.empty((128, 1536), np.float32)
    for sidx, std in enumerate((1.0, 2.0, 4.0, 8.0)):
        out[:, sidx * 384:(sidx + 1) * 384] = dmi / std + 1.0
    return out


def kernel(L0_attn_values, L0_attn_input_weights, L0_attn_output_weight,
           L0_ffn_clip_x, L0_ffn_clip_down, L0_ffn_w_gate, L0_ffn_w_up,
           L0_ffn_w_down, L1_attn_in_proj_w, L1_attn_in_proj_b,
           L1_attn_out_proj_w, L1_attn_out_proj_b, L1_ffn_w_gate,
           L1_ffn_w_up, L1_ffn_w_down):
    x = np.asarray(L0_attn_values, np.float32)
    feed = {
        "identf_d": np.eye(128, dtype=np.float32),
        "identb_d": np.eye(128, dtype=np.float32).astype(ml_dtypes.bfloat16),
        "onesf_d": np.ones((1, 128), np.float32),
        "L0w_gate": _padw(np.asarray(L0_ffn_w_gate, np.float32)),
        "L0w_up": _padw(np.asarray(L0_ffn_w_up, np.float32)),
        "L0w_down": np.ascontiguousarray(np.asarray(L0_ffn_w_down, np.float32)),
        "clipx": np.asarray(L0_ffn_clip_x, np.float32).reshape(1, 1),
        "clipd": np.asarray(L0_ffn_clip_down, np.float32).reshape(1, 1),
        "L1w_gate": _padw(np.asarray(L1_ffn_w_gate, np.float32)),
        "L1w_up": _padw(np.asarray(L1_ffn_w_up, np.float32)),
        "L1w_down": np.ascontiguousarray(np.asarray(L1_ffn_w_down, np.float32)),
        "W_inT_d": np.ascontiguousarray(np.asarray(L0_attn_input_weights, np.float32).T),
        "W_outT_d": np.ascontiguousarray(np.asarray(L0_attn_output_weight, np.float32).T),
        "dm_d": _dm_const(),
        "inprojT_d": np.ascontiguousarray(np.asarray(L1_attn_in_proj_w, np.float32).T),
        "bqkv_d": np.ascontiguousarray(
            np.asarray(L1_attn_in_proj_b, np.float32).reshape(24, 128).T),
        "bqkv_row_d": np.asarray(L1_attn_in_proj_b, np.float32).reshape(1, 3 * D),
        "outprojT1_d": np.ascontiguousarray(np.asarray(L1_attn_out_proj_w, np.float32).T),
        "bout_row_d": np.asarray(L1_attn_out_proj_b, np.float32).reshape(1, D),
        "msk_d": _msk_const().astype(ml_dtypes.bfloat16),
        "onesb_d": np.ones((1, 128), np.float32).astype(ml_dtypes.bfloat16),
    }

    cur = x
    for group in CONFIG["groups"]:
        nc = build_group(group)
        declared = {a.memorylocations[0].name
                    for a in nc.m.functions[0].allocations
                    if isinstance(a, mybir.MemoryLocationSet)
                    and a.kind == "ExternalInput"}
        in_maps = []
        for c in range(NCORES):
            xc = np.ascontiguousarray(
                cur[:, c * TLOC:(c + 1) * TLOC, :].reshape(R, D))
            m = {"x_in": xc}
            for k, v in feed.items():
                if k in declared:
                    m[k] = v
            in_maps.append(m)
        res = run_bass_kernel_spmd(nc, in_maps, core_ids=list(range(NCORES)))
        nxt = np.empty_like(cur)
        for c in range(NCORES):
            nxt[:, c * TLOC:(c + 1) * TLOC, :] = \
                res.results[c]["x_out"].reshape(B, TLOC, D)
        cur = nxt
    return cur
